# version 09
# baseline (speedup 1.0000x reference)
import numpy as np
import ml_dtypes
import concourse.bass as bass
import concourse.mybir as mybir
from concourse.bass_utils import run_bass_kernel_spmd

F32 = mybir.dt.float32
BF16 = mybir.dt.bfloat16
AF = mybir.ActivationFunctionType
ALU = mybir.AluOpType
BF = ml_dtypes.bfloat16
AX = mybir.AxisListType.X

NT = 1024
DM = 2048
NH = 16
HD = 5632
FC = DM // 128
TC = NT // 128
NBLK = 4
HC = HD // 128
NW_ELEMS = float(HD * DM)
NPRE = 3
GCH = 528

ENGINES = ("sync", "scalar", "vector", "tensor", "gpsimd")


class Sem:
    __slots__ = ("h", "v", "name")

    def __init__(self, h, name):
        self.h, self.v, self.name = h, 0, name


class Op:
    __slots__ = ("meth", "args", "kwargs", "incs")

    def __init__(self, meth, args, kwargs):
        self.meth, self.args, self.kwargs, self.incs = meth, args, kwargs, []

    def inc(self, sem, n):
        assert not self.incs, f"op {self.meth} already has a sync update"
        self.incs.append((sem, n, sem.v + n))
        sem.v += n
        return self

    def tok(self):
        if self.incs:
            s, n, va = self.incs[0]
            return (s, va)
        return None


class Rec:
    def __init__(self, nc):
        self.nc = nc
        self.q = {e: [] for e in ENGINES}
        self._n = 0
        self._lastwait = {}

    def sem(self, name=None):
        self._n += 1
        nm = name or f"s{self._n}"
        return Sem(self.nc.alloc_semaphore(nm), nm)

    def op(self, eng, meth, *args, **kwargs):
        o = Op(meth, args, kwargs)
        self.q[eng].append(o)
        return o

    def wait(self, eng, sem, val=None):
        v = sem.v if val is None else val
        if v <= 0:
            return
        key = (eng, sem.name)
        if self._lastwait.get(key, -1) >= v:
            return
        self._lastwait[key] = v
        self.q[eng].append(Op("wait_ge", (sem.h, v), {}))

    def dma(self, eng, out, in_, sem, **kw):
        return self.op(eng, "dma_start", out=out, in_=in_, **kw).inc(sem, 16)

    @staticmethod
    def _interval(ap):
        ext = 1
        for (step, count) in ap.ap:
            ext += abs(int(step)) * (int(count) - 1)
        off = int(ap.offset)
        return (ap.tensor.name, off, off + ext)

    def _rw_of(self, o):
        if o.meth in ("wait_ge", "nop"):
            return [], []
        writes, reads = [], []
        args = list(o.args)
        kw = dict(o.kwargs)
        if o.meth == "memset":
            writes.append(self._interval(args[0]))
            return writes, reads
        if "out" in kw:
            writes.append(self._interval(kw.pop("out")))
        for v in list(args) + list(kw.values()):
            if isinstance(v, bass.AP):
                reads.append(self._interval(v))
        return writes, reads

    def _auto_chain(self):
        def overlap(a, b):
            return a[0] == b[0] and a[1] < b[2] and b[1] < a[2]

        for eng in ("vector", "scalar", "gpsimd"):
            ops = self.q[eng]
            chain = self.sem(f"chain_{eng}")
            writers = []
            readers = []
            n = len(ops)
            deps = [None] * n
            is_dma = [o.meth == "dma_start" for o in ops]
            for i, o in enumerate(ops):
                ws, rs = self._rw_of(o)
                need = set()
                for r in rs:
                    for (w, j) in writers:
                        if (not is_dma[j]) and overlap(r, w):
                            need.add(j)
                for w in ws:
                    for (pw, j) in writers:
                        if (not is_dma[j]) and overlap(w, pw):
                            need.add(j)
                    for (pr, j) in readers:
                        if (not is_dma[j]) and overlap(w, pr):
                            need.add(j)
                deps[i] = need
                for w in ws:
                    writers.append((w, i))
                for r in rs:
                    readers.append((r, i))
            producers = set()
            for need in deps:
                producers |= need
            for j in sorted(producers):
                if not ops[j].incs:
                    ops[j].inc(chain, 1)
            new_ops = []
            lastw = {}
            for i, o in enumerate(ops):
                if deps[i]:
                    for j in sorted(deps[i]):
                        t = ops[j].tok()
                        s, v = t
                        if lastw.get(s.name, -1) < v:
                            new_ops.append(Op("wait_ge", (s.h, v), {}))
                            lastw[s.name] = v
                new_ops.append(o)
            self.q[eng] = new_ops

    def replay(self):
        self._auto_chain()
        with self.nc.Block() as block:
            for eng in ENGINES:
                ops = self.q[eng]

                def runner(engine, ops=ops):
                    for o in ops:
                        r = getattr(engine, o.meth)(*o.args, **o.kwargs)
                        if o.incs:
                            s, nn_, _ = o.incs[0]
                            r.then_inc(s.h, nn_)

                getattr(block, eng)(runner)


class Bank:
    def __init__(self, R, t, name):
        self.R, self.t, self.name = R, t, name
        self.sem = R.sem(f"bk_{name}")
        self.tok = None

    def acquire(self, eng="tensor"):
        if self.tok is not None:
            self.R.wait(eng, self.tok[0], self.tok[1])

    def release(self, eng):
        o = self.R.q[eng][-1]
        if not o.incs:
            o.inc(self.sem, 1)
        self.tok = o.tok()


class Stream:
    def __init__(self, R, name, nslots):
        self.R = R
        self.sems = [R.sem(f"{name}_{i}") for i in range(nslots)]
        self.uses = [0] * nslots

    def dma(self, eng, slot, out, in_, **kw):
        self.uses[slot] += 1
        return self.R.op(eng, "dma_start", out=out, in_=in_, **kw)\
            .inc(self.sems[slot], 16)

    def wait(self, eng, slot, use=None):
        u = self.uses[slot] if use is None else use
        self.R.wait(eng, self.sems[slot], 16 * u)

    def wait_all(self, eng):
        for i, s in enumerate(self.sems):
            self.R.wait(eng, s, 16 * self.uses[i])


BF16_CONSTS = {"IDX0", "IDX1", "QDROW", "IDEN_BF", "ONESC_BF"}


def make_consts():
    c = {}
    idx = []
    for k in range(2):
        s = 128 * k + np.arange(128)[:, None]
        m = np.arange(256)[None, :]
        d = (m - s).astype(np.float32)
        d[m < s] = 1e9
        idx.append(np.ascontiguousarray(d))
    c["IDX0"], c["IDX1"] = idx
    c["QDROW"] = np.broadcast_to(
        (np.arange(256, dtype=np.float32) + 1.0)[None, :], (128, 256)).copy()
    c["IDEN_BF"] = np.eye(128, dtype=np.float32)
    c["ONESC_BF"] = np.ones((128, 1), dtype=np.float32)
    kd = np.zeros((128, 2), dtype=np.float32)
    kd[:, 0] = 255.0 - np.arange(128)
    kd[:, 1] = 255.0 - (128 + np.arange(128))
    c["KD2"] = kd
    kp = np.zeros((128, 8), dtype=np.float32)
    for tb in range(8):
        kp[:, tb] = 1023.0 - (tb * 128 + np.arange(128))
    c["KDPRE"] = kp
    c["ONESR_F"] = np.ones((1, 128), dtype=np.float32)
    c["ONESC_F"] = np.ones((128, 1), dtype=np.float32)
    cc = np.zeros((128, 8), dtype=np.float32)
    cc[:, 0] = 1e-5
    cc[:, 1] = 1e-6
    cc[:, 2] = 1e-8
    cc[:, 3] = 1.0 / 2048.0
    cc[:, 4] = 256.0
    cc[:, 5] = 1024.0
    c["CCOL"] = cc
    c["CROW"] = np.ascontiguousarray(cc[0:1, :])
    return c


def load_consts(nc, R, names, consts, sem):
    t = {}
    buf = {}
    for k in names:
        arr = consts[k]
        if k in BF16_CONSTS:
            arr = arr.astype(BF)
            buf[k] = nc.alloc_sbuf_tensor(f"cb_{k}", list(arr.shape), BF16)
        else:
            buf[k] = nc.alloc_sbuf_tensor(f"cb_{k}", list(arr.shape), F32)
        t[k] = nc.inline_tensor(arr, name=f"c_{k}")
        R.dma("sync", buf[k][:], t[k].ap(), sem)
    return buf


def emit_layernorm(R, S, BK, src, dst, eps_row, src_sem, tag):
    pe, dve, act = R.s["pe"], R.s["dve"], R.s["act"]
    xsq = [S["scrB"][:, 0:1024], S["scrB"][:, 1024:2048]]
    ssum = [BK["b2"].t[0:1, 0:512], BK["b3"].t[0:1, 0:512]]
    ssq = [BK["b4"].t[0:1, 0:512], BK["b5"].t[0:1, 0:512]]
    for b in ("b2", "b3", "b4", "b5"):
        BK[b].acquire()
    R.wait("scalar", src_sem)
    R.wait("tensor", src_sem)
    sq_done = R.sem(f"lnsq{tag}")
    for fc in range(FC):
        srcc = src[:, fc * NT:(fc + 1) * NT]
        p = fc % 2
        if fc >= 2:
            R.wait("scalar", pe)
        R.op("scalar", "activation", out=xsq[p][:], in_=srcc, func=AF.Square)\
            .inc(sq_done, 1)
        R.wait("tensor", sq_done)
        for half in range(2):
            rsrc = src[:, fc * NT + half * 512: fc * NT + half * 512 + 512]
            R.op("tensor", "matmul", ssum[half], S["ONESC_BF"][:], rsrc,
                 start=(fc == 0), stop=(fc == FC - 1))
            o = R.op("tensor", "matmul", ssq[half], S["ONESC_BF"][:],
                     xsq[p][:, half * 512: half * 512 + 512],
                     start=(fc == 0), stop=(fc == FC - 1))
        o.inc(pe, 1)
    mean, ex2, tmpr = S["rowA"], S["rowB"], S["rowA2"]
    rstd = ex2
    R.wait("vector", pe)
    for half in range(2):
        sl = slice(half * 512, half * 512 + 512)
        R.op("vector", "tensor_scalar", out=mean[0:1, sl], in0=ssum[half],
             scalar1=1.0 / DM, scalar2=None, op0=ALU.mult)
        R.op("vector", "tensor_scalar", out=ex2[0:1, sl], in0=ssq[half],
             scalar1=1.0 / DM, scalar2=None, op0=ALU.mult)
    o = R.op("vector", "tensor_mul", out=tmpr[0:1, 0:1024],
             in0=mean[0:1, 0:1024], in1=mean[0:1, 0:1024])
    o = R.op("vector", "tensor_sub", out=ex2[0:1, 0:1024],
             in0=ex2[0:1, 0:1024], in1=tmpr[0:1, 0:1024])
    o.inc(dve, 1)
    for b in ("b2", "b3", "b4", "b5"):
        BK[b].release("vector")
    R.wait("scalar", dve)
    R.op("scalar", "activation", out=tmpr[0:1, 0:1024], in_=ex2[0:1, 0:1024],
         func=AF.Sqrt, bias=eps_row).inc(act, 1)
    R.wait("vector", act)
    R.op("vector", "reciprocal", out=rstd[0:1, 0:1024],
         in_=tmpr[0:1, 0:1024]).inc(dve, 1)
    for b in ("b2", "b3", "b4", "b5"):
        BK[b].acquire()
    R.wait("tensor", dve)
    bmap = (("b2", 0, True), ("b4", 1, True), ("b3", 0, False), ("b5", 1, False))
    for (bn, half, is_mean) in bmap:
        sl = slice(half * 512, half * 512 + 512)
        o = R.op("tensor", "matmul", BK[bn].t[:, 0:512], S["ONESR_F"][:],
                 (mean if is_mean else rstd)[0:1, sl], start=True, stop=True)
    o.inc(pe, 1)
    meanb = S["scrB"][:, 0:1024]
    rstdb = S["scrB"][:, 1024:2048]
    R.wait("vector", pe)
    for (bn, half, is_mean) in bmap:
        dstb = (meanb if is_mean else rstdb)
        o = R.op("vector", "tensor_copy",
                 out=dstb[:, half * 512: half * 512 + 512],
                 in_=BK[bn].t[:, 0:512])
    o.inc(dve, 1)
    for b in ("b2", "b3", "b4", "b5"):
        BK[b].release("vector")
    ttmp = S["scrB"][:, 2048:3072]
    done = R.sem(f"lnd{tag}")
    for fc in range(FC):
        srcc = src[:, fc * NT:(fc + 1) * NT]
        dstc = dst[:, fc * NT:(fc + 1) * NT]
        R.op("vector", "tensor_sub", out=ttmp[:], in0=srcc, in1=meanb)
        R.op("vector", "tensor_mul", out=dstc, in0=ttmp[:], in1=rstdb)\
            .inc(done, 1)
    return done


def common_setup(nc):
    R = Rec(nc)
    R.s = {k: R.sem(k) for k in ("pe", "dve", "act", "st")}
    BK = {}
    for i in range(6):
        BK[f"b{i}"] = Bank(R, nc.alloc_psum_tensor(f"pb{i}", [128, 512], F32), f"b{i}")
    BK["b6"] = Bank(R, nc.alloc_psum_tensor("pb6", [128, 512], F32), "b6")
    BK["b7"] = Bank(R, nc.alloc_psum_tensor("pb7", [128, 512], F32), "b7")
    return R, BK


def emit_slope_derived(R, S, BK, ld_const):
    pe, dve = R.s["pe"], R.s["dve"]
    sder = R.sem("sder")
    BK["b0"].acquire()
    R.wait("tensor", ld_const)
    R.op("tensor", "matmul", BK["b0"].t[:, 0:16], S["ONESR_F"][:],
         S["slope"][:], start=True, stop=True).inc(pe, 1)
    R.wait("vector", pe)
    R.op("vector", "tensor_scalar", out=S["sneg"][:], in0=BK["b0"].t[:, 0:16],
         scalar1=-1.0, scalar2=None, op0=ALU.mult).inc(dve, 1)
    BK["b0"].release("vector")
    R.wait("scalar", dve)
    R.op("scalar", "activation", out=S["bd_bc"][:], in_=S["sneg"][:],
         func=AF.Exp, scale=S["CCOL"][:, 4:5]).inc(sder, 1)
    R.op("scalar", "activation", out=S["bdc_bc"][:], in_=S["sneg"][:],
         func=AF.Exp, scale=S["CCOL"][:, 5:6]).inc(sder, 1)
    for h in range(NH):
        R.op("scalar", "activation", out=S["kdb_all"][:, 2 * h:2 * h + 2],
             in_=S["KD2"][:], func=AF.Exp, scale=S["sneg"][:, h:h + 1])\
            .inc(sder, 1)
    for h in range(NH):
        R.op("scalar", "activation", out=S["kdpre_all"][:, 8 * h:8 * h + 8],
             in_=S["KDPRE"][:], func=AF.Exp, scale=S["sneg"][:, h:h + 1])\
            .inc(sder, 1)
    return sder


def build_prog_fused(consts):
    nc = bass.Bass("TRN2", target_bir_lowering=False, debug=False)
    xT = nc.dram_tensor("xT", [DM, NT], F32, kind="ExternalInput").ap()
    xTpre = nc.dram_tensor("xTpre", [DM, NPRE * NT], F32, kind="ExternalInput").ap()
    slope = nc.dram_tensor("slope", [1, NH], F32, kind="ExternalInput").ap()
    wqkvu = nc.dram_tensor("wqkvu", [128, 64 * 2048], F32, kind="ExternalInput").ap()
    wout = nc.dram_tensor("wout", [128, 16 * 2048], F32, kind="ExternalInput").ap()
    w1t = nc.dram_tensor("w1t", [128, 44 * 2048], F32, kind="ExternalInput").ap()
    w2t = nc.dram_tensor("w2t", [128, 44 * 2048], F32, kind="ExternalInput").ap()
    w3t = nc.dram_tensor("w3t", [128, 16 * 5632], F32, kind="ExternalInput").ap()
    out_c = nc.dram_tensor("out_c", [DM, NT], F32, kind="ExternalOutput").ap()
    u_spill = nc.dram_tensor("u_spill", [128, FC * NT], BF16, kind="Internal").ap()
    x1_spill = nc.dram_tensor("x1_spill", [128, FC * NT], BF16, kind="Internal").ap()

    R, BK = common_setup(nc)
    R.phases = []

    def _twc():
        return sum(2 if o.meth == "matmul" else 1 for o in R.q["tensor"])
    R.twc = _twc
    s = R.s
    pe, dve, act = s["pe"], s["dve"], s["act"]
    A = nc.alloc_sbuf_tensor
    AR1 = A("ar1", [128, FC * NT], BF16)
    AR2 = A("ar2", [128, FC * NT], BF16)
    AR3 = A("ar3", [128, FC * NT], BF16)
    AR4 = A("ar4", [128, FC * NT], BF16)
    SCRB = A("scrb", [128, 6144], BF16)
    SCRF = A("scrf", [128, 5120], F32)
    WGB = [A(f"wgb{i}", [128, 1024], BF16) for i in range(4)]
    WQ = A("wq", [128, 8192], BF16)
    KVOUT = A("kvout", [128, NH * 128], F32)
    S = {
        "slope": A("s_slope", [1, NH], F32),
        "sneg": A("s_sneg", [128, NH], F32),
        "bd_bc": A("s_bdbc", [128, NH], F32),
        "bdc_bc": A("s_bdcbc", [128, NH], F32),
        "kdb_all": A("s_kdball", [128, 2 * NH], F32),
        "kdpre_all": A("s_kdpall", [128, 8 * NH], F32),
        "rowA": A("s_rowA", [1, 1024], F32),
        "rowA2": A("s_rowA2", [1, 1024], F32),
        "rowB": A("s_rowB", [1, 1024], F32),
        "gacc": A("s_gacc", [128, 4], F32),
        "gtmp": A("s_gtmp", [128, 4], F32),
        "gsum": A("s_gsum", [1, 4], F32),
        "th": A("s_th", [1, 4], F32),
        "nth": A("s_nth", [1, 4], F32),
        "thb": A("s_thb", [128, 4], F32),
        "nthb": A("s_nthb", [128, 4], F32),
        "kvbf": A("s_kvbf", [128, 128], BF16),
        "scrB": SCRB,
    }
    ld_const = R.sem("ldconst")
    CB = load_consts(nc, R,
                     ("IDX0", "IDX1", "QDROW", "KD2", "KDPRE", "ONESR_F",
                      "ONESC_F", "IDEN_BF", "ONESC_BF", "CCOL", "CROW"),
                     consts, ld_const)
    S.update(CB)
    R.dma("sync", S["slope"][:], slope, ld_const)
    R.wait("vector", ld_const)
    R.wait("tensor", ld_const)
    sder = emit_slope_derived(R, S, BK, ld_const)

    gb = [SCRF[:, 4096:4608], SCRF[:, 4608:5120]]
    gld = Stream(R, "gld", 2)
    gred = R.sem("gred")
    gbar = R.sem("gbar")
    gsrc = []
    for wi, wt in enumerate((w1t, w2t, w3t)):
        for kk in range(176):
            gsrc.append((wi, wt[:, kk * 512:(kk + 1) * 512]))
    gstate = {"dma": 0, "done": 0, "init": False, "tgt": 0.0}
    GT_PRE_PROJ, GT_PRE_ATTN, GT_OWN_PROJ, GT_OWN_ATTN = 3.5, 1.0, 1.8, 1.8

    def emit_gamma_dmas(upto):
        upto = min(upto, gstate["done"] + 2, GCH)
        while gstate["dma"] < upto:
            nch = gstate["dma"]
            if nch >= 2:
                R.wait("gpsimd", gred, nch - 1)
            gld.dma("gpsimd", nch % 2, gb[nch % 2][:], gsrc[nch][1])
            gstate["dma"] += 1

    def emit_gamma_chunks(upto):
        if not gstate["init"]:
            R.op("vector", "memset", S["gacc"][:], 0.0)
            gstate["init"] = True
        while gstate["done"] < min(upto, GCH):
            nch = gstate["done"]
            emit_gamma_dmas(nch + 2)
            wi = gsrc[nch][0]
            gld.wait("vector", nch % 2, nch // 2 + 1)
            R.op("vector", "tensor_reduce", out=S["gtmp"][:, wi:wi + 1],
                 in_=gb[nch % 2][:], axis=AX, op=ALU.add,
                 apply_absolute_value=True).inc(gbar, 1)
            R.wait("vector", gbar, nch + 1)
            R.op("vector", "tensor_tensor", out=S["gacc"][:, wi:wi + 1],
                 in0=S["gacc"][:, wi:wi + 1], in1=S["gtmp"][:, wi:wi + 1],
                 op=ALU.add).inc(gred, 1)
            gstate["done"] += 1
        emit_gamma_dmas(gstate["done"] + 2)

    def gtick(rate):
        gstate["tgt"] += rate
        emit_gamma_chunks(min(int(gstate["tgt"]), GCH))

    ldx = Stream(R, "ldx", 2)
    xc = [SCRF[:, 0:1024], SCRF[:, 1024:2048]]
    cast_done = R.sem("castdone")

    def emit_x_chunk(src_cols, guard_pe):
        base = cast_done.v
        for fc in range(FC):
            p = fc % 2
            if fc >= 2:
                R.wait("sync", cast_done, base + fc - 1)
            elif base > 0:
                R.wait("sync", cast_done, base)
            ldx.dma("sync", p, xc[p][:], src_cols(fc))
            ldx.wait("vector", p)
            if guard_pe and fc < 2:
                R.wait("vector", pe, guard_pe)
            R.op("vector", "tensor_copy", out=AR1[:, fc * NT:(fc + 1) * NT],
                 in_=xc[p][:]).inc(cast_done, 1)

    wld = Stream(R, "wld", 2)
    wcast = R.sem("wcast")
    WG = [SCRF[:, 2048:3072], SCRF[:, 3072:4096]]
    sevq = R.sem("sevq")
    vev = R.sem("vev")
    trev = R.sem("trev")
    sevu = R.sem("sevu")
    uev = Stream(R, "uev", 2)
    vstg = [SCRB[:, 3072 + i * 512: 3584 + i * 512] for i in range(2)]
    ustg = [SCRB[:, 4096 + i * 512: 4608 + i * 512] for i in range(2)]
    PRM = {"g": 0, "vtr_pe": {}, "un": 0, "stg4": {}}

    def emit_proj_phase(granules, wsrc, src, ln_done, prev_guard_pe,
                        grate=0.0):
        ng = len(granules)
        gp_after = {}
        wcast_after = {}

        def wload(gq):
            oc = granules[gq][0]
            for half in range(2):
                gg = PRM["g"]
                PRM["g"] += 1
                bgi = gg % 4
                if gg >= 2:
                    R.wait("sync", wcast, gg - 1)
                wld.dma("sync", gg % 2, WG[gg % 2][:],
                        wsrc[:, oc * 2048 + half * 1024: oc * 2048 + (half + 1) * 1024])
                wld.wait("vector", gg % 2)
                if gq >= 2:
                    R.wait("vector", pe, gp_after[gq - 2])
                elif prev_guard_pe:
                    R.wait("vector", pe, prev_guard_pe)
                R.op("vector", "tensor_copy", out=WGB[bgi][:],
                     in_=WG[gg % 2][:]).inc(wcast, 1)
            wcast_after[gq] = wcast.v

        wload(0)
        for j in range(ng):
            if j + 1 < ng:
                wload(j + 1)
            R.wait("tensor", wcast, wcast_after[j])
            if j == 0 and ln_done is not None:
                R.wait("tensor", ln_done, FC)
            _, kind, jj, dst = granules[j]
            b_lo, b_hi = WGB[(2 * j) % 4], WGB[(2 * j + 1) % 4]
            pnames = ("b0", "b1") if j % 2 == 0 else ("b2", "b3")
            for tc_ in range(2):
                bn = pnames[tc_]
                BK[bn].acquire()
                for kc in range(16):
                    wb = b_lo if kc < 8 else b_hi
                    off = (kc % 8) * 128
                    o = R.op("tensor", "matmul", BK[bn].t[:, 0:512],
                             wb[:, off:off + 128],
                             src[:, kc * NT + tc_ * 512: kc * NT + tc_ * 512 + 512],
                             start=(kc == 0), stop=(kc == 15))
                o.inc(pe, 1)
                pe_now = pe.v
                if kind in ("q", "k"):
                    dslice = dst[:, jj * NT + tc_ * 512: jj * NT + tc_ * 512 + 512]
                    R.wait("scalar", pe, pe_now)
                    R.op("scalar", "activation", out=dslice,
                         in_=BK[bn].t[:, 0:512], func=AF.Silu).inc(sevq, 1)
                    BK[bn].release("scalar")
                elif kind == "v":
                    R.wait("vector", pe, pe_now)
                    key = tc_
                    if key in PRM["vtr_pe"]:
                        R.wait("vector", pe, PRM["vtr_pe"][key])
                    R.op("vector", "tensor_copy", out=vstg[tc_][:],
                         in_=BK[bn].t[:, 0:512]).inc(vev, 1)
                    BK[bn].release("vector")
                    BK["b6"].acquire()
                    R.wait("tensor", vev)
                    for i4 in range(4):
                        o = R.op("tensor", "matmul",
                                 BK["b6"].t[:, i4 * 128:(i4 + 1) * 128],
                                 vstg[tc_][:, i4 * 128:(i4 + 1) * 128],
                                 S["IDEN_BF"][:], start=True, stop=True)
                    o.inc(pe, 1)
                    PRM["vtr_pe"][key] = pe.v
                    R.wait("vector", pe, pe.v)
                    for i4 in range(4):
                        tch = tc_ * 4 + i4
                        R.op("vector", "tensor_copy",
                             out=dst[:, tch * DM + jj * 128: tch * DM + (jj + 1) * 128],
                             in_=BK["b6"].t[:, i4 * 128:(i4 + 1) * 128]).inc(trev, 1)
                    BK["b6"].release("vector")
                else:
                    un = PRM["un"]
                    PRM["un"] += 1
                    R.wait("vector", pe, pe_now)
                    if un >= 2:
                        uev.wait("vector", un % 2, un // 2)
                    R.op("vector", "tensor_copy", out=ustg[un % 2][:],
                         in_=BK[bn].t[:, 0:512]).inc(sevu, 1)
                    BK[bn].release("vector")
                    R.wait("scalar", sevu)
                    uev.dma("scalar", un % 2,
                            u_spill[:, jj * NT + tc_ * 512: jj * NT + tc_ * 512 + 512],
                            ustg[un % 2][:])
            gp_after[j] = pe.v
            if grate:
                gtick(grate)
        return gp_after, pe.v

    sktok = R.sem("sktok")
    skv = R.sem("skv")
    stageC = vstg

    def emit_prefix_attn(cidx, k_evac_total, v_evac_total, proj_tail):
        R.wait("tensor", sevq, k_evac_total)
        R.wait("tensor", trev, v_evac_total)
        R.wait("vector", sder)
        stg4 = [SCRB[:, 4096 + i * 512: 4608 + i * 512] for i in range(4)]
        trb = ("b6", "b0")
        C_tok = {}
        sk_after = {}

        def emit_tr(h):
            for half in range(2):
                slot = (h % 2) * 2 + half
                BK[trb[half]].acquire()
                for i4 in range(4):
                    tb = half * 4 + i4
                    o = R.op("tensor", "matmul",
                             BK[trb[half]].t[:, i4 * 128:(i4 + 1) * 128],
                             AR3[:, h * NT + tb * 128: h * NT + (tb + 1) * 128],
                             S["IDEN_BF"][:], start=True, stop=True)
                o.inc(pe, 1)
                R.wait("vector", pe, pe.v)
                g = PRM["stg4"].get(slot)
                if g:
                    R.wait("vector", pe, g)
                for i4 in range(4):
                    tb = half * 4 + i4
                    o = R.op("vector", "tensor_scalar",
                             out=stg4[slot][:, i4 * 128:(i4 + 1) * 128],
                             in0=BK[trb[half]].t[:, i4 * 128:(i4 + 1) * 128],
                             scalar1=S["kdpre_all"][:, h * 8 + tb: h * 8 + tb + 1],
                             scalar2=None, op0=ALU.mult)
                o.inc(sktok, 1)
                BK[trb[half]].release("vector")
            sk_after[h] = sktok.v

        emit_tr(0)
        for h in range(NH):
            if h + 1 < NH:
                emit_tr(h + 1)
            cb = BK["b7"] if h % 2 == 0 else BK["b1"]
            cb.acquire()
            R.wait("tensor", sktok, sk_after[h])
            for tb in range(8):
                slot = (h % 2) * 2 + tb // 4
                o = R.op("tensor", "matmul", cb.t[:, 0:128],
                         stg4[slot][:, (tb % 4) * 128:(tb % 4) * 128 + 128],
                         AR2[:, tb * DM + h * 128: tb * DM + (h + 1) * 128],
                         start=(tb == 0), stop=(tb == 7))
            o.inc(pe, 1)
            PRM["stg4"][(h % 2) * 2] = PRM["stg4"][(h % 2) * 2 + 1] = pe.v
            C_tok[h] = pe.v
            R.wait("vector", pe, C_tok[h])
            kvslice = KVOUT[:, h * 128:(h + 1) * 128]
            if cidx == 0:
                R.op("vector", "tensor_copy", out=kvslice,
                     in_=cb.t[:, 0:128]).inc(skv, 1)
            else:
                R.op("vector", "scalar_tensor_tensor", out=kvslice,
                     in0=kvslice, scalar=S["bdc_bc"][:, h:h + 1],
                     in1=cb.t[:, 0:128], op0=ALU.mult, op1=ALU.add).inc(skv, 1)
            cb.release("vector")
            gtick(GT_PRE_ATTN)
        return pe.v

    prev_tail = None
    for c in range(NPRE):
        emit_x_chunk(lambda fc, c=c: xTpre[fc * 128:(fc + 1) * 128,
                                           c * NT:(c + 1) * NT], prev_tail)
        R.phases.append((f"pre{c}_ln", R.twc()))
        lnp = emit_layernorm(R, S, BK, AR1, AR1, S["CROW"][0:1, 0:1],
                             cast_done, f"p{c}")
        gran = [(16 + j, "k", j, AR3) for j in range(16)] + \
               [(32 + j, "v", j, AR2) for j in range(16)]
        R.phases.append((f"pre{c}_proj", R.twc()))
        _, tail = emit_proj_phase(gran, wqkvu, AR1, lnp, prev_tail,
                                  GT_PRE_PROJ)
        R.phases.append((f"pre{c}_attn", R.twc()))
        emit_prefix_attn(c, sevq.v, trev.v, tail)
        prev_tail = tail

    emit_x_chunk(lambda fc: xT[fc * 128:(fc + 1) * 128, :], prev_tail)
    R.phases.append(("own_ln", R.twc()))
    ln1 = emit_layernorm(R, S, BK, AR1, AR1, S["CROW"][0:1, 0:1],
                         cast_done, "1")
    gran = [(j, "q", j, AR2) for j in range(16)] + \
           [(16 + j, "k", j, AR3) for j in range(16)] + \
           [(48 + j, "u", j, None) for j in range(16)] + \
           [(32 + j, "v", j, AR4) for j in range(16)]
    R.phases.append(("own_proj", R.twc()))
    gp_own, own_tail = emit_proj_phase(gran, wqkvu, AR1, ln1, prev_tail,
                                       GT_OWN_PROJ)
    q_total, k_total, v_total = sevq.v, sevq.v, trev.v

    qkmb = [SCRB[:, 3072 + i * 512: 3584 + i * 512] for i in range(2)]
    R.phases.append(("own_attn", R.twc()))
    maskb = [SCRB[:, 4096 + i * 512: 4608 + i * 512] for i in range(2)]
    qdb = [SCRB[:, 5120 + i * 256: 5376 + i * 256] for i in range(2)]
    ktokb = [SCRB[:, 5632 + i * 256: 5888 + i * 256] for i in range(2)]
    smask = R.sem("smask")
    sqs = R.sem("sqs")
    skvbf = R.sem("skvbf")
    sqkm = R.sem("sqkm")
    soev = R.sem("soev")
    uload = R.sem("uload")
    sq2 = R.sem("sq2")
    osq = [SCRB[:, 0:1024], SCRB[:, 1024:2048]]
    stats_pe = {}
    BK["b4"].acquire()
    BK["b5"].acquire()
    R.wait("vector", pe, own_tail)
    R.wait("vector", sder)
    R.wait("tensor", sevq, q_total)
    R.wait("tensor", trev, v_total)
    R.wait("vector", sevq, q_total)
    uev.wait_all("scalar")
    C_pe_after = {}
    diag_pe_after = {}
    for h in range(NH):
        hp = h % 2
        R.wait("scalar", sder)
        if h >= 2:
            R.wait("scalar", sqkm, 4 * (h - 1))
            R.wait("scalar", sqs, h - 1)
        R.op("scalar", "activation", out=maskb[hp][:, 0:256], in_=S["IDX0"][:],
             func=AF.Exp, scale=S["sneg"][:, h:h + 1])
        R.op("scalar", "activation", out=maskb[hp][:, 256:512], in_=S["IDX1"][:],
             func=AF.Exp, scale=S["sneg"][:, h:h + 1])
        R.op("scalar", "activation", out=qdb[hp][:], in_=S["QDROW"][:],
             func=AF.Exp, scale=S["sneg"][:, h:h + 1]).inc(smask, 1)
        R.wait("vector", smask, h + 1)
        qd = qdb[hp]
        qrep = bass.AP(qd.tensor, qd.offset, [[qd.ap[0][0], 128], [0, 4], [1, 256]])
        R.op("vector", "tensor_tensor",
             out=AR1[:, h * NT:(h + 1) * NT].rearrange("p (b m) -> p b m", b=4),
             in0=AR2[:, h * NT:(h + 1) * NT].rearrange("p (b m) -> p b m", b=4),
             in1=qrep, op=ALU.mult).inc(sqs, 1)
        kvslice = KVOUT[:, h * 128:(h + 1) * 128]
        R.op("vector", "tensor_copy", out=S["kvbf"][:], in_=kvslice)\
            .inc(skvbf, 1)
        for i in range(NBLK):
            blk = h * NBLK + i
            BK["b6"].acquire()
            o = R.op("tensor", "matmul", BK["b6"].t[:, 0:128],
                     AR3[:, h * NT + i * 256: h * NT + i * 256 + 128],
                     S["IDEN_BF"][:], start=True, stop=True)
            o = R.op("tensor", "matmul", BK["b6"].t[:, 128:256],
                     AR3[:, h * NT + i * 256 + 128: h * NT + i * 256 + 256],
                     S["IDEN_BF"][:], start=True, stop=True)
            o.inc(pe, 1)
            tr_pe = pe.v
            R.wait("vector", pe, tr_pe)
            prevu = blk - 1 if i > 0 else ((h - 2) * NBLK + 3 if h >= 2 else None)
            if prevu is not None and prevu in C_pe_after:
                R.wait("vector", pe, C_pe_after[prevu])
            R.op("vector", "tensor_scalar", out=ktokb[hp][:, 0:128],
                 in0=BK["b6"].t[:, 0:128],
                 scalar1=S["kdb_all"][:, 2 * h:2 * h + 1],
                 scalar2=None, op0=ALU.mult)
            R.op("vector", "tensor_scalar", out=ktokb[hp][:, 128:256],
                 in0=BK["b6"].t[:, 128:256],
                 scalar1=S["kdb_all"][:, 2 * h + 1:2 * h + 2],
                 scalar2=None, op0=ALU.mult).inc(sktok, 1)
            BK["b6"].release("vector")
            BK["b7"].acquire()
            R.wait("tensor", sktok)
            tch = 2 * i
            R.op("tensor", "matmul", BK["b7"].t[:, 0:128],
                 ktokb[hp][:, 0:128],
                 AR4[:, tch * DM + h * 128: tch * DM + (h + 1) * 128],
                 start=True, stop=False)
            R.op("tensor", "matmul", BK["b7"].t[:, 0:128],
                 ktokb[hp][:, 128:256],
                 AR4[:, (tch + 1) * DM + h * 128: (tch + 1) * DM + (h + 1) * 128],
                 start=False, stop=True).inc(pe, 1)
            C_pe_after[blk] = pe.v
            qkbn = "b0" if i % 2 == 0 else "b3"
            BK[qkbn].acquire()
            R.op("tensor", "matmul", BK[qkbn].t[:, 0:256],
                 AR3[:, h * NT + i * 256: h * NT + i * 256 + 128],
                 AR2[:, h * NT + i * 256: h * NT + (i + 1) * 256],
                 start=True, stop=True)
            R.op("tensor", "matmul", BK[qkbn].t[:, 256:512],
                 AR3[:, h * NT + i * 256 + 128: h * NT + i * 256 + 256],
                 AR2[:, h * NT + i * 256: h * NT + (i + 1) * 256],
                 start=True, stop=True).inc(pe, 1)
            qk_pe = pe.v
            R.wait("vector", pe, qk_pe)
            prevq = blk - 1 if i > 0 else ((h - 2) * NBLK + 3 if h >= 2 else None)
            if prevq is not None and prevq in diag_pe_after:
                R.wait("vector", pe, diag_pe_after[prevq])
            R.op("vector", "tensor_tensor", out=qkmb[hp][:, 0:256],
                 in0=BK[qkbn].t[:, 0:256], in1=maskb[hp][:, 0:256], op=ALU.mult)
            R.op("vector", "tensor_tensor", out=qkmb[hp][:, 256:512],
                 in0=BK[qkbn].t[:, 256:512], in1=maskb[hp][:, 256:512],
                 op=ALU.mult).inc(sqkm, 1)
            BK[qkbn].release("vector")
            ob = "b1" if i % 2 == 0 else "b2"
            BK[ob].acquire()
            R.wait("tensor", skvbf, 4 * h + i + 1)
            R.wait("tensor", sqs, h + 1)
            R.op("tensor", "matmul", BK[ob].t[:, 0:256], S["kvbf"][:],
                 AR1[:, h * NT + i * 256: h * NT + (i + 1) * 256],
                 start=True, stop=False)
            R.wait("tensor", sqkm, blk + 1)
            assert skvbf.v == 4 * h + i + 1
            R.op("tensor", "matmul", BK[ob].t[:, 0:256],
                 AR4[:, tch * DM + h * 128: tch * DM + (h + 1) * 128],
                 qkmb[hp][:, 0:256], start=False, stop=False)
            R.op("tensor", "matmul", BK[ob].t[:, 0:256],
                 AR4[:, (tch + 1) * DM + h * 128: (tch + 1) * DM + (h + 1) * 128],
                 qkmb[hp][:, 256:512], start=False, stop=True).inc(pe, 1)
            diag_pe_after[blk] = pe.v
            R.wait("vector", pe, pe.v)
            R.op("vector", "scalar_tensor_tensor", out=kvslice, in0=kvslice,
                 scalar=S["bd_bc"][:, h:h + 1], in1=BK["b7"].t[:, 0:128],
                 op0=ALU.mult, op1=ALU.add).inc(skv, 1)
            BK["b7"].release("vector")
            if i < NBLK - 1:
                R.op("vector", "tensor_copy", out=S["kvbf"][:], in_=kvslice)\
                    .inc(skvbf, 1)
            R.op("vector", "tensor_copy",
                 out=AR2[:, h * NT + i * 256: h * NT + (i + 1) * 256],
                 in_=BK[ob].t[:, 0:256]).inc(soev, 1)
            BK[ob].release("vector")
            if i == NBLK - 1:
                R.wait("scalar", soev)
                if h >= 2:
                    R.wait("scalar", pe, stats_pe[h - 2])
                R.op("scalar", "activation", out=osq[h % 2][:],
                     in_=AR2[:, h * NT:(h + 1) * NT], func=AF.Square)\
                    .inc(sq2, 1)
                if h >= 1:
                    hh = h - 1
                    R.wait("tensor", sq2, hh + 1)
                    for half in range(2):
                        o = R.op("tensor", "matmul",
                                 (BK["b4"] if half == 0 else BK["b5"]).t[0:1, 0:512],
                                 S["ONESC_BF"][:],
                                 osq[hh % 2][:, half * 512: half * 512 + 512],
                                 start=(hh == 0), stop=False)
                    o.inc(pe, 1)
                    stats_pe[hh] = pe.v
                if h == 7:
                    uev.wait_all("sync")
                    R.wait("sync", pe, diag_pe_after[31])
                    for fc in range(0, FC, 2):
                        R.dma("sync", AR4[:, fc * NT:(fc + 1) * NT],
                              u_spill[:, fc * NT:(fc + 1) * NT], uload)
                if h == 15:
                    R.wait("sync", pe, diag_pe_after[63])
                    for fc in range(1, FC, 2):
                        R.dma("sync", AR4[:, fc * NT:(fc + 1) * NT],
                              u_spill[:, fc * NT:(fc + 1) * NT], uload)
                gtick(GT_OWN_ATTN)

    R.phases.append(("B3", R.twc()))
    R.wait("tensor", sq2, 16)
    for half in range(2):
        o = R.op("tensor", "matmul",
                 (BK["b4"] if half == 0 else BK["b5"]).t[0:1, 0:512],
                 S["ONESC_BF"][:], osq[15 % 2][:, half * 512: half * 512 + 512],
                 start=False, stop=True)
    o.inc(pe, 1)
    stats_pe[15] = pe.v
    R.wait("scalar", pe, stats_pe[15])
    R.op("scalar", "activation", out=S["rowA2"][0:1, 0:512],
         in_=BK["b4"].t[0:1, 0:512], func=AF.Sqrt,
         scale=S["CROW"][0:1, 3:4], bias=S["CROW"][0:1, 1:2])
    R.op("scalar", "activation", out=S["rowA2"][0:1, 512:1024],
         in_=BK["b5"].t[0:1, 0:512], func=AF.Sqrt,
         scale=S["CROW"][0:1, 3:4], bias=S["CROW"][0:1, 1:2]).inc(act, 1)
    BK["b4"].release("scalar")
    BK["b5"].release("scalar")
    R.wait("vector", act)
    R.op("vector", "reciprocal", out=S["rowA"][0:1, 0:1024],
         in_=S["rowA2"][0:1, 0:1024]).inc(dve, 1)
    BK["b4"].acquire()
    BK["b5"].acquire()
    R.wait("tensor", dve)
    R.op("tensor", "matmul", BK["b4"].t[:, 0:512], S["ONESR_F"][:],
         S["rowA"][0:1, 0:512], start=True, stop=True)
    R.op("tensor", "matmul", BK["b5"].t[:, 0:512], S["ONESR_F"][:],
         S["rowA"][0:1, 512:1024], start=True, stop=True).inc(pe, 1)
    rstdb = SCRB[:, 2048:3072]
    R.wait("vector", pe)
    R.op("vector", "tensor_copy", out=rstdb[:, 0:512], in_=BK["b4"].t[:, 0:512])
    R.op("vector", "tensor_copy", out=rstdb[:, 512:1024],
         in_=BK["b5"].t[:, 0:512]).inc(dve, 1)
    BK["b4"].release("vector")
    BK["b5"].release("vector")

    R.phases.append(("thresh", R.twc()))
    emit_gamma_chunks(GCH)
    BK["b7"].acquire()
    R.wait("tensor", gred, GCH)
    R.op("tensor", "matmul", BK["b7"].t[0:1, 0:4], S["ONESC_F"][:],
         S["gacc"][:], start=True, stop=True).inc(pe, 1)
    R.wait("vector", pe)
    R.op("vector", "tensor_copy", out=S["gsum"][:], in_=BK["b7"].t[0:1, 0:4])\
        .inc(dve, 1)
    BK["b7"].release("vector")
    R.op("vector", "tensor_scalar", out=S["th"][:], in0=S["gsum"][:],
         scalar1=1.0 / NW_ELEMS, scalar2=1e-8, op0=ALU.mult, op1=ALU.add)
    R.op("vector", "tensor_scalar", out=S["th"][:], in0=S["th"][:],
         scalar1=0.5, scalar2=None, op0=ALU.mult)
    R.op("vector", "tensor_scalar", out=S["nth"][:], in0=S["th"][:],
         scalar1=-1.0, scalar2=None, op0=ALU.mult).inc(dve, 1)
    BK["b5"].acquire()
    R.wait("tensor", dve)
    R.op("tensor", "matmul", BK["b5"].t[:, 0:4], S["ONESR_F"][:], S["th"][:],
         start=True, stop=True)
    R.op("tensor", "matmul", BK["b5"].t[:, 4:8], S["ONESR_F"][:], S["nth"][:],
         start=True, stop=True).inc(pe, 1)
    R.wait("vector", pe)
    R.op("vector", "tensor_copy", out=S["thb"][:], in_=BK["b5"].t[:, 0:4])
    R.op("vector", "tensor_copy", out=S["nthb"][:], in_=BK["b5"].t[:, 4:8])\
        .inc(dve, 1)
    BK["b5"].release("vector")
    th_ready = dve.v

    gdone = R.sem("gdone")
    ttmp = SCRB[:, 3072:4096]
    R.wait("vector", uload, 16 * 16)
    for fc in range(FC):
        R.op("vector", "tensor_mul", out=ttmp[:],
             in0=AR2[:, fc * NT:(fc + 1) * NT], in1=rstdb[:])
        R.op("vector", "tensor_mul", out=AR3[:, fc * NT:(fc + 1) * NT],
             in0=ttmp[:], in1=AR4[:, fc * NT:(fc + 1) * NT]).inc(gdone, 1)

    R.phases.append(("B4", R.twc()))
    xch = [SCRF[:, 0:1024], SCRF[:, 1024:2048]]
    x1done = R.sem("x1done")
    gp_B4 = {}
    b4_wc = {}

    def emit_wload4(jq):
        p = jq % 2
        if jq >= 2:
            R.wait("sync", x1done, 2 * (jq - 1))
        ldx.dma("sync", p, xch[p][:], xT[jq * 128:(jq + 1) * 128, :])
        for half in range(2):
            gg = PRM["g"]
            PRM["g"] += 1
            if gg >= 2:
                R.wait("sync", wcast, gg - 1)
            wld.dma("sync", gg % 2, WG[gg % 2][:],
                    wout[:, jq * 2048 + half * 1024: jq * 2048 + (half + 1) * 1024])
            wld.wait("vector", gg % 2)
            if jq >= 2:
                R.wait("vector", pe, gp_B4[jq - 2])
            else:
                R.wait("vector", pe, own_tail)
            R.op("vector", "tensor_copy", out=WGB[gg % 4][:],
                 in_=WG[gg % 2][:]).inc(wcast, 1)
        b4_wc[jq] = wcast.v

    R.wait("sync", cast_done)
    emit_wload4(0)
    for j in range(16):
        p = j % 2
        if j + 1 < 16:
            emit_wload4(j + 1)
        R.wait("tensor", wcast, b4_wc[j])
        if j == 0:
            R.wait("tensor", gdone, FC)
        b_lo, b_hi = WGB[(2 * j) % 4], WGB[(2 * j + 1) % 4]
        pnames = ("b0", "b1") if j % 2 == 0 else ("b2", "b3")
        for tc_ in range(2):
            bn = pnames[tc_]
            BK[bn].acquire()
            for kc in range(16):
                wb = b_lo if kc < 8 else b_hi
                off = (kc % 8) * 128
                o = R.op("tensor", "matmul", BK[bn].t[:, 0:512],
                         wb[:, off:off + 128],
                         AR3[:, kc * NT + tc_ * 512: kc * NT + tc_ * 512 + 512],
                         start=(kc == 0), stop=(kc == 15))
            o.inc(pe, 1)
            R.wait("vector", pe)
            ldx.wait("vector", p)
            R.op("vector", "tensor_tensor",
                 out=AR2[:, j * NT + tc_ * 512: j * NT + tc_ * 512 + 512],
                 in0=BK[bn].t[:, 0:512],
                 in1=xch[p][:, tc_ * 512: tc_ * 512 + 512],
                 op=ALU.add).inc(x1done, 1)
            BK[bn].release("vector")
        gp_B4[j] = pe.v

    R.phases.append(("B5_ln2", R.twc()))
    ln2_done = emit_layernorm(R, S, BK, AR2, AR1, S["CROW"][0:1, 0:1],
                              x1done, "2")
    spill = R.sem("spill")
    R.wait("scalar", ln2_done)
    R.dma("scalar", x1_spill, AR2[:], spill)

    def hidden_ap(kc, lo, width):
        if kc < 16:
            return AR2[:, kc * NT + lo: kc * NT + lo + width]
        if kc < 32:
            return AR4[:, (kc - 16) * NT + lo: (kc - 16) * NT + lo + width]
        return AR3[:, (kc - 32) * NT + lo: (kc - 32) * NT + lo + width]

    R.phases.append(("B7", R.twc()))
    f32bufs = {"w1": [SCRF[:, 0:1024], SCRF[:, 1024:2048]],
               "w2": [SCRF[:, 2048:3072], SCRF[:, 3072:4096]]}
    wqbufs = {"w1": [[WQ[:, 0:1024], WQ[:, 1024:2048]],
                     [WQ[:, 4096:5120], WQ[:, 5120:6144]]],
              "w2": [[WQ[:, 2048:3072], WQ[:, 3072:4096]],
                     [WQ[:, 6144:7168], WQ[:, 7168:8192]]]}
    tsD = SCRB[:, 0:2048]
    tsP = SCRB[:, 2048:4096]
    sgb = [SCRB[:, 4096 + i * 512: 4608 + i * 512] for i in range(4)]
    fld = {"w1": Stream(R, "fld1", 2), "w2": Stream(R, "fld2", 2)}
    qbar = R.sem("qbarD")
    qbar_n = [0]
    fq = {"w1": R.sem("fq1"), "w2": R.sem("fq2")}
    sgd = R.sem("sgd")
    hid = R.sem("hid")
    wsrc = {"w1": w1t, "w2": w2t}
    thcol = {"w1": 0, "w2": 1, "w3": 2}
    qtok = {}
    R.wait("sync", spill, 16)
    R.wait("sync", ln2_done)
    R.wait("vector", spill, 16)
    R.wait("gpsimd", spill, 16)
    R.wait("vector", gdone)
    uev.wait_all("scalar")
    mm_pe_after = {}

    def emit_quant7(jq):
        for wname in ("w1", "w2"):
            ts = tsD if wname == "w1" else tsP
            for half in range(2):
                if (wname, half) in qtok:
                    R.wait("sync", qbar, qtok[(wname, half)])
                fld[wname].dma("sync", half, f32bufs[wname][half][:],
                               wsrc[wname][:, jq * 2048 + half * 1024: jq * 2048 + (half + 1) * 1024])
                fld[wname].wait("vector", half)
                R.wait("vector", dve, th_ready)
                if 2 * jq + half >= 1:
                    R.wait("vector", fq[wname], 2 * jq + half)
                col = thcol[wname]
                src = f32bufs[wname][half][:]
                dst = wqbufs[wname][jq % 2][half][:]
                R.op("vector", "tensor_scalar", out=ts[:, 0:1024], in0=src,
                     scalar1=S["thb"][:, col:col + 1], scalar2=None,
                     op0=ALU.is_ge).inc(qbar, 1)
                R.op("vector", "tensor_scalar", out=ts[:, 1024:2048], in0=src,
                     scalar1=S["nthb"][:, col:col + 1], scalar2=None,
                     op0=ALU.is_le).inc(qbar, 1)
                qbar_n[0] += 2
                qtok[(wname, half)] = qbar_n[0]
                R.wait("gpsimd", qbar, qbar_n[0])
                if jq >= 2 and (jq - 2, wname) in mm_pe_after:
                    R.wait("gpsimd", pe, mm_pe_after[(jq - 2, wname)])
                R.op("gpsimd", "tensor_tensor", out=dst, in0=ts[:, 0:1024],
                     in1=ts[:, 1024:2048], op=ALU.subtract).inc(fq[wname], 1)

    w2banks = [("b4", "b5"), ("b6", "b7")]
    emit_quant7(0)
    for j in range(44):
        if j + 1 < 44:
            emit_quant7(j + 1)
        R.wait("tensor", fq["w1"], 2 * (j + 1))
        R.wait("tensor", fq["w2"], 2 * (j + 1))
        x1banks = ("b0", "b1") if j % 2 == 0 else ("b2", "b3")
        for tc_ in range(2):
            bn = x1banks[tc_]
            BK[bn].acquire()
            for kc in range(16):
                wb = wqbufs["w1"][j % 2][kc // 8]
                off = (kc % 8) * 128
                o = R.op("tensor", "matmul", BK[bn].t[:, 0:512],
                         wb[:, off:off + 128],
                         AR1[:, kc * NT + tc_ * 512: kc * NT + tc_ * 512 + 512],
                         start=(kc == 0), stop=(kc == 15))
            o.inc(pe, 1)
            if tc_ == 1:
                mm_pe_after[(j, "w1")] = pe.v
            bn2 = w2banks[j % 2][tc_]
            BK[bn2].acquire()
            for kc in range(16):
                wb = wqbufs["w2"][j % 2][kc // 8]
                off = (kc % 8) * 128
                o = R.op("tensor", "matmul", BK[bn2].t[:, 0:512],
                         wb[:, off:off + 128],
                         AR1[:, kc * NT + tc_ * 512: kc * NT + tc_ * 512 + 512],
                         start=(kc == 0), stop=(kc == 15))
            o.inc(pe, 1)
            if tc_ == 1:
                mm_pe_after[(j, "w2")] = pe.v
            sg = sgb[(j % 2) * 2 + tc_]
            R.wait("scalar", pe)
            if j >= 2:
                R.wait("scalar", hid, 2 * (j - 2) + tc_ + 1)
            R.op("scalar", "activation", out=sg[:], in_=BK[bn].t[:, 0:512],
                 func=AF.Sigmoid).inc(sgd, 1)
            BK[bn].release("scalar")
            R.wait("vector", sgd)
            R.op("vector", "tensor_tensor", out=hidden_ap(j, tc_ * 512, 512),
                 in0=BK[bn2].t[:, 0:512], in1=sg[:], op=ALU.mult)\
                .inc(hid, 1)
            BK[bn2].release("vector")

    R.phases.append(("B8", R.twc()))
    w3f = [SCRF[:, 0:1408], SCRF[:, 1408:2816]]
    OSTG = [SCRF[:, 2816:3328], SCRF[:, 3328:3840]]
    w3q = [[AR1[:, par * 5632 + i * 1408: par * 5632 + (i + 1) * 1408]
            for i in range(4)] for par in range(2)]
    t3a = [WQ[:, 0:1408], WQ[:, 2816:4224]]
    t3b = [WQ[:, 1408:2816], WQ[:, 4224:5632]]
    x1b = [SCRB[:, 0:1024], SCRB[:, 1024:2048]]
    f3ld = Stream(R, "f3ld", 2)
    f3q = R.sem("f3q")
    oev = R.sem("oev")
    ost = Stream(R, "ost", 2)
    R.wait("sync", fq["w1"], 88)
    R.wait("sync", fq["w2"], 88)
    R.wait("vector", sgd, 88)
    R.wait("vector", hid, 88)
    R.wait("gpsimd", pe)
    qtok3 = {}
    w3_pe_after = {}

    def emit_quant8(jq):
        p = jq % 2
        if jq >= 2:
            R.wait("sync", oev, (jq - 1) * 2)
        ldx.dma("sync", p, x1b[p][:], x1_spill[:, jq * NT:(jq + 1) * NT])
        for part in range(4):
            g = jq * 4 + part
            if g >= 2 and (g - 2) in qtok3:
                R.wait("sync", qbar, qtok3[g - 2])
            f3ld.dma("sync", part % 2, w3f[part % 2][:],
                     w3t[:, jq * 5632 + part * 1408: jq * 5632 + (part + 1) * 1408])
            f3ld.wait("vector", part % 2)
            R.wait("vector", dve, th_ready)
            col = thcol["w3"]
            if g >= 2:
                R.wait("vector", f3q, g - 1)
            gp2 = g % 2
            R.op("vector", "tensor_scalar", out=t3a[gp2][:], in0=w3f[part % 2][:],
                 scalar1=S["thb"][:, col:col + 1], scalar2=None, op0=ALU.is_ge)\
                .inc(qbar, 1)
            R.op("vector", "tensor_scalar", out=t3b[gp2][:], in0=w3f[part % 2][:],
                 scalar1=S["nthb"][:, col:col + 1], scalar2=None, op0=ALU.is_le)\
                .inc(qbar, 1)
            qbar_n[0] += 2
            qtok3[g] = qbar_n[0]
            R.wait("gpsimd", qbar, qbar_n[0])
            if jq >= 2 and (jq - 2) in w3_pe_after:
                R.wait("gpsimd", pe, w3_pe_after[jq - 2])
            R.op("gpsimd", "tensor_tensor", out=w3q[jq % 2][part][:],
                 in0=t3a[gp2][:], in1=t3b[gp2][:], op=ALU.subtract).inc(f3q, 1)

    emit_quant8(0)
    for j in range(16):
        p = j % 2
        if j + 1 < 16:
            emit_quant8(j + 1)
        R.wait("tensor", f3q, (j + 1) * 4)
        if j == 0:
            R.wait("tensor", hid, 88)
        pn3 = ("b0", "b1") if j % 2 == 0 else ("b2", "b3")
        for tc_ in range(2):
            bn = pn3[tc_]
            BK[bn].acquire()
            for kc in range(44):
                part = kc // 11
                off = (kc % 11) * 128
                o = R.op("tensor", "matmul", BK[bn].t[:, 0:512],
                         w3q[j % 2][part][:, off:off + 128],
                         hidden_ap(kc, tc_ * 512, 512),
                         start=(kc == 0), stop=(kc == 43))
            o.inc(pe, 1)
            R.wait("vector", pe)
            ldx.wait("vector", p)
            if j >= 1:
                ost.wait("vector", tc_, j)
            R.op("vector", "tensor_tensor", out=OSTG[tc_][:],
                 in0=BK[bn].t[:, 0:512],
                 in1=x1b[p][:, tc_ * 512: tc_ * 512 + 512],
                 op=ALU.add).inc(oev, 1)
            BK[bn].release("vector")
            R.wait("scalar", oev)
            ost.dma("scalar", tc_,
                    out_c[j * 128:(j + 1) * 128, tc_ * 512:(tc_ + 1) * 512],
                    OSTG[tc_][:])
        w3_pe_after[j] = pe.v
    ost.wait_all("scalar")
    R.phases.append(("end", R.twc()))
    global _LAST_PHASES
    _LAST_PHASES = R.phases
    R.replay()
    return nc



_CACHE = {}
_LAST_PHASES = None


def _tile_wT(w, oc):
    t = w.reshape(oc, 128, 16, 128).transpose(3, 0, 2, 1)
    return np.ascontiguousarray(t.reshape(128, oc * 16 * 128))


def _tile_w3T(w3):
    t = w3.reshape(16, 128, 44, 128).transpose(3, 0, 2, 1)
    return np.ascontiguousarray(t.reshape(128, 16 * 44 * 128))


def kernel(L0_attn_x, L0_attn_slope_rate, L0_attn_W_qkvu, L0_attn_W_out,
           L0_ffn_w1, L0_ffn_w2, L0_ffn_w3):
    x = np.asarray(L0_attn_x, dtype=np.float32).reshape(8192, DM)
    slope = np.ascontiguousarray(
        np.asarray(L0_attn_slope_rate, dtype=np.float32).reshape(1, NH))
    Wq = np.asarray(L0_attn_W_qkvu, dtype=np.float32)
    Wo = np.asarray(L0_attn_W_out, dtype=np.float32)
    w1 = np.asarray(L0_ffn_w1, dtype=np.float32)
    w2 = np.asarray(L0_ffn_w2, dtype=np.float32)
    w3 = np.asarray(L0_ffn_w3, dtype=np.float32)

    xT = np.ascontiguousarray(x.T)
    wqkvu_t = _tile_wT(Wq, 64)
    wout_t = _tile_wT(Wo, 16)
    w1_t = _tile_wT(w1, 44)
    w2_t = _tile_wT(w2, 44)
    w3_t = _tile_w3T(w3)

    consts = make_consts()
    if "F" not in _CACHE:
        _CACHE["F"] = build_prog_fused(consts)
    ncF = _CACHE["F"]

    in_F = []
    for c in range(8):
        pre = np.zeros((DM, NPRE * NT), dtype=np.float32)
        lo = c * NT - NPRE * NT
        if c > 0:
            src_lo = max(0, lo)
            pre[:, src_lo - lo:] = xT[:, src_lo: c * NT]
        in_F.append({
            "xT": np.ascontiguousarray(xT[:, c * NT:(c + 1) * NT]),
            "xTpre": pre,
            "slope": slope,
            "wqkvu": wqkvu_t,
            "wout": wout_t,
            "w1t": w1_t,
            "w2t": w2_t,
            "w3t": w3_t,
        })
    res = run_bass_kernel_spmd(ncF, in_F, core_ids=list(range(8)))

    out = np.concatenate([res.results[c]["out_c"] for c in range(8)], axis=1)
    return np.ascontiguousarray(out.T).reshape(1, 8192, DM).astype(np.float32)


if __name__ == "__main__":
    ncF = build_prog_fused(make_consts())
    print("build ok; sems:", ncF and "n/a")


# version 12
# speedup vs baseline: 1.2395x; 1.2395x over previous
import numpy as np
import ml_dtypes
import concourse.bass as bass
import concourse.mybir as mybir
from concourse.bass_utils import run_bass_kernel_spmd

F32 = mybir.dt.float32
BF16 = mybir.dt.bfloat16
AF = mybir.ActivationFunctionType
ALU = mybir.AluOpType
BF = ml_dtypes.bfloat16
AX = mybir.AxisListType.X

NT = 1024
DM = 2048
NH = 16
HD = 5632
FC = DM // 128
TC = NT // 128
NBLK = 4
HC = HD // 128
NW_ELEMS = float(HD * DM)
GCH = 528

ENGINES = ("sync", "scalar", "vector", "tensor", "gpsimd")


class Sem:
    __slots__ = ("h", "v", "name")

    def __init__(self, h, name):
        self.h, self.v, self.name = h, 0, name


class Op:
    __slots__ = ("meth", "args", "kwargs", "incs")

    def __init__(self, meth, args, kwargs):
        self.meth, self.args, self.kwargs, self.incs = meth, args, kwargs, []

    def inc(self, sem, n):
        assert not self.incs, f"op {self.meth} already has a sync update"
        self.incs.append((sem, n, sem.v + n))
        sem.v += n
        return self

    def tok(self):
        if self.incs:
            s, n, va = self.incs[0]
            return (s, va)
        return None


class Rec:
    def __init__(self, nc):
        self.nc = nc
        self.q = {e: [] for e in ENGINES}
        self._n = 0
        self._lastwait = {}

    def sem(self, name=None):
        self._n += 1
        nm = name or f"s{self._n}"
        return Sem(self.nc.alloc_semaphore(nm), nm)

    def op(self, eng, meth, *args, **kwargs):
        o = Op(meth, args, kwargs)
        self.q[eng].append(o)
        return o

    def wait(self, eng, sem, val=None):
        v = sem.v if val is None else val
        if v <= 0:
            return
        key = (eng, sem.name)
        if self._lastwait.get(key, -1) >= v:
            return
        self._lastwait[key] = v
        self.q[eng].append(Op("wait_ge", (sem.h, v), {}))

    def dma(self, eng, out, in_, sem, **kw):
        return self.op(eng, "dma_start", out=out, in_=in_, **kw).inc(sem, 16)

    @staticmethod
    def _interval(ap):
        ext = 1
        for (step, count) in ap.ap:
            ext += abs(int(step)) * (int(count) - 1)
        off = int(ap.offset)
        return (ap.tensor.name, off, off + ext)

    def _rw_of(self, o):
        if o.meth in ("wait_ge", "nop"):
            return [], []
        writes, reads = [], []
        args = list(o.args)
        kw = dict(o.kwargs)
        if o.meth == "memset":
            writes.append(self._interval(args[0]))
            return writes, reads
        if "out" in kw:
            writes.append(self._interval(kw.pop("out")))
        for v in list(args) + list(kw.values()):
            if isinstance(v, bass.AP):
                reads.append(self._interval(v))
        return writes, reads

    def _auto_chain(self):
        def overlap(a, b):
            return a[0] == b[0] and a[1] < b[2] and b[1] < a[2]

        for eng in ("vector", "scalar", "gpsimd"):
            ops = self.q[eng]
            chain = self.sem(f"chain_{eng}")
            writers = []
            readers = []
            n = len(ops)
            deps = [None] * n
            is_dma = [o.meth == "dma_start" for o in ops]
            for i, o in enumerate(ops):
                ws, rs = self._rw_of(o)
                need = set()
                for r in rs:
                    for (w, j) in writers:
                        if (not is_dma[j]) and overlap(r, w):
                            need.add(j)
                for w in ws:
                    for (pw, j) in writers:
                        if (not is_dma[j]) and overlap(w, pw):
                            need.add(j)
                    for (pr, j) in readers:
                        if (not is_dma[j]) and overlap(w, pr):
                            need.add(j)
                deps[i] = need
                for w in ws:
                    writers.append((w, i))
                for r in rs:
                    readers.append((r, i))
            producers = set()
            for need in deps:
                producers |= need
            for j in sorted(producers):
                if not ops[j].incs:
                    ops[j].inc(chain, 1)
            new_ops = []
            lastw = {}
            for i, o in enumerate(ops):
                if deps[i]:
                    for j in sorted(deps[i]):
                        t = ops[j].tok()
                        s, v = t
                        if lastw.get(s.name, -1) < v:
                            new_ops.append(Op("wait_ge", (s.h, v), {}))
                            lastw[s.name] = v
                new_ops.append(o)
            self.q[eng] = new_ops

    def replay(self):
        self._auto_chain()
        with self.nc.Block() as block:
            for eng in ENGINES:
                ops = self.q[eng]

                def runner(engine, ops=ops):
                    for o in ops:
                        r = getattr(engine, o.meth)(*o.args, **o.kwargs)
                        if o.incs:
                            s, nn_, _ = o.incs[0]
                            r.then_inc(s.h, nn_)

                getattr(block, eng)(runner)


class Bank:
    def __init__(self, R, t, name):
        self.R, self.t, self.name = R, t, name
        self.sem = R.sem(f"bk_{name}")
        self.tok = None

    def acquire(self, eng="tensor"):
        if self.tok is not None:
            self.R.wait(eng, self.tok[0], self.tok[1])

    def release(self, eng):
        o = self.R.q[eng][-1]
        if not o.incs:
            o.inc(self.sem, 1)
        self.tok = o.tok()


class Stream:
    def __init__(self, R, name, nslots):
        self.R = R
        self.sems = [R.sem(f"{name}_{i}") for i in range(nslots)]
        self.uses = [0] * nslots

    def dma(self, eng, slot, out, in_, **kw):
        self.uses[slot] += 1
        return self.R.op(eng, "dma_start", out=out, in_=in_, **kw)\
            .inc(self.sems[slot], 16)

    def wait(self, eng, slot, use=None):
        u = self.uses[slot] if use is None else use
        self.R.wait(eng, self.sems[slot], 16 * u)

    def wait_all(self, eng):
        for i, s in enumerate(self.sems):
            self.R.wait(eng, s, 16 * self.uses[i])


BF16_CONSTS = {"IDX0", "IDX1", "QDROW", "IDEN_BF", "ONESC_BF", "KD2"}


def make_consts():
    c = {}
    idx = []
    for k in range(2):
        s = 128 * k + np.arange(128)[:, None]
        m = np.arange(256)[None, :]
        d = (m - s).astype(np.float32)
        d[m < s] = 1e9
        idx.append(np.ascontiguousarray(d))
    c["IDX0"], c["IDX1"] = idx
    c["QDROW"] = np.broadcast_to(
        (np.arange(256, dtype=np.float32) + 1.0)[None, :], (128, 256)).copy()
    c["IDEN_BF"] = np.eye(128, dtype=np.float32)
    c["ONESC_BF"] = np.ones((128, 1), dtype=np.float32)
    kd = np.zeros((128, 2), dtype=np.float32)
    kd[:, 0] = 255.0 - np.arange(128)
    kd[:, 1] = 255.0 - (128 + np.arange(128))
    c["KD2"] = kd
    kp = np.zeros((128, 8), dtype=np.float32)
    for tb in range(8):
        kp[:, tb] = 1023.0 - (tb * 128 + np.arange(128))
    c["KDPRE"] = kp
    c["ONESR_F"] = np.ones((1, 128), dtype=np.float32)
    c["ONESC_F"] = np.ones((128, 1), dtype=np.float32)
    cc = np.zeros((128, 8), dtype=np.float32)
    cc[:, 0] = 1e-5
    cc[:, 1] = 1e-6
    cc[:, 2] = 1e-8
    cc[:, 3] = 1.0 / 2048.0
    cc[:, 4] = 256.0
    cc[:, 5] = 1024.0
    c["CCOL"] = cc
    c["CROW"] = np.ascontiguousarray(cc[0:1, :])
    return c


def load_consts(nc, R, names, consts, sem):
    t = {}
    buf = {}
    for k in names:
        arr = consts[k]
        if k in BF16_CONSTS:
            arr = arr.astype(BF)
            buf[k] = nc.alloc_sbuf_tensor(f"cb_{k}", list(arr.shape), BF16)
        else:
            buf[k] = nc.alloc_sbuf_tensor(f"cb_{k}", list(arr.shape), F32)
        t[k] = nc.inline_tensor(arr, name=f"c_{k}")
        R.dma("sync", buf[k][:], t[k].ap(), sem)
    return buf


def emit_layernorm(R, S, BK, src, dst, eps_row, src_sem, tag):
    pe, dve, act = R.s["pe"], R.s["dve"], R.s["act"]
    xsq = [S["scrB"][:, 0:1024], S["scrB"][:, 1024:2048]]
    ssum = [BK["b2"].t[0:1, 0:512], BK["b3"].t[0:1, 0:512]]
    ssq = [BK["b4"].t[0:1, 0:512], BK["b5"].t[0:1, 0:512]]
    for b in ("b2", "b3", "b4", "b5"):
        BK[b].acquire()
    R.wait("scalar", src_sem)
    R.wait("tensor", src_sem)
    sq_done = R.sem(f"lnsq{tag}")
    for fc in range(FC):
        srcc = src[:, fc * NT:(fc + 1) * NT]
        p = fc % 2
        if fc >= 2:
            R.wait("scalar", pe)
        R.op("scalar", "activation", out=xsq[p][:], in_=srcc, func=AF.Square)\
            .inc(sq_done, 1)
        R.wait("tensor", sq_done)
        for half in range(2):
            rsrc = src[:, fc * NT + half * 512: fc * NT + half * 512 + 512]
            R.op("tensor", "matmul", ssum[half], S["ONESC_BF"][:], rsrc,
                 start=(fc == 0), stop=(fc == FC - 1))
            o = R.op("tensor", "matmul", ssq[half], S["ONESC_BF"][:],
                     xsq[p][:, half * 512: half * 512 + 512],
                     start=(fc == 0), stop=(fc == FC - 1))
        o.inc(pe, 1)
    mean, ex2, tmpr = S["rowA"], S["rowB"], S["rowA2"]
    rstd = ex2
    R.wait("vector", pe)
    for half in range(2):
        sl = slice(half * 512, half * 512 + 512)
        R.op("vector", "tensor_scalar", out=mean[0:1, sl], in0=ssum[half],
             scalar1=1.0 / DM, scalar2=None, op0=ALU.mult)
        R.op("vector", "tensor_scalar", out=ex2[0:1, sl], in0=ssq[half],
             scalar1=1.0 / DM, scalar2=None, op0=ALU.mult)
    o = R.op("vector", "tensor_mul", out=tmpr[0:1, 0:1024],
             in0=mean[0:1, 0:1024], in1=mean[0:1, 0:1024])
    o = R.op("vector", "tensor_sub", out=ex2[0:1, 0:1024],
             in0=ex2[0:1, 0:1024], in1=tmpr[0:1, 0:1024])
    o.inc(dve, 1)
    for b in ("b2", "b3", "b4", "b5"):
        BK[b].release("vector")
    R.wait("scalar", dve)
    R.op("scalar", "activation", out=tmpr[0:1, 0:1024], in_=ex2[0:1, 0:1024],
         func=AF.Sqrt, bias=eps_row).inc(act, 1)
    R.wait("vector", act)
    R.op("vector", "reciprocal", out=rstd[0:1, 0:1024],
         in_=tmpr[0:1, 0:1024]).inc(dve, 1)
    for b in ("b2", "b3", "b4", "b5"):
        BK[b].acquire()
    R.wait("tensor", dve)
    bmap = (("b2", 0, True), ("b4", 1, True), ("b3", 0, False), ("b5", 1, False))
    for (bn, half, is_mean) in bmap:
        sl = slice(half * 512, half * 512 + 512)
        o = R.op("tensor", "matmul", BK[bn].t[:, 0:512], S["ONESR_F"][:],
                 (mean if is_mean else rstd)[0:1, sl], start=True, stop=True)
    o.inc(pe, 1)
    meanb = S["scrB"][:, 0:1024]
    rstdb = S["scrB"][:, 1024:2048]
    R.wait("vector", pe)
    for (bn, half, is_mean) in bmap:
        dstb = (meanb if is_mean else rstdb)
        o = R.op("vector", "tensor_copy",
                 out=dstb[:, half * 512: half * 512 + 512],
                 in_=BK[bn].t[:, 0:512])
    o.inc(dve, 1)
    for b in ("b2", "b3", "b4", "b5"):
        BK[b].release("vector")
    ttmp = S["scrB"][:, 2048:3072]
    done = R.sem(f"lnd{tag}")
    for fc in range(FC):
        srcc = src[:, fc * NT:(fc + 1) * NT]
        dstc = dst[:, fc * NT:(fc + 1) * NT]
        R.op("vector", "tensor_sub", out=ttmp[:], in0=srcc, in1=meanb)
        R.op("vector", "tensor_mul", out=dstc, in0=ttmp[:], in1=rstdb)\
            .inc(done, 1)
    return done


def common_setup(nc):
    R = Rec(nc)
    R.s = {k: R.sem(k) for k in ("pe", "dve", "act", "st")}
    BK = {}
    for i in range(6):
        BK[f"b{i}"] = Bank(R, nc.alloc_psum_tensor(f"pb{i}", [128, 512], F32), f"b{i}")
    BK["b6"] = Bank(R, nc.alloc_psum_tensor("pb6", [128, 512], F32), "b6")
    BK["b7"] = Bank(R, nc.alloc_psum_tensor("pb7", [128, 512], F32), "b7")
    return R, BK


def emit_slope_derived(R, S, BK, ld_const):
    pe, dve = R.s["pe"], R.s["dve"]
    sder = R.sem("sder")
    BK["b0"].acquire()
    R.wait("tensor", ld_const)
    R.op("tensor", "matmul", BK["b0"].t[:, 0:16], S["ONESR_F"][:],
         S["slope"][:], start=True, stop=True).inc(pe, 1)
    R.wait("vector", pe)
    R.op("vector", "tensor_scalar", out=S["sneg"][:], in0=BK["b0"].t[:, 0:16],
         scalar1=-1.0, scalar2=None, op0=ALU.mult).inc(dve, 1)
    BK["b0"].release("vector")
    R.wait("scalar", dve)
    R.wait("scalar", ld_const)
    R.op("scalar", "activation", out=S["bd_bc"][:], in_=S["sneg"][:],
         func=AF.Exp, scale=S["CCOL"][:, 4:5]).inc(sder, 1)
    for k in range(8):
        R.op("scalar", "activation", out=S["kvw"][:, k * NH:(k + 1) * NH],
             in_=S["sneg"][:], func=AF.Exp, scale=S["DIST"][:, k:k + 1])\
            .inc(sder, 1)
    for h in range(NH):
        R.op("scalar", "activation", out=S["kdb_all"][:, 2 * h:2 * h + 2],
             in_=S["KD2"][:], func=AF.Exp, scale=S["sneg"][:, h:h + 1])\
            .inc(sder, 1)
    for h in range(NH):
        R.op("scalar", "activation", out=S["kdpre_all"][:, 8 * h:8 * h + 8],
             in_=S["KDPRE"][:], func=AF.Exp, scale=S["sneg"][:, h:h + 1])\
            .inc(sder, 1)
    return sder


def build_prog_fused(consts):
    nc = bass.Bass("TRN2", target_bir_lowering=False, debug=False)
    xT = nc.dram_tensor("xT", [DM, NT], F32, kind="ExternalInput").ap()
    distc = nc.dram_tensor("distc", [128, 8], F32, kind="ExternalInput").ap()
    slope = nc.dram_tensor("slope", [1, NH], F32, kind="ExternalInput").ap()
    wqkvu = nc.dram_tensor("wqkvu", [128, 64 * 2048], F32, kind="ExternalInput").ap()
    wout = nc.dram_tensor("wout", [128, 16 * 2048], F32, kind="ExternalInput").ap()
    w1t = nc.dram_tensor("w1t", [128, 44 * 2048], F32, kind="ExternalInput").ap()
    w2t = nc.dram_tensor("w2t", [128, 44 * 2048], F32, kind="ExternalInput").ap()
    w3t = nc.dram_tensor("w3t", [128, 16 * 5632], F32, kind="ExternalInput").ap()
    out_c = nc.dram_tensor("out_c", [DM, NT], F32, kind="ExternalOutput").ap()
    u_spill = nc.dram_tensor("u_spill", [128, FC * NT], BF16, kind="Internal").ap()
    x1_spill = nc.dram_tensor("x1_spill", [128, FC * NT], BF16, kind="Internal").ap()
    l_out_t = nc.dram_tensor("l_out", [128, NH * 128], BF16, kind="Internal")
    l_all_t = nc.dram_tensor("l_all", [8 * 128, NH * 128], BF16, kind="Internal",
                             addr_space="Shared")

    R, BK = common_setup(nc)
    R.phases = []

    def _twc():
        return sum(2 if o.meth == "matmul" else 1 for o in R.q["tensor"])
    R.twc = _twc
    s = R.s
    pe, dve, act = s["pe"], s["dve"], s["act"]
    A = nc.alloc_sbuf_tensor
    AR1 = A("ar1", [128, FC * NT], BF16)
    AR2 = A("ar2", [128, FC * NT], BF16)
    AR3 = A("ar3", [128, FC * NT], BF16)
    AR4 = A("ar4", [128, FC * NT], BF16)
    SCRB = A("scrb", [128, 6144], BF16)
    SCRF = A("scrf", [128, 5120], F32)
    WGB = [A(f"wgb{i}", [128, 1024], BF16) for i in range(4)]
    WQ = A("wq", [128, 8192], BF16)
    KVOUT = A("kvout", [128, NH * 128], F32)
    S = {
        "slope": A("s_slope", [1, NH], F32),
        "DIST": A("s_dist", [128, 8], F32),
        "kvw": A("s_kvw", [128, 8 * NH], BF16),
        "sneg": A("s_sneg", [128, NH], F32),
        "bd_bc": A("s_bdbc", [128, NH], BF16),
        "kdb_all": A("s_kdball", [128, 2 * NH], F32),
        "kdpre_all": A("s_kdpall", [128, 8 * NH], F32),
        "rowA": A("s_rowA", [1, 1024], F32),
        "rowA2": A("s_rowA2", [1, 1024], F32),
        "rowB": A("s_rowB", [1, 1024], F32),
        "gacc": A("s_gacc", [128, 4], F32),
        "gtmp": A("s_gtmp", [128, 4], F32),
        "gsum": A("s_gsum", [1, 4], F32),
        "th": A("s_th", [1, 4], F32),
        "nth": A("s_nth", [1, 4], F32),
        "thb": A("s_thb", [128, 4], F32),
        "nthb": A("s_nthb", [128, 4], F32),
        "kvbf": SCRB[:, 2048:2176],
        "scrB": SCRB,
    }
    ld_const = R.sem("ldconst")
    CB = load_consts(nc, R,
                     ("IDX0", "IDX1", "QDROW", "KD2", "KDPRE", "ONESR_F",
                      "ONESC_F", "IDEN_BF", "ONESC_BF", "CCOL", "CROW"),
                     consts, ld_const)
    S.update(CB)
    R.dma("sync", S["slope"][:], slope, ld_const)
    R.dma("sync", S["DIST"][:], distc, ld_const)
    R.wait("vector", ld_const)
    R.wait("tensor", ld_const)
    sder = emit_slope_derived(R, S, BK, ld_const)

    gb = [SCRF[:, 4096:4608], SCRF[:, 4608:5120]]
    gld = Stream(R, "gld", 2)
    gred = R.sem("gred")
    gbar = R.sem("gbar")
    gsrc = []
    for wi, wt in enumerate((w1t, w2t, w3t)):
        for kk in range(176):
            gsrc.append((wi, wt[:, kk * 512:(kk + 1) * 512]))
    gstate = {"dma": 0, "done": 0, "init": False, "tgt": 0.0}
    GT_LKV, GT_OWN_PROJ, GT_OWN_ATTN = 4.0, 6.0, 5.0

    def emit_gamma_dmas(upto):
        upto = min(upto, gstate["done"] + 2, GCH)
        while gstate["dma"] < upto:
            nch = gstate["dma"]
            if nch >= 2:
                R.wait("gpsimd", gred, nch - 1)
            gld.dma("gpsimd", nch % 2, gb[nch % 2][:], gsrc[nch][1])
            gstate["dma"] += 1

    def emit_gamma_chunks(upto):
        if not gstate["init"]:
            R.op("vector", "memset", S["gacc"][:], 0.0)
            gstate["init"] = True
        while gstate["done"] < min(upto, GCH):
            nch = gstate["done"]
            emit_gamma_dmas(nch + 2)
            wi = gsrc[nch][0]
            gld.wait("vector", nch % 2, nch // 2 + 1)
            R.op("vector", "tensor_reduce", out=S["gtmp"][:, wi:wi + 1],
                 in_=gb[nch % 2][:], axis=AX, op=ALU.add,
                 apply_absolute_value=True).inc(gbar, 1)
            R.wait("vector", gbar, nch + 1)
            R.op("vector", "tensor_tensor", out=S["gacc"][:, wi:wi + 1],
                 in0=S["gacc"][:, wi:wi + 1], in1=S["gtmp"][:, wi:wi + 1],
                 op=ALU.add).inc(gred, 1)
            gstate["done"] += 1
        emit_gamma_dmas(gstate["done"] + 2)

    def gtick(rate):
        gstate["tgt"] += rate
        emit_gamma_chunks(min(int(gstate["tgt"]), GCH))

    ldx = Stream(R, "ldx", 2)
    xc = [SCRF[:, 0:1024], SCRF[:, 1024:2048]]
    cast_done = R.sem("castdone")

    def emit_x_chunk(src_cols, guard_pe):
        base = cast_done.v
        for fc in range(FC):
            p = fc % 2
            if fc >= 2:
                R.wait("sync", cast_done, base + fc - 1)
            elif base > 0:
                R.wait("sync", cast_done, base)
            ldx.dma("sync", p, xc[p][:], src_cols(fc))
            ldx.wait("vector", p)
            if guard_pe and fc < 2:
                R.wait("vector", pe, guard_pe)
            R.op("vector", "tensor_copy", out=AR1[:, fc * NT:(fc + 1) * NT],
                 in_=xc[p][:]).inc(cast_done, 1)

    wld = Stream(R, "wld", 2)
    wcast = R.sem("wcast")
    WG = [SCRF[:, 2048:3072], SCRF[:, 3072:4096]]
    sevq = R.sem("sevq")
    vev = R.sem("vev")
    trev = R.sem("trev")
    sevu = R.sem("sevu")
    uev = Stream(R, "uev", 2)
    vstg = [SCRB[:, 3072 + i * 512: 3584 + i * 512] for i in range(2)]
    ustg = [SCRB[:, 4096 + i * 512: 4608 + i * 512] for i in range(2)]
    PRM = {"g": 0, "vtr_pe": {}, "un": 0, "stg4": {}}

    def emit_proj_phase(granules, wsrc, src, ln_done, prev_guard_pe,
                        grate=0.0):
        ng = len(granules)
        gp_after = {}
        wcast_after = {}

        def wload(gq):
            oc = granules[gq][0]
            for half in range(2):
                gg = PRM["g"]
                PRM["g"] += 1
                bgi = gg % 4
                if gg >= 2:
                    R.wait("sync", wcast, gg - 1)
                wld.dma("sync", gg % 2, WG[gg % 2][:],
                        wsrc[:, oc * 2048 + half * 1024: oc * 2048 + (half + 1) * 1024])
                wld.wait("vector", gg % 2)
                if gq >= 2:
                    R.wait("vector", pe, gp_after[gq - 2])
                elif prev_guard_pe:
                    R.wait("vector", pe, prev_guard_pe)
                R.op("vector", "tensor_copy", out=WGB[bgi][:],
                     in_=WG[gg % 2][:]).inc(wcast, 1)
            wcast_after[gq] = wcast.v

        wload(0)
        for j in range(ng):
            if j + 1 < ng:
                wload(j + 1)
            R.wait("tensor", wcast, wcast_after[j])
            if j == 0 and ln_done is not None:
                R.wait("tensor", ln_done, FC)
            _, kind, jj, dst = granules[j]
            b_lo, b_hi = WGB[(2 * j) % 4], WGB[(2 * j + 1) % 4]
            pnames = ("b0", "b1") if j % 2 == 0 else ("b2", "b3")
            for tc_ in range(2):
                bn = pnames[tc_]
                BK[bn].acquire()
                for kc in range(16):
                    wb = b_lo if kc < 8 else b_hi
                    off = (kc % 8) * 128
                    o = R.op("tensor", "matmul", BK[bn].t[:, 0:512],
                             wb[:, off:off + 128],
                             src[:, kc * NT + tc_ * 512: kc * NT + tc_ * 512 + 512],
                             start=(kc == 0), stop=(kc == 15))
                o.inc(pe, 1)
                pe_now = pe.v
                if kind in ("q", "k"):
                    dslice = dst[:, jj * NT + tc_ * 512: jj * NT + tc_ * 512 + 512]
                    R.wait("scalar", pe, pe_now)
                    R.op("scalar", "activation", out=dslice,
                         in_=BK[bn].t[:, 0:512], func=AF.Silu).inc(sevq, 1)
                    BK[bn].release("scalar")
                elif kind == "v":
                    R.wait("vector", pe, pe_now)
                    key = tc_
                    if key in PRM["vtr_pe"]:
                        R.wait("vector", pe, PRM["vtr_pe"][key])
                    R.op("vector", "tensor_copy", out=vstg[tc_][:],
                         in_=BK[bn].t[:, 0:512]).inc(vev, 1)
                    BK[bn].release("vector")
                    BK["b6"].acquire()
                    R.wait("tensor", vev)
                    for i4 in range(4):
                        o = R.op("tensor", "matmul",
                                 BK["b6"].t[:, i4 * 128:(i4 + 1) * 128],
                                 vstg[tc_][:, i4 * 128:(i4 + 1) * 128],
                                 S["IDEN_BF"][:], start=True, stop=True)
                    o.inc(pe, 1)
                    PRM["vtr_pe"][key] = pe.v
                    R.wait("vector", pe, pe.v)
                    for i4 in range(4):
                        tch = tc_ * 4 + i4
                        R.op("vector", "tensor_copy",
                             out=dst[:, tch * DM + jj * 128: tch * DM + (jj + 1) * 128],
                             in_=BK["b6"].t[:, i4 * 128:(i4 + 1) * 128]).inc(trev, 1)
                    BK["b6"].release("vector")
                else:
                    un = PRM["un"]
                    PRM["un"] += 1
                    R.wait("vector", pe, pe_now)
                    if un >= 2:
                        uev.wait("vector", un % 2, un // 2)
                    R.op("vector", "tensor_copy", out=ustg[un % 2][:],
                         in_=BK[bn].t[:, 0:512]).inc(sevu, 1)
                    BK[bn].release("vector")
                    R.wait("scalar", sevu)
                    uev.dma("scalar", un % 2,
                            u_spill[:, jj * NT + tc_ * 512: jj * NT + tc_ * 512 + 512],
                            ustg[un % 2][:])
            gp_after[j] = pe.v
            if grate:
                gtick(grate)
        return gp_after, pe.v

    sktok = R.sem("sktok")
    skv = R.sem("skv")

    def emit_local_kv(k_evac_total, v_evac_total):
        R.wait("tensor", sevq, k_evac_total)
        R.wait("tensor", trev, v_evac_total)
        R.wait("vector", sder)
        uev.wait_all("vector")
        stg4 = [SCRB[:, 4096 + i * 512: 4608 + i * 512] for i in range(4)]
        trb = ("b6", "b0")
        C_tok = {}
        sk_after = {}

        def emit_tr(h):
            for half in range(2):
                slot = (h % 2) * 2 + half
                BK[trb[half]].acquire()
                for i4 in range(4):
                    tb = half * 4 + i4
                    o = R.op("tensor", "matmul",
                             BK[trb[half]].t[:, i4 * 128:(i4 + 1) * 128],
                             AR3[:, h * NT + tb * 128: h * NT + (tb + 1) * 128],
                             S["IDEN_BF"][:], start=True, stop=True)
                o.inc(pe, 1)
                R.wait("vector", pe, pe.v)
                g = PRM["stg4"].get(slot)
                if g:
                    R.wait("vector", pe, g)
                for i4 in range(4):
                    tb = half * 4 + i4
                    o = R.op("vector", "tensor_scalar",
                             out=stg4[slot][:, i4 * 128:(i4 + 1) * 128],
                             in0=BK[trb[half]].t[:, i4 * 128:(i4 + 1) * 128],
                             scalar1=S["kdpre_all"][:, h * 8 + tb: h * 8 + tb + 1],
                             scalar2=None, op0=ALU.mult)
                o.inc(sktok, 1)
                BK[trb[half]].release("vector")
            sk_after[h] = sktok.v

        emit_tr(0)
        for h in range(NH):
            if h + 1 < NH:
                emit_tr(h + 1)
            cb = BK["b7"] if h % 2 == 0 else BK["b1"]
            cb.acquire()
            R.wait("tensor", sktok, sk_after[h])
            for tb in range(8):
                slot = (h % 2) * 2 + tb // 4
                o = R.op("tensor", "matmul", cb.t[:, 0:128],
                         stg4[slot][:, (tb % 4) * 128:(tb % 4) * 128 + 128],
                         AR4[:, tb * DM + h * 128: tb * DM + (h + 1) * 128],
                         start=(tb == 0), stop=(tb == 7))
            o.inc(pe, 1)
            PRM["stg4"][(h % 2) * 2] = PRM["stg4"][(h % 2) * 2 + 1] = pe.v
            C_tok[h] = pe.v
            R.wait("vector", pe, C_tok[h])
            kvslice = KVOUT[:, h * 128:(h + 1) * 128]
            R.op("vector", "tensor_copy", out=kvslice,
                 in_=cb.t[:, 0:128]).inc(skv, 1)
            cb.release("vector")
            gtick(GT_LKV)
        return pe.v

    emit_x_chunk(lambda fc: xT[fc * 128:(fc + 1) * 128, :], None)
    R.phases.append(("own_ln", R.twc()))
    ln1 = emit_layernorm(R, S, BK, AR1, AR1, S["CROW"][0:1, 0:1],
                         cast_done, "1")
    gran = [(j, "q", j, AR2) for j in range(16)] + \
           [(16 + j, "k", j, AR3) for j in range(16)] + \
           [(48 + j, "u", j, None) for j in range(16)] + \
           [(32 + j, "v", j, AR4) for j in range(16)]
    R.phases.append(("own_proj", R.twc()))
    gp_own, own_tail = emit_proj_phase(gran, wqkvu, AR1, ln1, None,
                                       GT_OWN_PROJ)
    q_total, k_total, v_total = sevq.v, sevq.v, trev.v

    R.phases.append(("lkv", R.twc()))
    lkv_pe = emit_local_kv(sevq.v, trev.v)

    R.phases.append(("cc", R.twc()))
    lsp = R.sem("lsp")
    lod = R.sem("lod")
    ccs = R.sem("ccs")
    lin = R.sem("lin")
    cmb = R.sem("cmb")
    LBF = SCRB[:, 0:2048]
    LRK = [SCRB[:, 2048:4096], SCRB[:, 4096:6144]]
    R.wait("vector", skv)
    R.op("vector", "tensor_copy", out=LBF, in_=KVOUT[:]).inc(lsp, 1)
    R.op("vector", "memset", KVOUT[:], 0.0)
    R.wait("gpsimd", lsp, 1)
    R.dma("gpsimd", l_out_t.ap(), LBF, lod)
    R.wait("gpsimd", lod, 16)
    R.op("gpsimd", "collective_compute", "AllGather", ALU.bypass,
         replica_groups=[list(range(8))],
         ins=[l_out_t.ap().opt()], outs=[l_all_t.ap().opt()]).inc(ccs, 1)
    R.wait("gpsimd", ccs, 1)
    R.wait("gpsimd", pe, lkv_pe)
    l_all = l_all_t.ap()
    for r in range(8):
        if r >= 2:
            R.wait("gpsimd", cmb, r - 1)
        R.dma("gpsimd", LRK[r % 2][:], l_all[r * 128:(r + 1) * 128, :], lin)
    R.wait("vector", sder)
    for r in range(8):
        R.wait("vector", lin, 16 * (r + 1))
        for h in range(NH):
            o = R.op("vector", "scalar_tensor_tensor",
                     out=KVOUT[:, h * 128:(h + 1) * 128],
                     in0=LRK[r % 2][:, h * 128:(h + 1) * 128],
                     scalar=S["kvw"][:, r * NH + h: r * NH + h + 1],
                     in1=KVOUT[:, h * 128:(h + 1) * 128],
                     op0=ALU.mult, op1=ALU.add)
        o.inc(cmb, 1)

    qkmb = [SCRB[:, 3072 + i * 512: 3584 + i * 512] for i in range(2)]
    R.phases.append(("own_attn", R.twc()))
    maskb = [SCRB[:, 4096 + i * 512: 4608 + i * 512] for i in range(2)]
    qdb = [SCRB[:, 5120 + i * 256: 5376 + i * 256] for i in range(2)]
    ktokb = [SCRB[:, 5632 + i * 256: 5888 + i * 256] for i in range(2)]
    smask = R.sem("smask")
    sqs = R.sem("sqs")
    skvbf = R.sem("skvbf")
    sqkm = R.sem("sqkm")
    soev = R.sem("soev")
    uload = R.sem("uload")
    sq2 = R.sem("sq2")
    osq = [SCRB[:, 0:1024], SCRB[:, 1024:2048]]
    stats_pe = {}
    BK["b4"].acquire()
    BK["b5"].acquire()
    R.wait("vector", pe, own_tail)
    R.wait("vector", sder)
    R.wait("tensor", sevq, q_total)
    R.wait("tensor", trev, v_total)
    R.wait("vector", sevq, q_total)
    uev.wait_all("scalar")
    R.wait("scalar", cmb, 8)
    C_pe_after = {}
    diag_pe_after = {}
    for h in range(NH):
        hp = h % 2
        R.wait("scalar", sder)
        if h >= 2:
            R.wait("scalar", sqkm, 4 * (h - 1))
            R.wait("scalar", sqs, h - 1)
        R.op("scalar", "activation", out=maskb[hp][:, 0:256], in_=S["IDX0"][:],
             func=AF.Exp, scale=S["sneg"][:, h:h + 1])
        R.op("scalar", "activation", out=maskb[hp][:, 256:512], in_=S["IDX1"][:],
             func=AF.Exp, scale=S["sneg"][:, h:h + 1])
        R.op("scalar", "activation", out=qdb[hp][:], in_=S["QDROW"][:],
             func=AF.Exp, scale=S["sneg"][:, h:h + 1]).inc(smask, 1)
        R.wait("vector", smask, h + 1)
        qd = qdb[hp]
        qrep = bass.AP(qd.tensor, qd.offset, [[qd.ap[0][0], 128], [0, 4], [1, 256]])
        R.op("vector", "tensor_tensor",
             out=AR1[:, h * NT:(h + 1) * NT].rearrange("p (b m) -> p b m", b=4),
             in0=AR2[:, h * NT:(h + 1) * NT].rearrange("p (b m) -> p b m", b=4),
             in1=qrep, op=ALU.mult).inc(sqs, 1)
        kvslice = KVOUT[:, h * 128:(h + 1) * 128]
        R.op("vector", "tensor_copy", out=S["kvbf"][:], in_=kvslice)\
            .inc(skvbf, 1)
        for i in range(NBLK):
            blk = h * NBLK + i
            BK["b6"].acquire()
            o = R.op("tensor", "matmul", BK["b6"].t[:, 0:128],
                     AR3[:, h * NT + i * 256: h * NT + i * 256 + 128],
                     S["IDEN_BF"][:], start=True, stop=True)
            o = R.op("tensor", "matmul", BK["b6"].t[:, 128:256],
                     AR3[:, h * NT + i * 256 + 128: h * NT + i * 256 + 256],
                     S["IDEN_BF"][:], start=True, stop=True)
            o.inc(pe, 1)
            tr_pe = pe.v
            R.wait("vector", pe, tr_pe)
            prevu = blk - 1 if i > 0 else ((h - 2) * NBLK + 3 if h >= 2 else None)
            if prevu is not None and prevu in C_pe_after:
                R.wait("vector", pe, C_pe_after[prevu])
            R.op("vector", "tensor_scalar", out=ktokb[hp][:, 0:128],
                 in0=BK["b6"].t[:, 0:128],
                 scalar1=S["kdb_all"][:, 2 * h:2 * h + 1],
                 scalar2=None, op0=ALU.mult)
            R.op("vector", "tensor_scalar", out=ktokb[hp][:, 128:256],
                 in0=BK["b6"].t[:, 128:256],
                 scalar1=S["kdb_all"][:, 2 * h + 1:2 * h + 2],
                 scalar2=None, op0=ALU.mult).inc(sktok, 1)
            BK["b6"].release("vector")
            BK["b7"].acquire()
            R.wait("tensor", sktok)
            tch = 2 * i
            R.op("tensor", "matmul", BK["b7"].t[:, 0:128],
                 ktokb[hp][:, 0:128],
                 AR4[:, tch * DM + h * 128: tch * DM + (h + 1) * 128],
                 start=True, stop=False)
            R.op("tensor", "matmul", BK["b7"].t[:, 0:128],
                 ktokb[hp][:, 128:256],
                 AR4[:, (tch + 1) * DM + h * 128: (tch + 1) * DM + (h + 1) * 128],
                 start=False, stop=True).inc(pe, 1)
            C_pe_after[blk] = pe.v
            qkbn = "b0" if i % 2 == 0 else "b3"
            BK[qkbn].acquire()
            R.op("tensor", "matmul", BK[qkbn].t[:, 0:256],
                 AR3[:, h * NT + i * 256: h * NT + i * 256 + 128],
                 AR2[:, h * NT + i * 256: h * NT + (i + 1) * 256],
                 start=True, stop=True)
            R.op("tensor", "matmul", BK[qkbn].t[:, 256:512],
                 AR3[:, h * NT + i * 256 + 128: h * NT + i * 256 + 256],
                 AR2[:, h * NT + i * 256: h * NT + (i + 1) * 256],
                 start=True, stop=True).inc(pe, 1)
            qk_pe = pe.v
            R.wait("vector", pe, qk_pe)
            prevq = blk - 1 if i > 0 else ((h - 2) * NBLK + 3 if h >= 2 else None)
            if prevq is not None and prevq in diag_pe_after:
                R.wait("vector", pe, diag_pe_after[prevq])
            R.op("vector", "tensor_tensor", out=qkmb[hp][:, 0:256],
                 in0=BK[qkbn].t[:, 0:256], in1=maskb[hp][:, 0:256], op=ALU.mult)
            R.op("vector", "tensor_tensor", out=qkmb[hp][:, 256:512],
                 in0=BK[qkbn].t[:, 256:512], in1=maskb[hp][:, 256:512],
                 op=ALU.mult).inc(sqkm, 1)
            BK[qkbn].release("vector")
            ob = "b1" if i % 2 == 0 else "b2"
            BK[ob].acquire()
            R.wait("tensor", skvbf, 4 * h + i + 1)
            R.wait("tensor", sqs, h + 1)
            R.op("tensor", "matmul", BK[ob].t[:, 0:256], S["kvbf"][:],
                 AR1[:, h * NT + i * 256: h * NT + (i + 1) * 256],
                 start=True, stop=False)
            R.wait("tensor", sqkm, blk + 1)
            assert skvbf.v == 4 * h + i + 1
            R.op("tensor", "matmul", BK[ob].t[:, 0:256],
                 AR4[:, tch * DM + h * 128: tch * DM + (h + 1) * 128],
                 qkmb[hp][:, 0:256], start=False, stop=False)
            R.op("tensor", "matmul", BK[ob].t[:, 0:256],
                 AR4[:, (tch + 1) * DM + h * 128: (tch + 1) * DM + (h + 1) * 128],
                 qkmb[hp][:, 256:512], start=False, stop=True).inc(pe, 1)
            diag_pe_after[blk] = pe.v
            R.wait("vector", pe, pe.v)
            R.op("vector", "scalar_tensor_tensor", out=kvslice, in0=kvslice,
                 scalar=S["bd_bc"][:, h:h + 1], in1=BK["b7"].t[:, 0:128],
                 op0=ALU.mult, op1=ALU.add).inc(skv, 1)
            BK["b7"].release("vector")
            if i < NBLK - 1:
                R.op("vector", "tensor_copy", out=S["kvbf"][:], in_=kvslice)\
                    .inc(skvbf, 1)
            R.op("vector", "tensor_copy",
                 out=AR2[:, h * NT + i * 256: h * NT + (i + 1) * 256],
                 in_=BK[ob].t[:, 0:256]).inc(soev, 1)
            BK[ob].release("vector")
            if i == NBLK - 1:
                R.wait("scalar", soev)
                if h >= 2:
                    R.wait("scalar", pe, stats_pe[h - 2])
                R.op("scalar", "activation", out=osq[h % 2][:],
                     in_=AR2[:, h * NT:(h + 1) * NT], func=AF.Square)\
                    .inc(sq2, 1)
                if h >= 1:
                    hh = h - 1
                    R.wait("tensor", sq2, hh + 1)
                    for half in range(2):
                        o = R.op("tensor", "matmul",
                                 (BK["b4"] if half == 0 else BK["b5"]).t[0:1, 0:512],
                                 S["ONESC_BF"][:],
                                 osq[hh % 2][:, half * 512: half * 512 + 512],
                                 start=(hh == 0), stop=False)
                    o.inc(pe, 1)
                    stats_pe[hh] = pe.v
                if h == 7:
                    uev.wait_all("sync")
                    R.wait("sync", pe, diag_pe_after[31])
                    for fc in range(0, FC, 2):
                        R.dma("sync", AR4[:, fc * NT:(fc + 1) * NT],
                              u_spill[:, fc * NT:(fc + 1) * NT], uload)
                if h == 15:
                    R.wait("sync", pe, diag_pe_after[63])
                    for fc in range(1, FC, 2):
                        R.dma("sync", AR4[:, fc * NT:(fc + 1) * NT],
                              u_spill[:, fc * NT:(fc + 1) * NT], uload)
                gtick(GT_OWN_ATTN)

    R.phases.append(("B3", R.twc()))
    R.wait("tensor", sq2, 16)
    for half in range(2):
        o = R.op("tensor", "matmul",
                 (BK["b4"] if half == 0 else BK["b5"]).t[0:1, 0:512],
                 S["ONESC_BF"][:], osq[15 % 2][:, half * 512: half * 512 + 512],
                 start=False, stop=True)
    o.inc(pe, 1)
    stats_pe[15] = pe.v
    R.wait("scalar", pe, stats_pe[15])
    R.op("scalar", "activation", out=S["rowA2"][0:1, 0:512],
         in_=BK["b4"].t[0:1, 0:512], func=AF.Sqrt,
         scale=S["CROW"][0:1, 3:4], bias=S["CROW"][0:1, 1:2])
    R.op("scalar", "activation", out=S["rowA2"][0:1, 512:1024],
         in_=BK["b5"].t[0:1, 0:512], func=AF.Sqrt,
         scale=S["CROW"][0:1, 3:4], bias=S["CROW"][0:1, 1:2]).inc(act, 1)
    BK["b4"].release("scalar")
    BK["b5"].release("scalar")
    R.wait("vector", act)
    R.op("vector", "reciprocal", out=S["rowA"][0:1, 0:1024],
         in_=S["rowA2"][0:1, 0:1024]).inc(dve, 1)
    BK["b4"].acquire()
    BK["b5"].acquire()
    R.wait("tensor", dve)
    R.op("tensor", "matmul", BK["b4"].t[:, 0:512], S["ONESR_F"][:],
         S["rowA"][0:1, 0:512], start=True, stop=True)
    R.op("tensor", "matmul", BK["b5"].t[:, 0:512], S["ONESR_F"][:],
         S["rowA"][0:1, 512:1024], start=True, stop=True).inc(pe, 1)
    rstdb = SCRB[:, 2048:3072]
    R.wait("vector", pe)
    R.op("vector", "tensor_copy", out=rstdb[:, 0:512], in_=BK["b4"].t[:, 0:512])
    R.op("vector", "tensor_copy", out=rstdb[:, 512:1024],
         in_=BK["b5"].t[:, 0:512]).inc(dve, 1)
    BK["b4"].release("vector")
    BK["b5"].release("vector")

    R.phases.append(("thresh", R.twc()))
    emit_gamma_chunks(GCH)
    BK["b7"].acquire()
    R.wait("tensor", gred, GCH)
    R.op("tensor", "matmul", BK["b7"].t[0:1, 0:4], S["ONESC_F"][:],
         S["gacc"][:], start=True, stop=True).inc(pe, 1)
    R.wait("vector", pe)
    R.op("vector", "tensor_copy", out=S["gsum"][:], in_=BK["b7"].t[0:1, 0:4])\
        .inc(dve, 1)
    BK["b7"].release("vector")
    R.op("vector", "tensor_scalar", out=S["th"][:], in0=S["gsum"][:],
         scalar1=1.0 / NW_ELEMS, scalar2=1e-8, op0=ALU.mult, op1=ALU.add)
    R.op("vector", "tensor_scalar", out=S["th"][:], in0=S["th"][:],
         scalar1=0.5, scalar2=None, op0=ALU.mult)
    R.op("vector", "tensor_scalar", out=S["nth"][:], in0=S["th"][:],
         scalar1=-1.0, scalar2=None, op0=ALU.mult).inc(dve, 1)
    BK["b5"].acquire()
    R.wait("tensor", dve)
    R.op("tensor", "matmul", BK["b5"].t[:, 0:4], S["ONESR_F"][:], S["th"][:],
         start=True, stop=True)
    R.op("tensor", "matmul", BK["b5"].t[:, 4:8], S["ONESR_F"][:], S["nth"][:],
         start=True, stop=True).inc(pe, 1)
    R.wait("vector", pe)
    R.op("vector", "tensor_copy", out=S["thb"][:], in_=BK["b5"].t[:, 0:4])
    R.op("vector", "tensor_copy", out=S["nthb"][:], in_=BK["b5"].t[:, 4:8])\
        .inc(dve, 1)
    BK["b5"].release("vector")
    th_ready = dve.v

    gdone = R.sem("gdone")
    ttmp = SCRB[:, 3072:4096]
    R.wait("vector", uload, 16 * 16)
    for fc in range(FC):
        R.op("vector", "tensor_mul", out=ttmp[:],
             in0=AR2[:, fc * NT:(fc + 1) * NT], in1=rstdb[:])
        R.op("vector", "tensor_mul", out=AR3[:, fc * NT:(fc + 1) * NT],
             in0=ttmp[:], in1=AR4[:, fc * NT:(fc + 1) * NT]).inc(gdone, 1)

    R.phases.append(("B4", R.twc()))
    xch = [SCRF[:, 0:1024], SCRF[:, 1024:2048]]
    x1done = R.sem("x1done")
    gp_B4 = {}
    b4_wc = {}

    def emit_wload4(jq):
        p = jq % 2
        if jq >= 2:
            R.wait("sync", x1done, 2 * (jq - 1))
        ldx.dma("sync", p, xch[p][:], xT[jq * 128:(jq + 1) * 128, :])
        for half in range(2):
            gg = PRM["g"]
            PRM["g"] += 1
            if gg >= 2:
                R.wait("sync", wcast, gg - 1)
            wld.dma("sync", gg % 2, WG[gg % 2][:],
                    wout[:, jq * 2048 + half * 1024: jq * 2048 + (half + 1) * 1024])
            wld.wait("vector", gg % 2)
            if jq >= 2:
                R.wait("vector", pe, gp_B4[jq - 2])
            else:
                R.wait("vector", pe, own_tail)
            R.op("vector", "tensor_copy", out=WGB[gg % 4][:],
                 in_=WG[gg % 2][:]).inc(wcast, 1)
        b4_wc[jq] = wcast.v

    R.wait("sync", cast_done)
    emit_wload4(0)
    for j in range(16):
        p = j % 2
        if j + 1 < 16:
            emit_wload4(j + 1)
        R.wait("tensor", wcast, b4_wc[j])
        if j == 0:
            R.wait("tensor", gdone, FC)
        b_lo, b_hi = WGB[(2 * j) % 4], WGB[(2 * j + 1) % 4]
        pnames = ("b0", "b1") if j % 2 == 0 else ("b2", "b3")
        for tc_ in range(2):
            bn = pnames[tc_]
            BK[bn].acquire()
            for kc in range(16):
                wb = b_lo if kc < 8 else b_hi
                off = (kc % 8) * 128
                o = R.op("tensor", "matmul", BK[bn].t[:, 0:512],
                         wb[:, off:off + 128],
                         AR3[:, kc * NT + tc_ * 512: kc * NT + tc_ * 512 + 512],
                         start=(kc == 0), stop=(kc == 15))
            o.inc(pe, 1)
            R.wait("vector", pe)
            ldx.wait("vector", p)
            R.op("vector", "tensor_tensor",
                 out=AR2[:, j * NT + tc_ * 512: j * NT + tc_ * 512 + 512],
                 in0=BK[bn].t[:, 0:512],
                 in1=xch[p][:, tc_ * 512: tc_ * 512 + 512],
                 op=ALU.add).inc(x1done, 1)
            BK[bn].release("vector")
        gp_B4[j] = pe.v

    R.phases.append(("B5_ln2", R.twc()))
    ln2_done = emit_layernorm(R, S, BK, AR2, AR1, S["CROW"][0:1, 0:1],
                              x1done, "2")
    spill = R.sem("spill")
    R.wait("scalar", ln2_done)
    R.dma("scalar", x1_spill, AR2[:], spill)

    def hidden_ap(kc, lo, width):
        if kc < 16:
            return AR2[:, kc * NT + lo: kc * NT + lo + width]
        if kc < 32:
            return AR4[:, (kc - 16) * NT + lo: (kc - 16) * NT + lo + width]
        return AR3[:, (kc - 32) * NT + lo: (kc - 32) * NT + lo + width]

    R.phases.append(("B7", R.twc()))
    f32bufs = {"w1": [SCRF[:, 0:1024], SCRF[:, 1024:2048]],
               "w2": [SCRF[:, 2048:3072], SCRF[:, 3072:4096]]}
    wqbufs = {"w1": [[WQ[:, 0:1024], WQ[:, 1024:2048]],
                     [WQ[:, 4096:5120], WQ[:, 5120:6144]]],
              "w2": [[WQ[:, 2048:3072], WQ[:, 3072:4096]],
                     [WQ[:, 6144:7168], WQ[:, 7168:8192]]]}
    tsD = SCRB[:, 0:2048]
    tsP = SCRB[:, 2048:4096]
    sgb = [SCRB[:, 4096 + i * 512: 4608 + i * 512] for i in range(4)]
    fld = {"w1": Stream(R, "fld1", 2), "w2": Stream(R, "fld2", 2)}
    qbar = R.sem("qbarD")
    qbar_n = [0]
    fq = {"w1": R.sem("fq1"), "w2": R.sem("fq2")}
    sgd = R.sem("sgd")
    hid = R.sem("hid")
    wsrc = {"w1": w1t, "w2": w2t}
    thcol = {"w1": 0, "w2": 1, "w3": 2}
    qtok = {}
    R.wait("sync", spill, 16)
    R.wait("sync", ln2_done)
    R.wait("vector", spill, 16)
    R.wait("gpsimd", spill, 16)
    R.wait("vector", gdone)
    uev.wait_all("scalar")
    mm_pe_after = {}

    def emit_quant7(jq):
        for wname in ("w1", "w2"):
            ts = tsD if wname == "w1" else tsP
            for half in range(2):
                if (wname, half) in qtok:
                    R.wait("sync", qbar, qtok[(wname, half)])
                fld[wname].dma("sync", half, f32bufs[wname][half][:],
                               wsrc[wname][:, jq * 2048 + half * 1024: jq * 2048 + (half + 1) * 1024])
                fld[wname].wait("vector", half)
                R.wait("vector", dve, th_ready)
                if 2 * jq + half >= 1:
                    R.wait("vector", fq[wname], 2 * jq + half)
                col = thcol[wname]
                src = f32bufs[wname][half][:]
                dst = wqbufs[wname][jq % 2][half][:]
                R.op("vector", "tensor_scalar", out=ts[:, 0:1024], in0=src,
                     scalar1=S["thb"][:, col:col + 1], scalar2=None,
                     op0=ALU.is_ge).inc(qbar, 1)
                R.op("vector", "tensor_scalar", out=ts[:, 1024:2048], in0=src,
                     scalar1=S["nthb"][:, col:col + 1], scalar2=None,
                     op0=ALU.is_le).inc(qbar, 1)
                qbar_n[0] += 2
                qtok[(wname, half)] = qbar_n[0]
                R.wait("gpsimd", qbar, qbar_n[0])
                if jq >= 2 and (jq - 2, wname) in mm_pe_after:
                    R.wait("gpsimd", pe, mm_pe_after[(jq - 2, wname)])
                R.op("gpsimd", "tensor_tensor", out=dst, in0=ts[:, 0:1024],
                     in1=ts[:, 1024:2048], op=ALU.subtract).inc(fq[wname], 1)

    w2banks = [("b4", "b5"), ("b6", "b7")]
    emit_quant7(0)
    for j in range(44):
        if j + 1 < 44:
            emit_quant7(j + 1)
        R.wait("tensor", fq["w1"], 2 * (j + 1))
        R.wait("tensor", fq["w2"], 2 * (j + 1))
        x1banks = ("b0", "b1") if j % 2 == 0 else ("b2", "b3")
        for tc_ in range(2):
            bn = x1banks[tc_]
            BK[bn].acquire()
            for kc in range(16):
                wb = wqbufs["w1"][j % 2][kc // 8]
                off = (kc % 8) * 128
                o = R.op("tensor", "matmul", BK[bn].t[:, 0:512],
                         wb[:, off:off + 128],
                         AR1[:, kc * NT + tc_ * 512: kc * NT + tc_ * 512 + 512],
                         start=(kc == 0), stop=(kc == 15))
            o.inc(pe, 1)
            if tc_ == 1:
                mm_pe_after[(j, "w1")] = pe.v
            bn2 = w2banks[j % 2][tc_]
            BK[bn2].acquire()
            for kc in range(16):
                wb = wqbufs["w2"][j % 2][kc // 8]
                off = (kc % 8) * 128
                o = R.op("tensor", "matmul", BK[bn2].t[:, 0:512],
                         wb[:, off:off + 128],
                         AR1[:, kc * NT + tc_ * 512: kc * NT + tc_ * 512 + 512],
                         start=(kc == 0), stop=(kc == 15))
            o.inc(pe, 1)
            if tc_ == 1:
                mm_pe_after[(j, "w2")] = pe.v
            sg = sgb[(j % 2) * 2 + tc_]
            R.wait("scalar", pe)
            if j >= 2:
                R.wait("scalar", hid, 2 * (j - 2) + tc_ + 1)
            R.op("scalar", "activation", out=sg[:], in_=BK[bn].t[:, 0:512],
                 func=AF.Sigmoid).inc(sgd, 1)
            BK[bn].release("scalar")
            R.wait("vector", sgd)
            R.op("vector", "tensor_tensor", out=hidden_ap(j, tc_ * 512, 512),
                 in0=BK[bn2].t[:, 0:512], in1=sg[:], op=ALU.mult)\
                .inc(hid, 1)
            BK[bn2].release("vector")

    R.phases.append(("B8", R.twc()))
    w3f = [SCRF[:, 0:1408], SCRF[:, 1408:2816]]
    OSTG = [SCRF[:, 2816:3328], SCRF[:, 3328:3840]]
    w3q = [[AR1[:, par * 5632 + i * 1408: par * 5632 + (i + 1) * 1408]
            for i in range(4)] for par in range(2)]
    t3a = [WQ[:, 0:1408], WQ[:, 2816:4224]]
    t3b = [WQ[:, 1408:2816], WQ[:, 4224:5632]]
    x1b = [SCRB[:, 0:1024], SCRB[:, 1024:2048]]
    f3ld = Stream(R, "f3ld", 2)
    f3q = R.sem("f3q")
    oev = R.sem("oev")
    ost = Stream(R, "ost", 2)
    R.wait("sync", fq["w1"], 88)
    R.wait("sync", fq["w2"], 88)
    R.wait("vector", sgd, 88)
    R.wait("vector", hid, 88)
    R.wait("gpsimd", pe)
    qtok3 = {}
    w3_pe_after = {}

    def emit_quant8(jq):
        p = jq % 2
        if jq >= 2:
            R.wait("sync", oev, (jq - 1) * 2)
        ldx.dma("sync", p, x1b[p][:], x1_spill[:, jq * NT:(jq + 1) * NT])
        for part in range(4):
            g = jq * 4 + part
            if g >= 2 and (g - 2) in qtok3:
                R.wait("sync", qbar, qtok3[g - 2])
            f3ld.dma("sync", part % 2, w3f[part % 2][:],
                     w3t[:, jq * 5632 + part * 1408: jq * 5632 + (part + 1) * 1408])
            f3ld.wait("vector", part % 2)
            R.wait("vector", dve, th_ready)
            col = thcol["w3"]
            if g >= 2:
                R.wait("vector", f3q, g - 1)
            gp2 = g % 2
            R.op("vector", "tensor_scalar", out=t3a[gp2][:], in0=w3f[part % 2][:],
                 scalar1=S["thb"][:, col:col + 1], scalar2=None, op0=ALU.is_ge)\
                .inc(qbar, 1)
            R.op("vector", "tensor_scalar", out=t3b[gp2][:], in0=w3f[part % 2][:],
                 scalar1=S["nthb"][:, col:col + 1], scalar2=None, op0=ALU.is_le)\
                .inc(qbar, 1)
            qbar_n[0] += 2
            qtok3[g] = qbar_n[0]
            R.wait("gpsimd", qbar, qbar_n[0])
            if jq >= 2 and (jq - 2) in w3_pe_after:
                R.wait("gpsimd", pe, w3_pe_after[jq - 2])
            R.op("gpsimd", "tensor_tensor", out=w3q[jq % 2][part][:],
                 in0=t3a[gp2][:], in1=t3b[gp2][:], op=ALU.subtract).inc(f3q, 1)

    emit_quant8(0)
    for j in range(16):
        p = j % 2
        if j + 1 < 16:
            emit_quant8(j + 1)
        R.wait("tensor", f3q, (j + 1) * 4)
        if j == 0:
            R.wait("tensor", hid, 88)
        pn3 = ("b0", "b1") if j % 2 == 0 else ("b2", "b3")
        for tc_ in range(2):
            bn = pn3[tc_]
            BK[bn].acquire()
            for kc in range(44):
                part = kc // 11
                off = (kc % 11) * 128
                o = R.op("tensor", "matmul", BK[bn].t[:, 0:512],
                         w3q[j % 2][part][:, off:off + 128],
                         hidden_ap(kc, tc_ * 512, 512),
                         start=(kc == 0), stop=(kc == 43))
            o.inc(pe, 1)
            R.wait("vector", pe)
            ldx.wait("vector", p)
            if j >= 1:
                ost.wait("vector", tc_, j)
            R.op("vector", "tensor_tensor", out=OSTG[tc_][:],
                 in0=BK[bn].t[:, 0:512],
                 in1=x1b[p][:, tc_ * 512: tc_ * 512 + 512],
                 op=ALU.add).inc(oev, 1)
            BK[bn].release("vector")
            R.wait("scalar", oev)
            ost.dma("scalar", tc_,
                    out_c[j * 128:(j + 1) * 128, tc_ * 512:(tc_ + 1) * 512],
                    OSTG[tc_][:])
        w3_pe_after[j] = pe.v
    ost.wait_all("scalar")
    R.phases.append(("end", R.twc()))
    global _LAST_PHASES
    _LAST_PHASES = R.phases
    R.replay()
    return nc



_CACHE = {}
_LAST_PHASES = None


def _tile_wT(w, oc):
    t = w.reshape(oc, 128, 16, 128).transpose(3, 0, 2, 1)
    return np.ascontiguousarray(t.reshape(128, oc * 16 * 128))


def _tile_w3T(w3):
    t = w3.reshape(16, 128, 44, 128).transpose(3, 0, 2, 1)
    return np.ascontiguousarray(t.reshape(128, 16 * 44 * 128))


def kernel(L0_attn_x, L0_attn_slope_rate, L0_attn_W_qkvu, L0_attn_W_out,
           L0_ffn_w1, L0_ffn_w2, L0_ffn_w3):
    x = np.asarray(L0_attn_x, dtype=np.float32).reshape(8192, DM)
    slope = np.ascontiguousarray(
        np.asarray(L0_attn_slope_rate, dtype=np.float32).reshape(1, NH))
    Wq = np.asarray(L0_attn_W_qkvu, dtype=np.float32)
    Wo = np.asarray(L0_attn_W_out, dtype=np.float32)
    w1 = np.asarray(L0_ffn_w1, dtype=np.float32)
    w2 = np.asarray(L0_ffn_w2, dtype=np.float32)
    w3 = np.asarray(L0_ffn_w3, dtype=np.float32)

    xT = np.ascontiguousarray(x.T)
    wqkvu_t = _tile_wT(Wq, 64)
    wout_t = _tile_wT(Wo, 16)
    w1_t = _tile_wT(w1, 44)
    w2_t = _tile_wT(w2, 44)
    w3_t = _tile_w3T(w3)

    consts = make_consts()
    if "F" not in _CACHE:
        _CACHE["F"] = build_prog_fused(consts)
    ncF = _CACHE["F"]

    in_F = []
    for c in range(8):
        dist = np.full((128, 8), 1e9, dtype=np.float32)
        for j in range(c):
            dist[:, j] = 1024.0 * (c - 1 - j)
        in_F.append({
            "xT": np.ascontiguousarray(xT[:, c * NT:(c + 1) * NT]),
            "distc": dist,
            "slope": slope,
            "wqkvu": wqkvu_t,
            "wout": wout_t,
            "w1t": w1_t,
            "w2t": w2_t,
            "w3t": w3_t,
        })
    res = run_bass_kernel_spmd(ncF, in_F, core_ids=list(range(8)))

    out = np.concatenate([res.results[c]["out_c"] for c in range(8)], axis=1)
    return np.ascontiguousarray(out.T).reshape(1, 8192, DM).astype(np.float32)


if __name__ == "__main__":
    ncF = build_prog_fused(make_consts())
    print("build ok; sems:", ncF and "n/a")


# version 13
# speedup vs baseline: 1.2934x; 1.0435x over previous
import numpy as np
import ml_dtypes
import concourse.bass as bass
import concourse.mybir as mybir
from concourse.bass_utils import run_bass_kernel_spmd

F32 = mybir.dt.float32
BF16 = mybir.dt.bfloat16
AF = mybir.ActivationFunctionType
ALU = mybir.AluOpType
BF = ml_dtypes.bfloat16
AX = mybir.AxisListType.X

NT = 1024
DM = 2048
NH = 16
HD = 5632
FC = DM // 128
TC = NT // 128
NBLK = 4
HC = HD // 128
NW_ELEMS = float(HD * DM)
GCH = 528

ENGINES = ("sync", "scalar", "vector", "tensor", "gpsimd")


class Sem:
    __slots__ = ("h", "v", "name")

    def __init__(self, h, name):
        self.h, self.v, self.name = h, 0, name


class Op:
    __slots__ = ("meth", "args", "kwargs", "incs")

    def __init__(self, meth, args, kwargs):
        self.meth, self.args, self.kwargs, self.incs = meth, args, kwargs, []

    def inc(self, sem, n):
        assert not self.incs, f"op {self.meth} already has a sync update"
        self.incs.append((sem, n, sem.v + n))
        sem.v += n
        return self

    def tok(self):
        if self.incs:
            s, n, va = self.incs[0]
            return (s, va)
        return None


class Rec:
    def __init__(self, nc):
        self.nc = nc
        self.q = {e: [] for e in ENGINES}
        self._n = 0
        self._lastwait = {}

    def sem(self, name=None):
        self._n += 1
        nm = name or f"s{self._n}"
        return Sem(self.nc.alloc_semaphore(nm), nm)

    def op(self, eng, meth, *args, **kwargs):
        o = Op(meth, args, kwargs)
        self.q[eng].append(o)
        return o

    def wait(self, eng, sem, val=None):
        v = sem.v if val is None else val
        if v <= 0:
            return
        key = (eng, sem.name)
        if self._lastwait.get(key, -1) >= v:
            return
        self._lastwait[key] = v
        self.q[eng].append(Op("wait_ge", (sem.h, v), {}))

    def dma(self, eng, out, in_, sem, **kw):
        return self.op(eng, "dma_start", out=out, in_=in_, **kw).inc(sem, 16)

    @staticmethod
    def _interval(ap):
        ext = 1
        for (step, count) in ap.ap:
            ext += abs(int(step)) * (int(count) - 1)
        off = int(ap.offset)
        return (ap.tensor.name, off, off + ext)

    def _rw_of(self, o):
        if o.meth in ("wait_ge", "nop"):
            return [], []
        writes, reads = [], []
        args = list(o.args)
        kw = dict(o.kwargs)
        if o.meth == "memset":
            writes.append(self._interval(args[0]))
            return writes, reads
        if "out" in kw:
            writes.append(self._interval(kw.pop("out")))
        for v in list(args) + list(kw.values()):
            if isinstance(v, bass.AP):
                reads.append(self._interval(v))
        return writes, reads

    def _auto_chain(self):
        def overlap(a, b):
            return a[0] == b[0] and a[1] < b[2] and b[1] < a[2]

        for eng in ("vector", "scalar", "gpsimd"):
            ops = self.q[eng]
            chain = self.sem(f"chain_{eng}")
            writers = []
            readers = []
            n = len(ops)
            deps = [None] * n
            is_dma = [o.meth == "dma_start" for o in ops]
            for i, o in enumerate(ops):
                ws, rs = self._rw_of(o)
                need = set()
                for r in rs:
                    for (w, j) in writers:
                        if (not is_dma[j]) and overlap(r, w):
                            need.add(j)
                for w in ws:
                    for (pw, j) in writers:
                        if (not is_dma[j]) and overlap(w, pw):
                            need.add(j)
                    for (pr, j) in readers:
                        if (not is_dma[j]) and overlap(w, pr):
                            need.add(j)
                deps[i] = need
                for w in ws:
                    writers.append((w, i))
                for r in rs:
                    readers.append((r, i))
            producers = set()
            for need in deps:
                producers |= need
            for j in sorted(producers):
                if not ops[j].incs:
                    ops[j].inc(chain, 1)
            new_ops = []
            lastw = {}
            for i, o in enumerate(ops):
                if deps[i]:
                    for j in sorted(deps[i]):
                        t = ops[j].tok()
                        s, v = t
                        if lastw.get(s.name, -1) < v:
                            new_ops.append(Op("wait_ge", (s.h, v), {}))
                            lastw[s.name] = v
                new_ops.append(o)
            self.q[eng] = new_ops

    def replay(self):
        self._auto_chain()
        with self.nc.Block() as block:
            for eng in ENGINES:
                ops = self.q[eng]

                def runner(engine, ops=ops):
                    for o in ops:
                        r = getattr(engine, o.meth)(*o.args, **o.kwargs)
                        if o.incs:
                            s, nn_, _ = o.incs[0]
                            r.then_inc(s.h, nn_)

                getattr(block, eng)(runner)


class Bank:
    def __init__(self, R, t, name):
        self.R, self.t, self.name = R, t, name
        self.sem = R.sem(f"bk_{name}")
        self.tok = None

    def acquire(self, eng="tensor"):
        if self.tok is not None:
            self.R.wait(eng, self.tok[0], self.tok[1])

    def release(self, eng):
        o = self.R.q[eng][-1]
        if not o.incs:
            o.inc(self.sem, 1)
        self.tok = o.tok()


class Stream:
    def __init__(self, R, name, nslots):
        self.R = R
        self.sems = [R.sem(f"{name}_{i}") for i in range(nslots)]
        self.uses = [0] * nslots

    def dma(self, eng, slot, out, in_, **kw):
        self.uses[slot] += 1
        return self.R.op(eng, "dma_start", out=out, in_=in_, **kw)\
            .inc(self.sems[slot], 16)

    def wait(self, eng, slot, use=None):
        u = self.uses[slot] if use is None else use
        self.R.wait(eng, self.sems[slot], 16 * u)

    def wait_all(self, eng):
        for i, s in enumerate(self.sems):
            self.R.wait(eng, s, 16 * self.uses[i])


BF16_CONSTS = {"IDX0", "IDX1", "QDROW", "IDEN_BF", "ONESC_BF", "KD2"}


def make_consts():
    c = {}
    idx = []
    for k in range(2):
        s = 128 * k + np.arange(128)[:, None]
        m = np.arange(256)[None, :]
        d = (m - s).astype(np.float32)
        d[m < s] = 1e9
        idx.append(np.ascontiguousarray(d))
    c["IDX0"], c["IDX1"] = idx
    c["QDROW"] = np.broadcast_to(
        (np.arange(256, dtype=np.float32) + 1.0)[None, :], (128, 256)).copy()
    c["IDEN_BF"] = np.eye(128, dtype=np.float32)
    c["ONESC_BF"] = np.ones((128, 1), dtype=np.float32)
    kd = np.zeros((128, 2), dtype=np.float32)
    kd[:, 0] = 255.0 - np.arange(128)
    kd[:, 1] = 255.0 - (128 + np.arange(128))
    c["KD2"] = kd
    kp = np.zeros((128, 8), dtype=np.float32)
    for tb in range(8):
        kp[:, tb] = 1023.0 - (tb * 128 + np.arange(128))
    c["KDPRE"] = kp
    c["ONESR_F"] = np.ones((1, 128), dtype=np.float32)
    c["ONESC_F"] = np.ones((128, 1), dtype=np.float32)
    cc = np.zeros((128, 8), dtype=np.float32)
    cc[:, 0] = 1e-5
    cc[:, 1] = 1e-6
    cc[:, 2] = 1e-8
    cc[:, 3] = 1.0 / 2048.0
    cc[:, 4] = 256.0
    cc[:, 5] = 1024.0
    c["CCOL"] = cc
    c["CROW"] = np.ascontiguousarray(cc[0:1, :])
    return c


def load_consts(nc, R, names, consts, sem):
    t = {}
    buf = {}
    for k in names:
        arr = consts[k]
        if k in BF16_CONSTS:
            arr = arr.astype(BF)
            buf[k] = nc.alloc_sbuf_tensor(f"cb_{k}", list(arr.shape), BF16)
        else:
            buf[k] = nc.alloc_sbuf_tensor(f"cb_{k}", list(arr.shape), F32)
        t[k] = nc.inline_tensor(arr, name=f"c_{k}")
        R.dma("sync", buf[k][:], t[k].ap(), sem)
    return buf


def emit_layernorm(R, S, BK, src, dst, eps_row, src_sem, tag):
    pe, dve, act = R.s["pe"], R.s["dve"], R.s["act"]
    xsq = [S["scrB"][:, 0:1024], S["scrB"][:, 1024:2048]]
    ssum = [BK["b2"].t[0:1, 0:512], BK["b3"].t[0:1, 0:512]]
    ssq = [BK["b4"].t[0:1, 0:512], BK["b5"].t[0:1, 0:512]]
    for b in ("b2", "b3", "b4", "b5"):
        BK[b].acquire()
    R.wait("scalar", src_sem)
    R.wait("tensor", src_sem)
    sq_done = R.sem(f"lnsq{tag}")
    for fc in range(FC):
        srcc = src[:, fc * NT:(fc + 1) * NT]
        p = fc % 2
        if fc >= 2:
            R.wait("scalar", pe)
        R.op("scalar", "activation", out=xsq[p][:], in_=srcc, func=AF.Square)\
            .inc(sq_done, 1)
        R.wait("tensor", sq_done)
        for half in range(2):
            rsrc = src[:, fc * NT + half * 512: fc * NT + half * 512 + 512]
            R.op("tensor", "matmul", ssum[half], S["ONESC_BF"][:], rsrc,
                 start=(fc == 0), stop=(fc == FC - 1))
            o = R.op("tensor", "matmul", ssq[half], S["ONESC_BF"][:],
                     xsq[p][:, half * 512: half * 512 + 512],
                     start=(fc == 0), stop=(fc == FC - 1))
        o.inc(pe, 1)
    mean, ex2, tmpr = S["rowA"], S["rowB"], S["rowA2"]
    rstd = ex2
    R.wait("vector", pe)
    for half in range(2):
        sl = slice(half * 512, half * 512 + 512)
        R.op("vector", "tensor_scalar", out=mean[0:1, sl], in0=ssum[half],
             scalar1=1.0 / DM, scalar2=None, op0=ALU.mult)
        R.op("vector", "tensor_scalar", out=ex2[0:1, sl], in0=ssq[half],
             scalar1=1.0 / DM, scalar2=None, op0=ALU.mult)
    o = R.op("vector", "tensor_mul", out=tmpr[0:1, 0:1024],
             in0=mean[0:1, 0:1024], in1=mean[0:1, 0:1024])
    o = R.op("vector", "tensor_sub", out=ex2[0:1, 0:1024],
             in0=ex2[0:1, 0:1024], in1=tmpr[0:1, 0:1024])
    o.inc(dve, 1)
    for b in ("b2", "b3", "b4", "b5"):
        BK[b].release("vector")
    R.wait("scalar", dve)
    R.op("scalar", "activation", out=tmpr[0:1, 0:1024], in_=ex2[0:1, 0:1024],
         func=AF.Sqrt, bias=eps_row).inc(act, 1)
    R.wait("vector", act)
    R.op("vector", "reciprocal", out=rstd[0:1, 0:1024],
         in_=tmpr[0:1, 0:1024]).inc(dve, 1)
    for b in ("b2", "b3", "b4", "b5"):
        BK[b].acquire()
    R.wait("tensor", dve)
    bmap = (("b2", 0, True), ("b4", 1, True), ("b3", 0, False), ("b5", 1, False))
    for (bn, half, is_mean) in bmap:
        sl = slice(half * 512, half * 512 + 512)
        o = R.op("tensor", "matmul", BK[bn].t[:, 0:512], S["ONESR_F"][:],
                 (mean if is_mean else rstd)[0:1, sl], start=True, stop=True)
    o.inc(pe, 1)
    meanb = S["scrB"][:, 0:1024]
    rstdb = S["scrB"][:, 1024:2048]
    R.wait("vector", pe)
    for (bn, half, is_mean) in bmap:
        dstb = (meanb if is_mean else rstdb)
        o = R.op("vector", "tensor_copy",
                 out=dstb[:, half * 512: half * 512 + 512],
                 in_=BK[bn].t[:, 0:512])
    o.inc(dve, 1)
    for b in ("b2", "b3", "b4", "b5"):
        BK[b].release("vector")
    ttmp = S["scrB"][:, 2048:3072]
    done = R.sem(f"lnd{tag}")
    for fc in range(FC):
        srcc = src[:, fc * NT:(fc + 1) * NT]
        dstc = dst[:, fc * NT:(fc + 1) * NT]
        R.op("vector", "tensor_sub", out=ttmp[:], in0=srcc, in1=meanb)
        R.op("vector", "tensor_mul", out=dstc, in0=ttmp[:], in1=rstdb)\
            .inc(done, 1)
    return done


def common_setup(nc):
    R = Rec(nc)
    R.s = {k: R.sem(k) for k in ("pe", "dve", "act", "st")}
    BK = {}
    for i in range(6):
        BK[f"b{i}"] = Bank(R, nc.alloc_psum_tensor(f"pb{i}", [128, 512], F32), f"b{i}")
    BK["b6"] = Bank(R, nc.alloc_psum_tensor("pb6", [128, 512], F32), "b6")
    BK["b7"] = Bank(R, nc.alloc_psum_tensor("pb7", [128, 512], F32), "b7")
    return R, BK


def emit_slope_derived(R, S, BK, ld_const):
    pe, dve = R.s["pe"], R.s["dve"]
    sder = R.sem("sder")
    BK["b0"].acquire()
    R.wait("tensor", ld_const)
    R.op("tensor", "matmul", BK["b0"].t[:, 0:16], S["ONESR_F"][:],
         S["slope"][:], start=True, stop=True).inc(pe, 1)
    R.wait("vector", pe)
    R.op("vector", "tensor_scalar", out=S["sneg"][:], in0=BK["b0"].t[:, 0:16],
         scalar1=-1.0, scalar2=None, op0=ALU.mult).inc(dve, 1)
    BK["b0"].release("vector")
    R.wait("scalar", dve)
    R.wait("scalar", ld_const)
    R.op("scalar", "activation", out=S["bd_bc"][:], in_=S["sneg"][:],
         func=AF.Exp, scale=S["CCOL"][:, 4:5]).inc(sder, 1)
    for k in range(8):
        R.op("scalar", "activation", out=S["kvw"][:, k * NH:(k + 1) * NH],
             in_=S["sneg"][:], func=AF.Exp, scale=S["DIST"][:, k:k + 1])\
            .inc(sder, 1)
    for h in range(NH):
        R.op("scalar", "activation", out=S["kdb_all"][:, 2 * h:2 * h + 2],
             in_=S["KD2"][:], func=AF.Exp, scale=S["sneg"][:, h:h + 1])\
            .inc(sder, 1)
    for h in range(NH):
        R.op("scalar", "activation", out=S["kdpre_all"][:, 8 * h:8 * h + 8],
             in_=S["KDPRE"][:], func=AF.Exp, scale=S["sneg"][:, h:h + 1])\
            .inc(sder, 1)
    return sder


def build_prog_fused(consts):
    nc = bass.Bass("TRN2", target_bir_lowering=False, debug=False)
    xT = nc.dram_tensor("xT", [DM, NT], F32, kind="ExternalInput").ap()
    distc = nc.dram_tensor("distc", [128, 8], F32, kind="ExternalInput").ap()
    slope = nc.dram_tensor("slope", [1, NH], F32, kind="ExternalInput").ap()
    wqkvu = nc.dram_tensor("wqkvu", [128, 64 * 2048], F32, kind="ExternalInput").ap()
    wout = nc.dram_tensor("wout", [128, 16 * 2048], F32, kind="ExternalInput").ap()
    w1t = nc.dram_tensor("w1t", [128, 44 * 2048], F32, kind="ExternalInput").ap()
    w2t = nc.dram_tensor("w2t", [128, 44 * 2048], F32, kind="ExternalInput").ap()
    w3t = nc.dram_tensor("w3t", [128, 16 * 5632], F32, kind="ExternalInput").ap()
    out_c = nc.dram_tensor("out_c", [DM, NT], F32, kind="ExternalOutput").ap()
    u_spill = nc.dram_tensor("u_spill", [128, FC * NT], BF16, kind="Internal").ap()
    x1_spill = nc.dram_tensor("x1_spill", [128, FC * NT], BF16, kind="Internal").ap()
    l_out_t = nc.dram_tensor("l_out", [128, NH * 128], BF16, kind="Internal")
    l_all_t = nc.dram_tensor("l_all", [8 * 128, NH * 128], BF16, kind="Internal",
                             addr_space="Shared")

    R, BK = common_setup(nc)
    R.phases = []

    def _twc():
        return sum(2 if o.meth == "matmul" else 1 for o in R.q["tensor"])
    R.twc = _twc
    s = R.s
    pe, dve, act = s["pe"], s["dve"], s["act"]
    A = nc.alloc_sbuf_tensor
    AR1 = A("ar1", [128, FC * NT], BF16)
    AR2 = A("ar2", [128, FC * NT], BF16)
    AR3 = A("ar3", [128, FC * NT], BF16)
    AR4 = A("ar4", [128, FC * NT], BF16)
    SCRB = A("scrb", [128, 6144], BF16)
    SCRF = A("scrf", [128, 5120], F32)
    WGB = [A(f"wgb{i}", [128, 1024], BF16) for i in range(4)]
    WQ = A("wq", [128, 8192], BF16)
    KVOUT = A("kvout", [128, NH * 128], F32)
    S = {
        "slope": A("s_slope", [1, NH], F32),
        "DIST": A("s_dist", [128, 8], F32),
        "kvw": A("s_kvw", [128, 8 * NH], BF16),
        "sneg": A("s_sneg", [128, NH], F32),
        "bd_bc": A("s_bdbc", [128, NH], BF16),
        "kdb_all": A("s_kdball", [128, 2 * NH], F32),
        "kdpre_all": A("s_kdpall", [128, 8 * NH], F32),
        "rowA": A("s_rowA", [1, 1024], F32),
        "rowA2": A("s_rowA2", [1, 1024], F32),
        "rowB": A("s_rowB", [1, 1024], F32),
        "gacc": A("s_gacc", [128, 4], F32),
        "gtmp": A("s_gtmp", [128, 4], F32),
        "gsum": A("s_gsum", [1, 4], F32),
        "th": A("s_th", [1, 4], F32),
        "nth": A("s_nth", [1, 4], F32),
        "thb": A("s_thb", [128, 4], F32),
        "nthb": A("s_nthb", [128, 4], F32),
        "kvbf": SCRB[:, 2048:2176],
        "scrB": SCRB,
    }
    ld_const = R.sem("ldconst")
    CB = load_consts(nc, R,
                     ("IDX0", "IDX1", "QDROW", "KD2", "KDPRE", "ONESR_F",
                      "ONESC_F", "IDEN_BF", "ONESC_BF", "CCOL", "CROW"),
                     consts, ld_const)
    S.update(CB)
    R.dma("sync", S["slope"][:], slope, ld_const)
    R.dma("sync", S["DIST"][:], distc, ld_const)
    R.wait("vector", ld_const)
    R.wait("tensor", ld_const)
    sder = emit_slope_derived(R, S, BK, ld_const)

    gb = [SCRF[:, 4096:4608], SCRF[:, 4608:5120]]
    gld = Stream(R, "gld", 2)
    gred = R.sem("gred")
    gbar = R.sem("gbar")
    gsrc = []
    for wi, wt in enumerate((w1t, w2t, w3t)):
        for kk in range(176):
            gsrc.append((wi, wt[:, kk * 512:(kk + 1) * 512]))
    gstate = {"dma": 0, "done": 0, "init": False, "tgt": 0.0}
    GT_XLN, GT_OWN_PROJ, GT_LKV, GT_OWN_ATTN, GT_B4 = 3.0, 2.0, 2.0, 8.0, 12.0

    def emit_gamma_dmas(upto):
        upto = min(upto, gstate["done"] + 2, GCH)
        while gstate["dma"] < upto:
            nch = gstate["dma"]
            if nch >= 2:
                R.wait("gpsimd", gred, nch - 1)
            gld.dma("gpsimd", nch % 2, gb[nch % 2][:], gsrc[nch][1])
            gstate["dma"] += 1

    def emit_gamma_chunks(upto):
        if not gstate["init"]:
            R.op("vector", "memset", S["gacc"][:], 0.0)
            gstate["init"] = True
        while gstate["done"] < min(upto, GCH):
            nch = gstate["done"]
            emit_gamma_dmas(nch + 2)
            wi = gsrc[nch][0]
            gld.wait("vector", nch % 2, nch // 2 + 1)
            R.op("vector", "tensor_reduce", out=S["gtmp"][:, wi:wi + 1],
                 in_=gb[nch % 2][:], axis=AX, op=ALU.add,
                 apply_absolute_value=True).inc(gbar, 1)
            R.wait("vector", gbar, nch + 1)
            R.op("vector", "tensor_tensor", out=S["gacc"][:, wi:wi + 1],
                 in0=S["gacc"][:, wi:wi + 1], in1=S["gtmp"][:, wi:wi + 1],
                 op=ALU.add).inc(gred, 1)
            gstate["done"] += 1
        emit_gamma_dmas(gstate["done"] + 2)

    def gtick(rate):
        gstate["tgt"] += rate
        emit_gamma_chunks(min(int(gstate["tgt"]), GCH))

    ldx = Stream(R, "ldx", 2)
    xc = [SCRF[:, 0:1024], SCRF[:, 1024:2048]]
    cast_done = R.sem("castdone")

    def emit_x_chunk(src_cols, guard_pe):
        base = cast_done.v
        for fc in range(FC):
            p = fc % 2
            if fc >= 2:
                R.wait("sync", cast_done, base + fc - 1)
            elif base > 0:
                R.wait("sync", cast_done, base)
            ldx.dma("sync", p, xc[p][:], src_cols(fc))
            ldx.wait("vector", p)
            if guard_pe and fc < 2:
                R.wait("vector", pe, guard_pe)
            R.op("vector", "tensor_copy", out=AR1[:, fc * NT:(fc + 1) * NT],
                 in_=xc[p][:]).inc(cast_done, 1)
            gtick(GT_XLN)

    wld = Stream(R, "wld", 2)
    wcast = R.sem("wcast")
    WG = [SCRF[:, 2048:3072], SCRF[:, 3072:4096]]
    sevq = R.sem("sevq")
    vev = R.sem("vev")
    trev = R.sem("trev")
    sevu = R.sem("sevu")
    uev = Stream(R, "uev", 2)
    vstg = [SCRB[:, 3072 + i * 512: 3584 + i * 512] for i in range(2)]
    ustg = [SCRB[:, 4096 + i * 512: 4608 + i * 512] for i in range(2)]
    PRM = {"g": 0, "vtr_pe": {}, "un": 0, "stg4": {}}

    def emit_proj_phase(granules, wsrc, src, ln_done, prev_guard_pe,
                        grate=0.0):
        ng = len(granules)
        gp_after = {}
        wcast_after = {}

        def wload(gq):
            oc = granules[gq][0]
            for half in range(2):
                gg = PRM["g"]
                PRM["g"] += 1
                bgi = gg % 4
                if gg >= 2:
                    R.wait("sync", wcast, gg - 1)
                wld.dma("sync", gg % 2, WG[gg % 2][:],
                        wsrc[:, oc * 2048 + half * 1024: oc * 2048 + (half + 1) * 1024])
                wld.wait("scalar", gg % 2)
                if gq >= 2:
                    R.wait("scalar", pe, gp_after[gq - 2])
                elif prev_guard_pe:
                    R.wait("scalar", pe, prev_guard_pe)
                R.op("scalar", "activation", out=WGB[bgi][:],
                     in_=WG[gg % 2][:], func=AF.Copy).inc(wcast, 1)
            wcast_after[gq] = wcast.v

        wload(0)
        for j in range(ng):
            if j + 1 < ng:
                wload(j + 1)
            R.wait("tensor", wcast, wcast_after[j])
            if j == 0 and ln_done is not None:
                R.wait("tensor", ln_done, FC)
            _, kind, jj, dst = granules[j]
            b_lo, b_hi = WGB[(2 * j) % 4], WGB[(2 * j + 1) % 4]
            pnames = ("b0", "b1") if j % 2 == 0 else ("b2", "b3")
            for tc_ in range(2):
                bn = pnames[tc_]
                BK[bn].acquire()
                for kc in range(16):
                    wb = b_lo if kc < 8 else b_hi
                    off = (kc % 8) * 128
                    o = R.op("tensor", "matmul", BK[bn].t[:, 0:512],
                             wb[:, off:off + 128],
                             src[:, kc * NT + tc_ * 512: kc * NT + tc_ * 512 + 512],
                             start=(kc == 0), stop=(kc == 15))
                o.inc(pe, 1)
                pe_now = pe.v
                if kind in ("q", "k"):
                    dslice = dst[:, jj * NT + tc_ * 512: jj * NT + tc_ * 512 + 512]
                    R.wait("scalar", pe, pe_now)
                    R.op("scalar", "activation", out=dslice,
                         in_=BK[bn].t[:, 0:512], func=AF.Silu).inc(sevq, 1)
                    BK[bn].release("scalar")
                elif kind == "v":
                    R.wait("vector", pe, pe_now)
                    key = tc_
                    if key in PRM["vtr_pe"]:
                        R.wait("vector", pe, PRM["vtr_pe"][key])
                    R.op("vector", "tensor_copy", out=vstg[tc_][:],
                         in_=BK[bn].t[:, 0:512]).inc(vev, 1)
                    BK[bn].release("vector")
                    BK["b6"].acquire()
                    R.wait("tensor", vev)
                    for i4 in range(4):
                        o = R.op("tensor", "matmul",
                                 BK["b6"].t[:, i4 * 128:(i4 + 1) * 128],
                                 vstg[tc_][:, i4 * 128:(i4 + 1) * 128],
                                 S["IDEN_BF"][:], start=True, stop=True)
                    o.inc(pe, 1)
                    PRM["vtr_pe"][key] = pe.v
                    R.wait("vector", pe, pe.v)
                    for i4 in range(4):
                        tch = tc_ * 4 + i4
                        R.op("vector", "tensor_copy",
                             out=dst[:, tch * DM + jj * 128: tch * DM + (jj + 1) * 128],
                             in_=BK["b6"].t[:, i4 * 128:(i4 + 1) * 128]).inc(trev, 1)
                    BK["b6"].release("vector")
                else:
                    un = PRM["un"]
                    PRM["un"] += 1
                    R.wait("vector", pe, pe_now)
                    if un >= 2:
                        uev.wait("vector", un % 2, un // 2)
                    R.op("vector", "tensor_copy", out=ustg[un % 2][:],
                         in_=BK[bn].t[:, 0:512]).inc(sevu, 1)
                    BK[bn].release("vector")
                    R.wait("scalar", sevu)
                    uev.dma("scalar", un % 2,
                            u_spill[:, jj * NT + tc_ * 512: jj * NT + tc_ * 512 + 512],
                            ustg[un % 2][:])
            gp_after[j] = pe.v
            if grate:
                gtick(grate)
        return gp_after, pe.v

    sktok = R.sem("sktok")
    skv = R.sem("skv")

    def emit_local_kv(k_evac_total, v_evac_total):
        R.wait("tensor", sevq, k_evac_total)
        R.wait("tensor", trev, v_evac_total)
        R.wait("vector", sder)
        uev.wait_all("vector")
        stg4 = [SCRB[:, 2048 + i * 512: 2560 + i * 512] for i in range(8)]
        trb = ("b6", "b0")
        C_tok = {}
        sk_after = {}

        def emit_tr(h):
            for half in range(2):
                slot = (h % 4) * 2 + half
                BK[trb[half]].acquire()
                for i4 in range(4):
                    tb = half * 4 + i4
                    o = R.op("tensor", "matmul",
                             BK[trb[half]].t[:, i4 * 128:(i4 + 1) * 128],
                             AR3[:, h * NT + tb * 128: h * NT + (tb + 1) * 128],
                             S["IDEN_BF"][:], start=True, stop=True)
                o.inc(pe, 1)
                R.wait("vector", pe, pe.v)
                g = PRM["stg4"].get(slot)
                if g:
                    R.wait("vector", pe, g)
                for i4 in range(4):
                    tb = half * 4 + i4
                    o = R.op("vector", "tensor_scalar",
                             out=stg4[slot][:, i4 * 128:(i4 + 1) * 128],
                             in0=BK[trb[half]].t[:, i4 * 128:(i4 + 1) * 128],
                             scalar1=S["kdpre_all"][:, h * 8 + tb: h * 8 + tb + 1],
                             scalar2=None, op0=ALU.mult)
                o.inc(sktok, 1)
                BK[trb[half]].release("vector")
            sk_after[h] = sktok.v

        emit_tr(0)
        emit_tr(1)
        emit_tr(2)
        for h in range(NH):
            if h + 3 < NH:
                emit_tr(h + 3)
            cb = BK["b7"] if h % 2 == 0 else BK["b1"]
            cb.acquire()
            R.wait("tensor", sktok, sk_after[h])
            for tb in range(8):
                slot = (h % 4) * 2 + tb // 4
                o = R.op("tensor", "matmul", cb.t[:, 0:128],
                         stg4[slot][:, (tb % 4) * 128:(tb % 4) * 128 + 128],
                         AR4[:, tb * DM + h * 128: tb * DM + (h + 1) * 128],
                         start=(tb == 0), stop=(tb == 7))
            o.inc(pe, 1)
            PRM["stg4"][(h % 4) * 2] = PRM["stg4"][(h % 4) * 2 + 1] = pe.v
            C_tok[h] = pe.v
            R.wait("vector", pe, C_tok[h])
            kvslice = KVOUT[:, h * 128:(h + 1) * 128]
            R.op("vector", "tensor_copy", out=kvslice,
                 in_=cb.t[:, 0:128]).inc(skv, 1)
            cb.release("vector")
            gtick(GT_LKV)
        return pe.v

    emit_x_chunk(lambda fc: xT[fc * 128:(fc + 1) * 128, :], None)
    R.phases.append(("own_ln", R.twc()))
    ln1 = emit_layernorm(R, S, BK, AR1, AR1, S["CROW"][0:1, 0:1],
                         cast_done, "1")
    gran = [(j, "q", j, AR2) for j in range(16)] + \
           [(16 + j, "k", j, AR3) for j in range(16)] + \
           [(48 + j, "u", j, None) for j in range(16)] + \
           [(32 + j, "v", j, AR4) for j in range(16)]
    R.phases.append(("own_proj", R.twc()))
    gp_own, own_tail = emit_proj_phase(gran, wqkvu, AR1, ln1, None,
                                       GT_OWN_PROJ)
    q_total, k_total, v_total = sevq.v, sevq.v, trev.v

    R.phases.append(("lkv", R.twc()))
    lkv_pe = emit_local_kv(sevq.v, trev.v)

    R.phases.append(("cc", R.twc()))
    lsp = R.sem("lsp")
    lod = R.sem("lod")
    ccs = R.sem("ccs")
    lin = R.sem("lin")
    cmb = R.sem("cmb")
    LBF = SCRB[:, 0:2048]
    LRK = [SCRB[:, 2048:4096], SCRB[:, 4096:6144]]
    R.wait("vector", skv)
    R.op("vector", "tensor_copy", out=LBF, in_=KVOUT[:]).inc(lsp, 1)
    R.op("vector", "memset", KVOUT[:], 0.0)
    R.wait("gpsimd", lsp, 1)
    R.dma("gpsimd", l_out_t.ap(), LBF, lod)
    R.wait("gpsimd", lod, 16)
    R.op("gpsimd", "collective_compute", "AllGather", ALU.bypass,
         replica_groups=[list(range(8))],
         ins=[l_out_t.ap().opt()], outs=[l_all_t.ap().opt()]).inc(ccs, 1)
    R.wait("gpsimd", ccs, 1)
    R.wait("gpsimd", pe, lkv_pe)
    l_all = l_all_t.ap()
    for r in range(8):
        if r >= 2:
            R.wait("gpsimd", cmb, r - 1)
        R.dma("gpsimd", LRK[r % 2][:], l_all[r * 128:(r + 1) * 128, :], lin)
    R.wait("vector", sder)
    for r in range(8):
        R.wait("vector", lin, 16 * (r + 1))
        for h in range(NH):
            o = R.op("vector", "scalar_tensor_tensor",
                     out=KVOUT[:, h * 128:(h + 1) * 128],
                     in0=LRK[r % 2][:, h * 128:(h + 1) * 128],
                     scalar=S["kvw"][:, r * NH + h: r * NH + h + 1],
                     in1=KVOUT[:, h * 128:(h + 1) * 128],
                     op0=ALU.mult, op1=ALU.add)
        o.inc(cmb, 1)

    qkmb = [SCRB[:, 3072 + i * 512: 3584 + i * 512] for i in range(2)]
    R.phases.append(("own_attn", R.twc()))
    maskb = [SCRB[:, 4096 + i * 512: 4608 + i * 512] for i in range(2)]
    qdb = [SCRB[:, 5120 + i * 256: 5376 + i * 256] for i in range(2)]
    ktokb = [SCRB[:, 5632 + i * 256: 5888 + i * 256] for i in range(2)]
    smask = R.sem("smask")
    sqs = R.sem("sqs")
    skvbf = R.sem("skvbf")
    sqkm = R.sem("sqkm")
    soev = R.sem("soev")
    uload = R.sem("uload")
    sq2 = R.sem("sq2")
    osq = [SCRB[:, 0:1024], SCRB[:, 1024:2048]]
    stats_pe = {}
    BK["b4"].acquire()
    BK["b5"].acquire()
    R.wait("vector", pe, own_tail)
    R.wait("vector", sder)
    R.wait("tensor", sevq, q_total)
    R.wait("tensor", trev, v_total)
    R.wait("vector", sevq, q_total)
    uev.wait_all("scalar")
    R.wait("scalar", cmb, 8)
    C_pe_after = {}
    diag_pe_after = {}
    for h in range(NH):
        hp = h % 2
        R.wait("scalar", sder)
        if h >= 2:
            R.wait("scalar", sqkm, 4 * (h - 1))
            R.wait("scalar", sqs, h - 1)
        R.op("scalar", "activation", out=maskb[hp][:, 0:256], in_=S["IDX0"][:],
             func=AF.Exp, scale=S["sneg"][:, h:h + 1])
        R.op("scalar", "activation", out=maskb[hp][:, 256:512], in_=S["IDX1"][:],
             func=AF.Exp, scale=S["sneg"][:, h:h + 1])
        R.op("scalar", "activation", out=qdb[hp][:], in_=S["QDROW"][:],
             func=AF.Exp, scale=S["sneg"][:, h:h + 1]).inc(smask, 1)
        R.wait("vector", smask, h + 1)
        qd = qdb[hp]
        qrep = bass.AP(qd.tensor, qd.offset, [[qd.ap[0][0], 128], [0, 4], [1, 256]])
        R.op("vector", "tensor_tensor",
             out=AR1[:, h * NT:(h + 1) * NT].rearrange("p (b m) -> p b m", b=4),
             in0=AR2[:, h * NT:(h + 1) * NT].rearrange("p (b m) -> p b m", b=4),
             in1=qrep, op=ALU.mult).inc(sqs, 1)
        kvslice = KVOUT[:, h * 128:(h + 1) * 128]
        R.op("vector", "tensor_copy", out=S["kvbf"][:], in_=kvslice)\
            .inc(skvbf, 1)
        for i in range(NBLK):
            blk = h * NBLK + i
            BK["b6"].acquire()
            o = R.op("tensor", "matmul", BK["b6"].t[:, 0:128],
                     AR3[:, h * NT + i * 256: h * NT + i * 256 + 128],
                     S["IDEN_BF"][:], start=True, stop=True)
            o = R.op("tensor", "matmul", BK["b6"].t[:, 128:256],
                     AR3[:, h * NT + i * 256 + 128: h * NT + i * 256 + 256],
                     S["IDEN_BF"][:], start=True, stop=True)
            o.inc(pe, 1)
            tr_pe = pe.v
            R.wait("vector", pe, tr_pe)
            prevu = blk - 1 if i > 0 else ((h - 2) * NBLK + 3 if h >= 2 else None)
            if prevu is not None and prevu in C_pe_after:
                R.wait("vector", pe, C_pe_after[prevu])
            R.op("vector", "tensor_scalar", out=ktokb[hp][:, 0:128],
                 in0=BK["b6"].t[:, 0:128],
                 scalar1=S["kdb_all"][:, 2 * h:2 * h + 1],
                 scalar2=None, op0=ALU.mult)
            R.op("vector", "tensor_scalar", out=ktokb[hp][:, 128:256],
                 in0=BK["b6"].t[:, 128:256],
                 scalar1=S["kdb_all"][:, 2 * h + 1:2 * h + 2],
                 scalar2=None, op0=ALU.mult).inc(sktok, 1)
            BK["b6"].release("vector")
            BK["b7"].acquire()
            R.wait("tensor", sktok)
            tch = 2 * i
            R.op("tensor", "matmul", BK["b7"].t[:, 0:128],
                 ktokb[hp][:, 0:128],
                 AR4[:, tch * DM + h * 128: tch * DM + (h + 1) * 128],
                 start=True, stop=False)
            R.op("tensor", "matmul", BK["b7"].t[:, 0:128],
                 ktokb[hp][:, 128:256],
                 AR4[:, (tch + 1) * DM + h * 128: (tch + 1) * DM + (h + 1) * 128],
                 start=False, stop=True).inc(pe, 1)
            C_pe_after[blk] = pe.v
            qkbn = "b0" if i % 2 == 0 else "b3"
            BK[qkbn].acquire()
            R.op("tensor", "matmul", BK[qkbn].t[:, 0:256],
                 AR3[:, h * NT + i * 256: h * NT + i * 256 + 128],
                 AR2[:, h * NT + i * 256: h * NT + (i + 1) * 256],
                 start=True, stop=True)
            R.op("tensor", "matmul", BK[qkbn].t[:, 256:512],
                 AR3[:, h * NT + i * 256 + 128: h * NT + i * 256 + 256],
                 AR2[:, h * NT + i * 256: h * NT + (i + 1) * 256],
                 start=True, stop=True).inc(pe, 1)
            qk_pe = pe.v
            R.wait("vector", pe, qk_pe)
            prevq = blk - 1 if i > 0 else ((h - 2) * NBLK + 3 if h >= 2 else None)
            if prevq is not None and prevq in diag_pe_after:
                R.wait("vector", pe, diag_pe_after[prevq])
            R.op("vector", "tensor_tensor", out=qkmb[hp][:, 0:256],
                 in0=BK[qkbn].t[:, 0:256], in1=maskb[hp][:, 0:256], op=ALU.mult)
            R.op("vector", "tensor_tensor", out=qkmb[hp][:, 256:512],
                 in0=BK[qkbn].t[:, 256:512], in1=maskb[hp][:, 256:512],
                 op=ALU.mult).inc(sqkm, 1)
            BK[qkbn].release("vector")
            ob = "b1" if i % 2 == 0 else "b2"
            BK[ob].acquire()
            R.wait("tensor", skvbf, 4 * h + i + 1)
            R.wait("tensor", sqs, h + 1)
            R.op("tensor", "matmul", BK[ob].t[:, 0:256], S["kvbf"][:],
                 AR1[:, h * NT + i * 256: h * NT + (i + 1) * 256],
                 start=True, stop=False)
            R.wait("tensor", sqkm, blk + 1)
            assert skvbf.v == 4 * h + i + 1
            R.op("tensor", "matmul", BK[ob].t[:, 0:256],
                 AR4[:, tch * DM + h * 128: tch * DM + (h + 1) * 128],
                 qkmb[hp][:, 0:256], start=False, stop=False)
            R.op("tensor", "matmul", BK[ob].t[:, 0:256],
                 AR4[:, (tch + 1) * DM + h * 128: (tch + 1) * DM + (h + 1) * 128],
                 qkmb[hp][:, 256:512], start=False, stop=True).inc(pe, 1)
            diag_pe_after[blk] = pe.v
            R.wait("vector", pe, pe.v)
            R.op("vector", "scalar_tensor_tensor", out=kvslice, in0=kvslice,
                 scalar=S["bd_bc"][:, h:h + 1], in1=BK["b7"].t[:, 0:128],
                 op0=ALU.mult, op1=ALU.add).inc(skv, 1)
            BK["b7"].release("vector")
            if i < NBLK - 1:
                R.op("vector", "tensor_copy", out=S["kvbf"][:], in_=kvslice)\
                    .inc(skvbf, 1)
            R.op("vector", "tensor_copy",
                 out=AR2[:, h * NT + i * 256: h * NT + (i + 1) * 256],
                 in_=BK[ob].t[:, 0:256]).inc(soev, 1)
            BK[ob].release("vector")
            if i == NBLK - 1:
                R.wait("scalar", soev)
                if h >= 2:
                    R.wait("scalar", pe, stats_pe[h - 2])
                R.op("scalar", "activation", out=osq[h % 2][:],
                     in_=AR2[:, h * NT:(h + 1) * NT], func=AF.Square)\
                    .inc(sq2, 1)
                if h >= 1:
                    hh = h - 1
                    R.wait("tensor", sq2, hh + 1)
                    for half in range(2):
                        o = R.op("tensor", "matmul",
                                 (BK["b4"] if half == 0 else BK["b5"]).t[0:1, 0:512],
                                 S["ONESC_BF"][:],
                                 osq[hh % 2][:, half * 512: half * 512 + 512],
                                 start=(hh == 0), stop=False)
                    o.inc(pe, 1)
                    stats_pe[hh] = pe.v
                if h == 7:
                    uev.wait_all("sync")
                    R.wait("sync", pe, diag_pe_after[31])
                    for fc in range(0, FC, 2):
                        R.dma("sync", AR4[:, fc * NT:(fc + 1) * NT],
                              u_spill[:, fc * NT:(fc + 1) * NT], uload)
                if h == 15:
                    R.wait("sync", pe, diag_pe_after[63])
                    for fc in range(1, FC, 2):
                        R.dma("sync", AR4[:, fc * NT:(fc + 1) * NT],
                              u_spill[:, fc * NT:(fc + 1) * NT], uload)
                gtick(GT_OWN_ATTN)

    R.phases.append(("B3", R.twc()))
    R.wait("tensor", sq2, 16)
    for half in range(2):
        o = R.op("tensor", "matmul",
                 (BK["b4"] if half == 0 else BK["b5"]).t[0:1, 0:512],
                 S["ONESC_BF"][:], osq[15 % 2][:, half * 512: half * 512 + 512],
                 start=False, stop=True)
    o.inc(pe, 1)
    stats_pe[15] = pe.v
    R.wait("scalar", pe, stats_pe[15])
    R.op("scalar", "activation", out=S["rowA2"][0:1, 0:512],
         in_=BK["b4"].t[0:1, 0:512], func=AF.Sqrt,
         scale=S["CROW"][0:1, 3:4], bias=S["CROW"][0:1, 1:2])
    R.op("scalar", "activation", out=S["rowA2"][0:1, 512:1024],
         in_=BK["b5"].t[0:1, 0:512], func=AF.Sqrt,
         scale=S["CROW"][0:1, 3:4], bias=S["CROW"][0:1, 1:2]).inc(act, 1)
    BK["b4"].release("scalar")
    BK["b5"].release("scalar")
    R.wait("vector", act)
    R.op("vector", "reciprocal", out=S["rowA"][0:1, 0:1024],
         in_=S["rowA2"][0:1, 0:1024]).inc(dve, 1)
    BK["b4"].acquire()
    BK["b5"].acquire()
    R.wait("tensor", dve)
    R.op("tensor", "matmul", BK["b4"].t[:, 0:512], S["ONESR_F"][:],
         S["rowA"][0:1, 0:512], start=True, stop=True)
    R.op("tensor", "matmul", BK["b5"].t[:, 0:512], S["ONESR_F"][:],
         S["rowA"][0:1, 512:1024], start=True, stop=True).inc(pe, 1)
    rstdb = SCRB[:, 2048:3072]
    R.wait("vector", pe)
    R.op("vector", "tensor_copy", out=rstdb[:, 0:512], in_=BK["b4"].t[:, 0:512])
    R.op("vector", "tensor_copy", out=rstdb[:, 512:1024],
         in_=BK["b5"].t[:, 0:512]).inc(dve, 1)
    BK["b4"].release("vector")
    BK["b5"].release("vector")


    gdone = R.sem("gdone")
    ttmp = SCRB[:, 3072:4096]
    R.wait("vector", uload, 16 * 16)
    for fc in range(FC):
        R.op("vector", "tensor_mul", out=ttmp[:],
             in0=AR2[:, fc * NT:(fc + 1) * NT], in1=rstdb[:])
        R.op("vector", "tensor_mul", out=AR3[:, fc * NT:(fc + 1) * NT],
             in0=ttmp[:], in1=AR4[:, fc * NT:(fc + 1) * NT]).inc(gdone, 1)

    R.phases.append(("B4", R.twc()))
    xch = [SCRF[:, 0:1024], SCRF[:, 1024:2048]]
    x1done = R.sem("x1done")
    gp_B4 = {}
    b4_wc = {}

    def emit_wload4(jq):
        p = jq % 2
        if jq >= 2:
            R.wait("sync", x1done, 2 * (jq - 1))
        ldx.dma("sync", p, xch[p][:], xT[jq * 128:(jq + 1) * 128, :])
        for half in range(2):
            gg = PRM["g"]
            PRM["g"] += 1
            if gg >= 2:
                R.wait("sync", wcast, gg - 1)
            wld.dma("sync", gg % 2, WG[gg % 2][:],
                    wout[:, jq * 2048 + half * 1024: jq * 2048 + (half + 1) * 1024])
            wld.wait("scalar", gg % 2)
            if jq >= 2:
                R.wait("scalar", pe, gp_B4[jq - 2])
            else:
                R.wait("scalar", pe, own_tail)
            R.op("scalar", "activation", out=WGB[gg % 4][:],
                 in_=WG[gg % 2][:], func=AF.Copy).inc(wcast, 1)
        b4_wc[jq] = wcast.v

    R.wait("sync", cast_done)
    emit_wload4(0)
    for j in range(16):
        p = j % 2
        if j + 1 < 16:
            emit_wload4(j + 1)
        R.wait("tensor", wcast, b4_wc[j])
        if j == 0:
            R.wait("tensor", gdone, FC)
        b_lo, b_hi = WGB[(2 * j) % 4], WGB[(2 * j + 1) % 4]
        pnames = ("b0", "b1") if j % 2 == 0 else ("b2", "b3")
        for tc_ in range(2):
            bn = pnames[tc_]
            BK[bn].acquire()
            for kc in range(16):
                wb = b_lo if kc < 8 else b_hi
                off = (kc % 8) * 128
                o = R.op("tensor", "matmul", BK[bn].t[:, 0:512],
                         wb[:, off:off + 128],
                         AR3[:, kc * NT + tc_ * 512: kc * NT + tc_ * 512 + 512],
                         start=(kc == 0), stop=(kc == 15))
            o.inc(pe, 1)
            R.wait("vector", pe)
            ldx.wait("vector", p)
            R.op("vector", "tensor_tensor",
                 out=AR2[:, j * NT + tc_ * 512: j * NT + tc_ * 512 + 512],
                 in0=BK[bn].t[:, 0:512],
                 in1=xch[p][:, tc_ * 512: tc_ * 512 + 512],
                 op=ALU.add).inc(x1done, 1)
            BK[bn].release("vector")
        gp_B4[j] = pe.v
        gtick(GT_B4)

    R.phases.append(("B5_ln2", R.twc()))
    ln2_done = emit_layernorm(R, S, BK, AR2, AR1, S["CROW"][0:1, 0:1],
                              x1done, "2")
    spill = R.sem("spill")
    R.wait("scalar", ln2_done)
    R.dma("scalar", x1_spill, AR2[:], spill)

    R.phases.append(("thresh", R.twc()))
    emit_gamma_chunks(GCH)
    BK["b7"].acquire()
    R.wait("tensor", gred, GCH)
    R.op("tensor", "matmul", BK["b7"].t[0:1, 0:4], S["ONESC_F"][:],
         S["gacc"][:], start=True, stop=True).inc(pe, 1)
    R.wait("vector", pe)
    R.op("vector", "tensor_copy", out=S["gsum"][:], in_=BK["b7"].t[0:1, 0:4])\
        .inc(dve, 1)
    BK["b7"].release("vector")
    R.op("vector", "tensor_scalar", out=S["th"][:], in0=S["gsum"][:],
         scalar1=1.0 / NW_ELEMS, scalar2=1e-8, op0=ALU.mult, op1=ALU.add)
    R.op("vector", "tensor_scalar", out=S["th"][:], in0=S["th"][:],
         scalar1=0.5, scalar2=None, op0=ALU.mult)
    R.op("vector", "tensor_scalar", out=S["nth"][:], in0=S["th"][:],
         scalar1=-1.0, scalar2=None, op0=ALU.mult).inc(dve, 1)
    BK["b5"].acquire()
    R.wait("tensor", dve)
    R.op("tensor", "matmul", BK["b5"].t[:, 0:4], S["ONESR_F"][:], S["th"][:],
         start=True, stop=True)
    R.op("tensor", "matmul", BK["b5"].t[:, 4:8], S["ONESR_F"][:], S["nth"][:],
         start=True, stop=True).inc(pe, 1)
    R.wait("vector", pe)
    R.op("vector", "tensor_copy", out=S["thb"][:], in_=BK["b5"].t[:, 0:4])
    R.op("vector", "tensor_copy", out=S["nthb"][:], in_=BK["b5"].t[:, 4:8])\
        .inc(dve, 1)
    BK["b5"].release("vector")
    th_ready = dve.v

    def hidden_ap(kc, lo, width):
        if kc < 16:
            return AR2[:, kc * NT + lo: kc * NT + lo + width]
        if kc < 32:
            return AR4[:, (kc - 16) * NT + lo: (kc - 16) * NT + lo + width]
        return AR3[:, (kc - 32) * NT + lo: (kc - 32) * NT + lo + width]

    R.phases.append(("B7", R.twc()))
    f32bufs = {"w1": [SCRF[:, 0:1024], SCRF[:, 1024:2048]],
               "w2": [SCRF[:, 2048:3072], SCRF[:, 3072:4096]]}
    wqbufs = {"w1": [[WQ[:, 0:1024], WQ[:, 1024:2048]],
                     [WQ[:, 4096:5120], WQ[:, 5120:6144]]],
              "w2": [[WQ[:, 2048:3072], WQ[:, 3072:4096]],
                     [WQ[:, 6144:7168], WQ[:, 7168:8192]]]}
    tsD = SCRB[:, 0:2048]
    tsP = SCRB[:, 2048:4096]
    sgb = [SCRB[:, 4096 + i * 512: 4608 + i * 512] for i in range(4)]
    fld = {"w1": Stream(R, "fld1", 2), "w2": Stream(R, "fld2", 2)}
    qbar = R.sem("qbarD")
    qbar_n = [0]
    fq = {"w1": R.sem("fq1"), "w2": R.sem("fq2")}
    sgd = R.sem("sgd")
    hid = R.sem("hid")
    wsrc = {"w1": w1t, "w2": w2t}
    thcol = {"w1": 0, "w2": 1, "w3": 2}
    qtok = {}
    R.wait("sync", spill, 16)
    R.wait("sync", ln2_done)
    R.wait("vector", spill, 16)
    R.wait("gpsimd", spill, 16)
    R.wait("vector", gdone)
    uev.wait_all("scalar")
    mm_pe_after = {}

    def emit_quant7(jq):
        for wname in ("w1", "w2"):
            ts = tsD if wname == "w1" else tsP
            for half in range(2):
                if (wname, half) in qtok:
                    R.wait("sync", qbar, qtok[(wname, half)])
                fld[wname].dma("sync", half, f32bufs[wname][half][:],
                               wsrc[wname][:, jq * 2048 + half * 1024: jq * 2048 + (half + 1) * 1024])
                fld[wname].wait("vector", half)
                R.wait("vector", dve, th_ready)
                if 2 * jq + half >= 1:
                    R.wait("vector", fq[wname], 2 * jq + half)
                col = thcol[wname]
                src = f32bufs[wname][half][:]
                dst = wqbufs[wname][jq % 2][half][:]
                R.op("vector", "tensor_scalar", out=ts[:, 0:1024], in0=src,
                     scalar1=S["thb"][:, col:col + 1], scalar2=None,
                     op0=ALU.is_ge).inc(qbar, 1)
                R.op("vector", "tensor_scalar", out=ts[:, 1024:2048], in0=src,
                     scalar1=S["nthb"][:, col:col + 1], scalar2=None,
                     op0=ALU.is_le).inc(qbar, 1)
                qbar_n[0] += 2
                qtok[(wname, half)] = qbar_n[0]
                R.wait("gpsimd", qbar, qbar_n[0])
                if jq >= 2 and (jq - 2, wname) in mm_pe_after:
                    R.wait("gpsimd", pe, mm_pe_after[(jq - 2, wname)])
                R.op("gpsimd", "tensor_tensor", out=dst, in0=ts[:, 0:1024],
                     in1=ts[:, 1024:2048], op=ALU.subtract).inc(fq[wname], 1)

    w2banks = [("b4", "b5"), ("b6", "b7")]
    emit_quant7(0)
    for j in range(44):
        if j + 1 < 44:
            emit_quant7(j + 1)
        R.wait("tensor", fq["w1"], 2 * (j + 1))
        R.wait("tensor", fq["w2"], 2 * (j + 1))
        x1banks = ("b0", "b1") if j % 2 == 0 else ("b2", "b3")
        for tc_ in range(2):
            bn = x1banks[tc_]
            BK[bn].acquire()
            for kc in range(16):
                wb = wqbufs["w1"][j % 2][kc // 8]
                off = (kc % 8) * 128
                o = R.op("tensor", "matmul", BK[bn].t[:, 0:512],
                         wb[:, off:off + 128],
                         AR1[:, kc * NT + tc_ * 512: kc * NT + tc_ * 512 + 512],
                         start=(kc == 0), stop=(kc == 15))
            o.inc(pe, 1)
            if tc_ == 1:
                mm_pe_after[(j, "w1")] = pe.v
            bn2 = w2banks[j % 2][tc_]
            BK[bn2].acquire()
            for kc in range(16):
                wb = wqbufs["w2"][j % 2][kc // 8]
                off = (kc % 8) * 128
                o = R.op("tensor", "matmul", BK[bn2].t[:, 0:512],
                         wb[:, off:off + 128],
                         AR1[:, kc * NT + tc_ * 512: kc * NT + tc_ * 512 + 512],
                         start=(kc == 0), stop=(kc == 15))
            o.inc(pe, 1)
            if tc_ == 1:
                mm_pe_after[(j, "w2")] = pe.v
            sg = sgb[(j % 2) * 2 + tc_]
            R.wait("scalar", pe)
            if j >= 2:
                R.wait("scalar", hid, 2 * (j - 2) + tc_ + 1)
            R.op("scalar", "activation", out=sg[:], in_=BK[bn].t[:, 0:512],
                 func=AF.Sigmoid).inc(sgd, 1)
            BK[bn].release("scalar")
            R.wait("vector", sgd)
            R.op("vector", "tensor_tensor", out=hidden_ap(j, tc_ * 512, 512),
                 in0=BK[bn2].t[:, 0:512], in1=sg[:], op=ALU.mult)\
                .inc(hid, 1)
            BK[bn2].release("vector")

    R.phases.append(("B8", R.twc()))
    w3f = [SCRF[:, 0:1408], SCRF[:, 1408:2816]]
    OSTG = [SCRF[:, 2816:3328], SCRF[:, 3328:3840]]
    w3q = [[AR1[:, par * 5632 + i * 1408: par * 5632 + (i + 1) * 1408]
            for i in range(4)] for par in range(2)]
    t3a = [WQ[:, 0:1408], WQ[:, 2816:4224]]
    t3b = [WQ[:, 1408:2816], WQ[:, 4224:5632]]
    x1b = [SCRB[:, 0:1024], SCRB[:, 1024:2048]]
    f3ld = Stream(R, "f3ld", 2)
    f3q = R.sem("f3q")
    oev = R.sem("oev")
    ost = Stream(R, "ost", 2)
    R.wait("sync", fq["w1"], 88)
    R.wait("sync", fq["w2"], 88)
    R.wait("vector", sgd, 88)
    R.wait("vector", hid, 88)
    R.wait("gpsimd", pe)
    qtok3 = {}
    w3_pe_after = {}

    def emit_quant8(jq):
        p = jq % 2
        if jq >= 2:
            R.wait("sync", oev, (jq - 1) * 2)
        ldx.dma("sync", p, x1b[p][:], x1_spill[:, jq * NT:(jq + 1) * NT])
        for part in range(4):
            g = jq * 4 + part
            if g >= 2 and (g - 2) in qtok3:
                R.wait("sync", qbar, qtok3[g - 2])
            f3ld.dma("sync", part % 2, w3f[part % 2][:],
                     w3t[:, jq * 5632 + part * 1408: jq * 5632 + (part + 1) * 1408])
            f3ld.wait("vector", part % 2)
            R.wait("vector", dve, th_ready)
            col = thcol["w3"]
            if g >= 2:
                R.wait("vector", f3q, g - 1)
            gp2 = g % 2
            R.op("vector", "tensor_scalar", out=t3a[gp2][:], in0=w3f[part % 2][:],
                 scalar1=S["thb"][:, col:col + 1], scalar2=None, op0=ALU.is_ge)\
                .inc(qbar, 1)
            R.op("vector", "tensor_scalar", out=t3b[gp2][:], in0=w3f[part % 2][:],
                 scalar1=S["nthb"][:, col:col + 1], scalar2=None, op0=ALU.is_le)\
                .inc(qbar, 1)
            qbar_n[0] += 2
            qtok3[g] = qbar_n[0]
            R.wait("gpsimd", qbar, qbar_n[0])
            if jq >= 2 and (jq - 2) in w3_pe_after:
                R.wait("gpsimd", pe, w3_pe_after[jq - 2])
            R.op("gpsimd", "tensor_tensor", out=w3q[jq % 2][part][:],
                 in0=t3a[gp2][:], in1=t3b[gp2][:], op=ALU.subtract).inc(f3q, 1)

    emit_quant8(0)
    for j in range(16):
        p = j % 2
        if j + 1 < 16:
            emit_quant8(j + 1)
        R.wait("tensor", f3q, (j + 1) * 4)
        if j == 0:
            R.wait("tensor", hid, 88)
        pn3 = ("b0", "b1") if j % 2 == 0 else ("b2", "b3")
        for tc_ in range(2):
            bn = pn3[tc_]
            BK[bn].acquire()
            for kc in range(44):
                part = kc // 11
                off = (kc % 11) * 128
                o = R.op("tensor", "matmul", BK[bn].t[:, 0:512],
                         w3q[j % 2][part][:, off:off + 128],
                         hidden_ap(kc, tc_ * 512, 512),
                         start=(kc == 0), stop=(kc == 43))
            o.inc(pe, 1)
            R.wait("vector", pe)
            ldx.wait("vector", p)
            if j >= 1:
                ost.wait("vector", tc_, j)
            R.op("vector", "tensor_tensor", out=OSTG[tc_][:],
                 in0=BK[bn].t[:, 0:512],
                 in1=x1b[p][:, tc_ * 512: tc_ * 512 + 512],
                 op=ALU.add).inc(oev, 1)
            BK[bn].release("vector")
            R.wait("scalar", oev)
            ost.dma("scalar", tc_,
                    out_c[j * 128:(j + 1) * 128, tc_ * 512:(tc_ + 1) * 512],
                    OSTG[tc_][:])
        w3_pe_after[j] = pe.v
    ost.wait_all("scalar")
    R.phases.append(("end", R.twc()))
    global _LAST_PHASES
    _LAST_PHASES = R.phases
    R.replay()
    return nc



_CACHE = {}
_LAST_PHASES = None


def _tile_wT(w, oc):
    t = w.reshape(oc, 128, 16, 128).transpose(3, 0, 2, 1)
    return np.ascontiguousarray(t.reshape(128, oc * 16 * 128))


def _tile_w3T(w3):
    t = w3.reshape(16, 128, 44, 128).transpose(3, 0, 2, 1)
    return np.ascontiguousarray(t.reshape(128, 16 * 44 * 128))


def kernel(L0_attn_x, L0_attn_slope_rate, L0_attn_W_qkvu, L0_attn_W_out,
           L0_ffn_w1, L0_ffn_w2, L0_ffn_w3):
    x = np.asarray(L0_attn_x, dtype=np.float32).reshape(8192, DM)
    slope = np.ascontiguousarray(
        np.asarray(L0_attn_slope_rate, dtype=np.float32).reshape(1, NH))
    Wq = np.asarray(L0_attn_W_qkvu, dtype=np.float32)
    Wo = np.asarray(L0_attn_W_out, dtype=np.float32)
    w1 = np.asarray(L0_ffn_w1, dtype=np.float32)
    w2 = np.asarray(L0_ffn_w2, dtype=np.float32)
    w3 = np.asarray(L0_ffn_w3, dtype=np.float32)

    xT = np.ascontiguousarray(x.T)
    wqkvu_t = _tile_wT(Wq, 64)
    wout_t = _tile_wT(Wo, 16)
    w1_t = _tile_wT(w1, 44)
    w2_t = _tile_wT(w2, 44)
    w3_t = _tile_w3T(w3)

    consts = make_consts()
    if "F" not in _CACHE:
        _CACHE["F"] = build_prog_fused(consts)
    ncF = _CACHE["F"]

    in_F = []
    for c in range(8):
        dist = np.full((128, 8), 1e9, dtype=np.float32)
        for j in range(c):
            dist[:, j] = 1024.0 * (c - 1 - j)
        in_F.append({
            "xT": np.ascontiguousarray(xT[:, c * NT:(c + 1) * NT]),
            "distc": dist,
            "slope": slope,
            "wqkvu": wqkvu_t,
            "wout": wout_t,
            "w1t": w1_t,
            "w2t": w2_t,
            "w3t": w3_t,
        })
    res = run_bass_kernel_spmd(ncF, in_F, core_ids=list(range(8)))

    out = np.concatenate([res.results[c]["out_c"] for c in range(8)], axis=1)
    return np.ascontiguousarray(out.T).reshape(1, 8192, DM).astype(np.float32)


if __name__ == "__main__":
    ncF = build_prog_fused(make_consts())
    print("build ok; sems:", ncF and "n/a")


# version 14
# speedup vs baseline: 1.4988x; 1.1588x over previous
import numpy as np
import ml_dtypes
import concourse.bass as bass
import concourse.mybir as mybir
from concourse.bass_utils import run_bass_kernel_spmd

F32 = mybir.dt.float32
BF16 = mybir.dt.bfloat16
AF = mybir.ActivationFunctionType
ALU = mybir.AluOpType
BF = ml_dtypes.bfloat16
AX = mybir.AxisListType.X

NT = 1024
DM = 2048
NH = 16
HD = 5632
FC = DM // 128
TC = NT // 128
NBLK = 4
HC = HD // 128
NW_ELEMS = float(HD * DM)
GCH = 528

ENGINES = ("sync", "scalar", "vector", "tensor", "gpsimd")


class Sem:
    __slots__ = ("h", "v", "name")

    def __init__(self, h, name):
        self.h, self.v, self.name = h, 0, name


class Op:
    __slots__ = ("meth", "args", "kwargs", "incs")

    def __init__(self, meth, args, kwargs):
        self.meth, self.args, self.kwargs, self.incs = meth, args, kwargs, []

    def inc(self, sem, n):
        assert not self.incs, f"op {self.meth} already has a sync update"
        self.incs.append((sem, n, sem.v + n))
        sem.v += n
        return self

    def tok(self):
        if self.incs:
            s, n, va = self.incs[0]
            return (s, va)
        return None


class Rec:
    def __init__(self, nc):
        self.nc = nc
        self.q = {e: [] for e in ENGINES}
        self._n = 0
        self._lastwait = {}

    def sem(self, name=None):
        self._n += 1
        nm = name or f"s{self._n}"
        return Sem(self.nc.alloc_semaphore(nm), nm)

    def op(self, eng, meth, *args, **kwargs):
        o = Op(meth, args, kwargs)
        self.q[eng].append(o)
        return o

    def wait(self, eng, sem, val=None):
        v = sem.v if val is None else val
        if v <= 0:
            return
        key = (eng, sem.name)
        if self._lastwait.get(key, -1) >= v:
            return
        self._lastwait[key] = v
        self.q[eng].append(Op("wait_ge", (sem.h, v), {}))

    def dma(self, eng, out, in_, sem, **kw):
        return self.op(eng, "dma_start", out=out, in_=in_, **kw).inc(sem, 16)

    @staticmethod
    def _interval(ap):
        ext = 1
        for (step, count) in ap.ap:
            ext += abs(int(step)) * (int(count) - 1)
        off = int(ap.offset)
        return (ap.tensor.name, off, off + ext)

    def _rw_of(self, o):
        if o.meth in ("wait_ge", "nop"):
            return [], []
        writes, reads = [], []
        args = list(o.args)
        kw = dict(o.kwargs)
        if o.meth == "memset":
            writes.append(self._interval(args[0]))
            return writes, reads
        if "out" in kw:
            writes.append(self._interval(kw.pop("out")))
        for v in list(args) + list(kw.values()):
            if isinstance(v, bass.AP):
                reads.append(self._interval(v))
        return writes, reads

    def _auto_chain(self):
        def overlap(a, b):
            return a[0] == b[0] and a[1] < b[2] and b[1] < a[2]

        for eng in ("vector", "scalar", "gpsimd"):
            ops = self.q[eng]
            chain = self.sem(f"chain_{eng}")
            writers = []
            readers = []
            n = len(ops)
            deps = [None] * n
            is_dma = [o.meth == "dma_start" for o in ops]
            for i, o in enumerate(ops):
                ws, rs = self._rw_of(o)
                need = set()
                for r in rs:
                    for (w, j) in writers:
                        if (not is_dma[j]) and overlap(r, w):
                            need.add(j)
                for w in ws:
                    for (pw, j) in writers:
                        if (not is_dma[j]) and overlap(w, pw):
                            need.add(j)
                    for (pr, j) in readers:
                        if (not is_dma[j]) and overlap(w, pr):
                            need.add(j)
                deps[i] = need
                for w in ws:
                    writers.append((w, i))
                for r in rs:
                    readers.append((r, i))
            producers = set()
            for need in deps:
                producers |= need
            for j in sorted(producers):
                if not ops[j].incs:
                    ops[j].inc(chain, 1)
            new_ops = []
            lastw = {}
            for i, o in enumerate(ops):
                if deps[i]:
                    for j in sorted(deps[i]):
                        t = ops[j].tok()
                        s, v = t
                        if lastw.get(s.name, -1) < v:
                            new_ops.append(Op("wait_ge", (s.h, v), {}))
                            lastw[s.name] = v
                new_ops.append(o)
            self.q[eng] = new_ops

    def replay(self):
        self._auto_chain()
        with self.nc.Block() as block:
            for eng in ENGINES:
                ops = self.q[eng]

                def runner(engine, ops=ops):
                    for o in ops:
                        r = getattr(engine, o.meth)(*o.args, **o.kwargs)
                        if o.incs:
                            s, nn_, _ = o.incs[0]
                            r.then_inc(s.h, nn_)

                getattr(block, eng)(runner)


class Bank:
    def __init__(self, R, t, name):
        self.R, self.t, self.name = R, t, name
        self.sem = R.sem(f"bk_{name}")
        self.tok = None

    def acquire(self, eng="tensor"):
        if self.tok is not None:
            self.R.wait(eng, self.tok[0], self.tok[1])

    def release(self, eng):
        o = self.R.q[eng][-1]
        if not o.incs:
            o.inc(self.sem, 1)
        self.tok = o.tok()


class Stream:
    def __init__(self, R, name, nslots):
        self.R = R
        self.sems = [R.sem(f"{name}_{i}") for i in range(nslots)]
        self.uses = [0] * nslots

    def dma(self, eng, slot, out, in_, **kw):
        self.uses[slot] += 1
        return self.R.op(eng, "dma_start", out=out, in_=in_, **kw)\
            .inc(self.sems[slot], 16)

    def wait(self, eng, slot, use=None):
        u = self.uses[slot] if use is None else use
        self.R.wait(eng, self.sems[slot], 16 * u)

    def wait_all(self, eng):
        for i, s in enumerate(self.sems):
            self.R.wait(eng, s, 16 * self.uses[i])


BF16_CONSTS = {"IDX0", "IDX1", "QDROW", "IDEN_BF", "ONESC_BF", "KD2"}


def make_consts():
    c = {}
    idx = []
    for k in range(2):
        s = 128 * k + np.arange(128)[:, None]
        m = np.arange(256)[None, :]
        d = (m - s).astype(np.float32)
        d[m < s] = 1e9
        idx.append(np.ascontiguousarray(d))
    c["IDX0"], c["IDX1"] = idx
    c["QDROW"] = np.broadcast_to(
        (np.arange(256, dtype=np.float32) + 1.0)[None, :], (128, 256)).copy()
    c["IDEN_BF"] = np.eye(128, dtype=np.float32)
    c["ONESC_BF"] = np.ones((128, 1), dtype=np.float32)
    kd = np.zeros((128, 2), dtype=np.float32)
    kd[:, 0] = 255.0 - np.arange(128)
    kd[:, 1] = 255.0 - (128 + np.arange(128))
    c["KD2"] = kd
    kp = np.zeros((128, 8), dtype=np.float32)
    for tb in range(8):
        kp[:, tb] = 1023.0 - (tb * 128 + np.arange(128))
    c["KDPRE"] = kp
    c["ONESR_F"] = np.ones((1, 128), dtype=np.float32)
    c["ONESC_F"] = np.ones((128, 1), dtype=np.float32)
    cc = np.zeros((128, 8), dtype=np.float32)
    cc[:, 0] = 1e-5
    cc[:, 1] = 1e-6
    cc[:, 2] = 1e-8
    cc[:, 3] = 1.0 / 2048.0
    cc[:, 4] = 256.0
    cc[:, 5] = 1024.0
    c["CCOL"] = cc
    c["CROW"] = np.ascontiguousarray(cc[0:1, :])
    return c


def load_consts(nc, R, names, consts, sem):
    t = {}
    buf = {}
    for k in names:
        arr = consts[k]
        if k in BF16_CONSTS:
            arr = arr.astype(BF)
            buf[k] = nc.alloc_sbuf_tensor(f"cb_{k}", list(arr.shape), BF16)
        else:
            buf[k] = nc.alloc_sbuf_tensor(f"cb_{k}", list(arr.shape), F32)
        t[k] = nc.inline_tensor(arr, name=f"c_{k}")
        R.dma("sync", buf[k][:], t[k].ap(), sem)
    return buf


def emit_layernorm(R, S, BK, src, dst, eps_row, src_sem, tag):
    pe, dve, act = R.s["pe"], R.s["dve"], R.s["act"]
    xsq = [S["scrB"][:, 0:1024], S["scrB"][:, 1024:2048]]
    ssum = [BK["b2"].t[0:1, 0:512], BK["b3"].t[0:1, 0:512]]
    ssq = [BK["b4"].t[0:1, 0:512], BK["b5"].t[0:1, 0:512]]
    for b in ("b2", "b3", "b4", "b5"):
        BK[b].acquire()
    R.wait("scalar", src_sem)
    R.wait("tensor", src_sem)
    sq_done = R.sem(f"lnsq{tag}")
    for fc in range(FC):
        srcc = src[:, fc * NT:(fc + 1) * NT]
        p = fc % 2
        if fc >= 2:
            R.wait("scalar", pe)
        R.op("scalar", "activation", out=xsq[p][:], in_=srcc, func=AF.Square)\
            .inc(sq_done, 1)
        R.wait("tensor", sq_done)
        for half in range(2):
            rsrc = src[:, fc * NT + half * 512: fc * NT + half * 512 + 512]
            R.op("tensor", "matmul", ssum[half], S["ONESC_BF"][:], rsrc,
                 start=(fc == 0), stop=(fc == FC - 1))
            o = R.op("tensor", "matmul", ssq[half], S["ONESC_BF"][:],
                     xsq[p][:, half * 512: half * 512 + 512],
                     start=(fc == 0), stop=(fc == FC - 1))
        o.inc(pe, 1)
    mean, ex2, tmpr = S["rowA"], S["rowB"], S["rowA2"]
    rstd = ex2
    R.wait("vector", pe)
    for half in range(2):
        sl = slice(half * 512, half * 512 + 512)
        R.op("vector", "tensor_scalar", out=mean[0:1, sl], in0=ssum[half],
             scalar1=1.0 / DM, scalar2=None, op0=ALU.mult)
        R.op("vector", "tensor_scalar", out=ex2[0:1, sl], in0=ssq[half],
             scalar1=1.0 / DM, scalar2=None, op0=ALU.mult)
    o = R.op("vector", "tensor_mul", out=tmpr[0:1, 0:1024],
             in0=mean[0:1, 0:1024], in1=mean[0:1, 0:1024])
    o = R.op("vector", "tensor_sub", out=ex2[0:1, 0:1024],
             in0=ex2[0:1, 0:1024], in1=tmpr[0:1, 0:1024])
    o.inc(dve, 1)
    for b in ("b2", "b3", "b4", "b5"):
        BK[b].release("vector")
    R.wait("scalar", dve)
    R.op("scalar", "activation", out=tmpr[0:1, 0:1024], in_=ex2[0:1, 0:1024],
         func=AF.Sqrt, bias=eps_row).inc(act, 1)
    R.wait("vector", act)
    R.op("vector", "reciprocal", out=rstd[0:1, 0:1024],
         in_=tmpr[0:1, 0:1024]).inc(dve, 1)
    for b in ("b2", "b3", "b4", "b5"):
        BK[b].acquire()
    R.wait("tensor", dve)
    bmap = (("b2", 0, True), ("b4", 1, True), ("b3", 0, False), ("b5", 1, False))
    for (bn, half, is_mean) in bmap:
        sl = slice(half * 512, half * 512 + 512)
        o = R.op("tensor", "matmul", BK[bn].t[:, 0:512], S["ONESR_F"][:],
                 (mean if is_mean else rstd)[0:1, sl], start=True, stop=True)
    o.inc(pe, 1)
    meanb = S["scrB"][:, 0:1024]
    rstdb = S["scrB"][:, 1024:2048]
    R.wait("vector", pe)
    for (bn, half, is_mean) in bmap:
        dstb = (meanb if is_mean else rstdb)
        o = R.op("vector", "tensor_copy",
                 out=dstb[:, half * 512: half * 512 + 512],
                 in_=BK[bn].t[:, 0:512])
    o.inc(dve, 1)
    for b in ("b2", "b3", "b4", "b5"):
        BK[b].release("vector")
    ttmp = S["scrB"][:, 2048:3072]
    done = R.sem(f"lnd{tag}")
    for fc in range(FC):
        srcc = src[:, fc * NT:(fc + 1) * NT]
        dstc = dst[:, fc * NT:(fc + 1) * NT]
        R.op("vector", "tensor_sub", out=ttmp[:], in0=srcc, in1=meanb)
        R.op("vector", "tensor_mul", out=dstc, in0=ttmp[:], in1=rstdb)\
            .inc(done, 1)
    return done


def common_setup(nc):
    R = Rec(nc)
    R.s = {k: R.sem(k) for k in ("pe", "dve", "act", "st")}
    BK = {}
    for i in range(6):
        BK[f"b{i}"] = Bank(R, nc.alloc_psum_tensor(f"pb{i}", [128, 512], F32), f"b{i}")
    BK["b6"] = Bank(R, nc.alloc_psum_tensor("pb6", [128, 512], F32), "b6")
    BK["b7"] = Bank(R, nc.alloc_psum_tensor("pb7", [128, 512], F32), "b7")
    return R, BK


def emit_slope_derived(R, S, BK, ld_const):
    pe, dve = R.s["pe"], R.s["dve"]
    sder = R.sem("sder")
    BK["b0"].acquire()
    R.wait("tensor", ld_const)
    R.op("tensor", "matmul", BK["b0"].t[:, 0:16], S["ONESR_F"][:],
         S["slope"][:], start=True, stop=True).inc(pe, 1)
    R.wait("vector", pe)
    R.op("vector", "tensor_scalar", out=S["sneg"][:], in0=BK["b0"].t[:, 0:16],
         scalar1=-1.0, scalar2=None, op0=ALU.mult).inc(dve, 1)
    BK["b0"].release("vector")
    R.wait("scalar", dve)
    R.wait("scalar", ld_const)
    R.op("scalar", "activation", out=S["bd_bc"][:], in_=S["sneg"][:],
         func=AF.Exp, scale=S["CCOL"][:, 4:5]).inc(sder, 1)
    for k in range(8):
        R.op("scalar", "activation", out=S["kvw"][:, k * NH:(k + 1) * NH],
             in_=S["sneg"][:], func=AF.Exp, scale=S["DIST"][:, k:k + 1])\
            .inc(sder, 1)
    for h in range(NH):
        R.op("scalar", "activation", out=S["kdb_all"][:, 2 * h:2 * h + 2],
             in_=S["KD2"][:], func=AF.Exp, scale=S["sneg"][:, h:h + 1])\
            .inc(sder, 1)
    for h in range(NH):
        R.op("scalar", "activation", out=S["kdpre_all"][:, 8 * h:8 * h + 8],
             in_=S["KDPRE"][:], func=AF.Exp, scale=S["sneg"][:, h:h + 1])\
            .inc(sder, 1)
    return sder


def build_prog_fused(consts):
    nc = bass.Bass("TRN2", target_bir_lowering=False, debug=False)
    xT = nc.dram_tensor("xT", [DM, NT], F32, kind="ExternalInput").ap()
    distc = nc.dram_tensor("distc", [128, 8], F32, kind="ExternalInput").ap()
    slope = nc.dram_tensor("slope", [1, NH], F32, kind="ExternalInput").ap()
    wqkvu = nc.dram_tensor("wqkvu", [128, 64 * 2048], F32, kind="ExternalInput").ap()
    wout = nc.dram_tensor("wout", [128, 16 * 2048], F32, kind="ExternalInput").ap()
    w1t = nc.dram_tensor("w1t", [128, 44 * 2048], F32, kind="ExternalInput").ap()
    w2t = nc.dram_tensor("w2t", [128, 44 * 2048], F32, kind="ExternalInput").ap()
    w3t = nc.dram_tensor("w3t", [128, 16 * 5632], F32, kind="ExternalInput").ap()
    out_c = nc.dram_tensor("out_c", [DM, NT], F32, kind="ExternalOutput").ap()
    u_spill = nc.dram_tensor("u_spill", [128, FC * NT], BF16, kind="Internal").ap()
    x1_spill = nc.dram_tensor("x1_spill", [128, FC * NT], BF16, kind="Internal").ap()
    l_out_t = nc.dram_tensor("l_out", [128, NH * 128], BF16, kind="Internal")
    l_all_t = nc.dram_tensor("l_all", [8 * 128, NH * 128], BF16, kind="Internal",
                             addr_space="Shared")

    R, BK = common_setup(nc)
    R.phases = []

    def _twc():
        return sum(2 if o.meth == "matmul" else 1 for o in R.q["tensor"])
    R.twc = _twc
    s = R.s
    pe, dve, act = s["pe"], s["dve"], s["act"]
    A = nc.alloc_sbuf_tensor
    AR1 = A("ar1", [128, FC * NT], BF16)
    AR2 = A("ar2", [128, FC * NT], BF16)
    AR3 = A("ar3", [128, FC * NT], BF16)
    AR4 = A("ar4", [128, FC * NT], BF16)
    SCRB = A("scrb", [128, 6144], BF16)
    SCRF = A("scrf", [128, 5120], F32)
    WGB = [A(f"wgb{i}", [128, 1024], BF16) for i in range(4)]
    WQ = A("wq", [128, 8192], BF16)
    KVOUT = A("kvout", [128, NH * 128], F32)
    S = {
        "slope": A("s_slope", [1, NH], F32),
        "DIST": A("s_dist", [128, 8], F32),
        "kvw": A("s_kvw", [128, 8 * NH], BF16),
        "sneg": A("s_sneg", [128, NH], F32),
        "bd_bc": A("s_bdbc", [128, NH], BF16),
        "kdb_all": A("s_kdball", [128, 2 * NH], F32),
        "kdpre_all": A("s_kdpall", [128, 8 * NH], F32),
        "rowA": A("s_rowA", [1, 1024], F32),
        "rowA2": A("s_rowA2", [1, 1024], F32),
        "rowB": A("s_rowB", [1, 1024], F32),
        "gacc": A("s_gacc", [128, 4], F32),
        "gtmp": A("s_gtmp", [128, 4], F32),
        "gsum": A("s_gsum", [1, 4], F32),
        "th": A("s_th", [1, 4], F32),
        "nth": A("s_nth", [1, 4], F32),
        "thb": A("s_thb", [128, 4], F32),
        "nthb": A("s_nthb", [128, 4], F32),
        "kvbf": SCRB[:, 2048:2176],
        "scrB": SCRB,
    }
    ld_const = R.sem("ldconst")
    CB = load_consts(nc, R,
                     ("IDX0", "IDX1", "QDROW", "KD2", "KDPRE", "ONESR_F",
                      "ONESC_F", "IDEN_BF", "ONESC_BF", "CCOL", "CROW"),
                     consts, ld_const)
    S.update(CB)
    R.dma("sync", S["slope"][:], slope, ld_const)
    R.dma("sync", S["DIST"][:], distc, ld_const)
    R.wait("vector", ld_const)
    R.wait("tensor", ld_const)
    sder = emit_slope_derived(R, S, BK, ld_const)

    wq_addr = nc.lookup_mloc(WQ).addr
    GPOOL = [nc.alloc_sbuf_tensor_at("gpool", [128, 512], F32,
                                     offset=wq_addr + i * 2048)
             for i in range(8)]
    gld = Stream(R, "gld", 8)
    gred = R.sem("gred")
    GLAG = 6
    gsrc = []
    for wi, wt in enumerate((w1t, w2t, w3t)):
        for kk in range(176):
            gsrc.append((wi, wt[:, kk * 512:(kk + 1) * 512]))
    gstate = {"dma": 0, "done": 0, "init": False, "tgt": 0.0}
    GT_XLN, GT_OWN_PROJ, GT_LKV, GT_OWN_ATTN, GT_B4 = 3.0, 2.0, 2.0, 8.0, 12.0

    def emit_gamma_dmas(upto):
        upto = min(upto, gstate["done"] + GLAG, GCH)
        while gstate["dma"] < upto:
            nch = gstate["dma"]
            if nch >= 8:
                R.wait("gpsimd", gred, nch - 7)
            gld.dma("gpsimd", nch % 8, GPOOL[nch % 8][:], gsrc[nch][1])
            gstate["dma"] += 1

    def emit_gamma_chunks(upto):
        if not gstate["init"]:
            R.op("vector", "memset", S["gacc"][:], 0.0)
            gstate["init"] = True
        while gstate["done"] < min(upto, GCH):
            nch = gstate["done"]
            emit_gamma_dmas(nch + GLAG)
            wi = gsrc[nch][0]
            gld.wait("vector", nch % 8, nch // 8 + 1)
            R.op("vector", "tensor_reduce", out=S["gtmp"][:, wi:wi + 1],
                 in_=GPOOL[nch % 8][:], axis=AX, op=ALU.add,
                 apply_absolute_value=True)
            R.op("vector", "tensor_tensor", out=S["gacc"][:, wi:wi + 1],
                 in0=S["gacc"][:, wi:wi + 1], in1=S["gtmp"][:, wi:wi + 1],
                 op=ALU.add).inc(gred, 1)
            gstate["done"] += 1
        emit_gamma_dmas(gstate["done"] + GLAG)

    def gtick(rate):
        gstate["tgt"] += rate
        emit_gamma_chunks(min(int(gstate["tgt"]), GCH))

    ldx = Stream(R, "ldx", 2)
    xc = [SCRF[:, 0:1024], SCRF[:, 1024:2048]]
    cast_done = R.sem("castdone")

    def emit_x_chunk(src_cols, guard_pe):
        base = cast_done.v
        for fc in range(FC):
            p = fc % 2
            if fc >= 2:
                R.wait("sync", cast_done, base + fc - 1)
            elif base > 0:
                R.wait("sync", cast_done, base)
            ldx.dma("sync", p, xc[p][:], src_cols(fc))
            ldx.wait("vector", p)
            if guard_pe and fc < 2:
                R.wait("vector", pe, guard_pe)
            R.op("vector", "tensor_copy", out=AR1[:, fc * NT:(fc + 1) * NT],
                 in_=xc[p][:]).inc(cast_done, 1)
            gtick(GT_XLN)

    wld = Stream(R, "wld", 2)
    wcast = R.sem("wcast")
    WG = [SCRF[:, 2048:3072], SCRF[:, 3072:4096]]
    sevq = R.sem("sevq")
    vev = R.sem("vev")
    trev = R.sem("trev")
    sevu = R.sem("sevu")
    uev = Stream(R, "uev", 2)
    vstg = [SCRB[:, 3072 + i * 512: 3584 + i * 512] for i in range(2)]
    ustg = [SCRB[:, 4096 + i * 512: 4608 + i * 512] for i in range(2)]
    PRM = {"g": 0, "vtr_pe": {}, "un": 0, "stg4": {}}

    def emit_proj_phase(granules, wsrc, src, ln_done, prev_guard_pe,
                        grate=0.0):
        ng = len(granules)
        gp_after = {}
        wcast_after = {}

        def wload(gq):
            oc = granules[gq][0]
            for half in range(2):
                gg = PRM["g"]
                PRM["g"] += 1
                bgi = gg % 4
                if gg >= 2:
                    R.wait("sync", wcast, gg - 1)
                wld.dma("sync", gg % 2, WG[gg % 2][:],
                        wsrc[:, oc * 2048 + half * 1024: oc * 2048 + (half + 1) * 1024])
                wld.wait("scalar", gg % 2)
                if gq >= 2:
                    R.wait("scalar", pe, gp_after[gq - 2])
                elif prev_guard_pe:
                    R.wait("scalar", pe, prev_guard_pe)
                R.op("scalar", "activation", out=WGB[bgi][:],
                     in_=WG[gg % 2][:], func=AF.Copy).inc(wcast, 1)
            wcast_after[gq] = wcast.v

        wload(0)
        for j in range(ng):
            if j + 1 < ng:
                wload(j + 1)
            R.wait("tensor", wcast, wcast_after[j])
            if j == 0 and ln_done is not None:
                R.wait("tensor", ln_done, FC)
            _, kind, jj, dst = granules[j]
            b_lo, b_hi = WGB[(2 * j) % 4], WGB[(2 * j + 1) % 4]
            pnames = ("b0", "b1") if j % 2 == 0 else ("b2", "b3")
            for tc_ in range(2):
                bn = pnames[tc_]
                BK[bn].acquire()
                for kc in range(16):
                    wb = b_lo if kc < 8 else b_hi
                    off = (kc % 8) * 128
                    o = R.op("tensor", "matmul", BK[bn].t[:, 0:512],
                             wb[:, off:off + 128],
                             src[:, kc * NT + tc_ * 512: kc * NT + tc_ * 512 + 512],
                             start=(kc == 0), stop=(kc == 15))
                o.inc(pe, 1)
                pe_now = pe.v
                if kind in ("q", "k"):
                    dslice = dst[:, jj * NT + tc_ * 512: jj * NT + tc_ * 512 + 512]
                    R.wait("scalar", pe, pe_now)
                    R.op("scalar", "activation", out=dslice,
                         in_=BK[bn].t[:, 0:512], func=AF.Silu).inc(sevq, 1)
                    BK[bn].release("scalar")
                elif kind == "v":
                    R.wait("vector", pe, pe_now)
                    key = tc_
                    if key in PRM["vtr_pe"]:
                        R.wait("vector", pe, PRM["vtr_pe"][key])
                    R.op("vector", "tensor_copy", out=vstg[tc_][:],
                         in_=BK[bn].t[:, 0:512]).inc(vev, 1)
                    BK[bn].release("vector")
                    BK["b6"].acquire()
                    R.wait("tensor", vev)
                    for i4 in range(4):
                        o = R.op("tensor", "matmul",
                                 BK["b6"].t[:, i4 * 128:(i4 + 1) * 128],
                                 vstg[tc_][:, i4 * 128:(i4 + 1) * 128],
                                 S["IDEN_BF"][:], start=True, stop=True)
                    o.inc(pe, 1)
                    PRM["vtr_pe"][key] = pe.v
                    R.wait("vector", pe, pe.v)
                    for i4 in range(4):
                        tch = tc_ * 4 + i4
                        R.op("vector", "tensor_copy",
                             out=dst[:, tch * DM + jj * 128: tch * DM + (jj + 1) * 128],
                             in_=BK["b6"].t[:, i4 * 128:(i4 + 1) * 128]).inc(trev, 1)
                    BK["b6"].release("vector")
                else:
                    un = PRM["un"]
                    PRM["un"] += 1
                    R.wait("vector", pe, pe_now)
                    if un >= 2:
                        uev.wait("vector", un % 2, un // 2)
                    R.op("vector", "tensor_copy", out=ustg[un % 2][:],
                         in_=BK[bn].t[:, 0:512]).inc(sevu, 1)
                    BK[bn].release("vector")
                    R.wait("scalar", sevu)
                    uev.dma("scalar", un % 2,
                            u_spill[:, jj * NT + tc_ * 512: jj * NT + tc_ * 512 + 512],
                            ustg[un % 2][:])
            gp_after[j] = pe.v
            if grate:
                gtick(grate)
        return gp_after, pe.v

    sktok = R.sem("sktok")
    skv = R.sem("skv")

    def emit_local_kv(k_evac_total, v_evac_total):
        R.wait("tensor", sevq, k_evac_total)
        R.wait("tensor", trev, v_evac_total)
        R.wait("vector", sder)
        uev.wait_all("vector")
        stg4 = [SCRB[:, 2048 + i * 512: 2560 + i * 512] for i in range(8)]
        trb = ("b6", "b0")
        C_tok = {}
        sk_after = {}

        def emit_tr(h):
            for half in range(2):
                slot = (h % 4) * 2 + half
                BK[trb[half]].acquire()
                for i4 in range(4):
                    tb = half * 4 + i4
                    o = R.op("tensor", "matmul",
                             BK[trb[half]].t[:, i4 * 128:(i4 + 1) * 128],
                             AR3[:, h * NT + tb * 128: h * NT + (tb + 1) * 128],
                             S["IDEN_BF"][:], start=True, stop=True)
                o.inc(pe, 1)
                R.wait("vector", pe, pe.v)
                g = PRM["stg4"].get(slot)
                if g:
                    R.wait("vector", pe, g)
                for i4 in range(4):
                    tb = half * 4 + i4
                    o = R.op("vector", "tensor_scalar",
                             out=stg4[slot][:, i4 * 128:(i4 + 1) * 128],
                             in0=BK[trb[half]].t[:, i4 * 128:(i4 + 1) * 128],
                             scalar1=S["kdpre_all"][:, h * 8 + tb: h * 8 + tb + 1],
                             scalar2=None, op0=ALU.mult)
                o.inc(sktok, 1)
                BK[trb[half]].release("vector")
            sk_after[h] = sktok.v

        emit_tr(0)
        emit_tr(1)
        emit_tr(2)
        for h in range(NH):
            if h + 3 < NH:
                emit_tr(h + 3)
            cb = BK["b7"] if h % 2 == 0 else BK["b1"]
            cb.acquire()
            R.wait("tensor", sktok, sk_after[h])
            for tb in range(8):
                slot = (h % 4) * 2 + tb // 4
                o = R.op("tensor", "matmul", cb.t[:, 0:128],
                         stg4[slot][:, (tb % 4) * 128:(tb % 4) * 128 + 128],
                         AR4[:, tb * DM + h * 128: tb * DM + (h + 1) * 128],
                         start=(tb == 0), stop=(tb == 7))
            o.inc(pe, 1)
            PRM["stg4"][(h % 4) * 2] = PRM["stg4"][(h % 4) * 2 + 1] = pe.v
            C_tok[h] = pe.v
            R.wait("vector", pe, C_tok[h])
            kvslice = KVOUT[:, h * 128:(h + 1) * 128]
            R.op("vector", "tensor_copy", out=kvslice,
                 in_=cb.t[:, 0:128]).inc(skv, 1)
            cb.release("vector")
            gtick(GT_LKV)
        return pe.v

    emit_x_chunk(lambda fc: xT[fc * 128:(fc + 1) * 128, :], None)
    R.phases.append(("own_ln", R.twc()))
    ln1 = emit_layernorm(R, S, BK, AR1, AR1, S["CROW"][0:1, 0:1],
                         cast_done, "1")
    gran = [(j, "q", j, AR2) for j in range(16)] + \
           [(16 + j, "k", j, AR3) for j in range(16)] + \
           [(48 + j, "u", j, None) for j in range(16)] + \
           [(32 + j, "v", j, AR4) for j in range(16)]
    R.phases.append(("own_proj", R.twc()))
    gp_own, own_tail = emit_proj_phase(gran, wqkvu, AR1, ln1, None,
                                       GT_OWN_PROJ)
    q_total, k_total, v_total = sevq.v, sevq.v, trev.v

    R.phases.append(("lkv", R.twc()))
    lkv_pe = emit_local_kv(sevq.v, trev.v)

    R.phases.append(("cc", R.twc()))
    lsp = R.sem("lsp")
    lod = R.sem("lod")
    ccs = R.sem("ccs")
    lin = R.sem("lin")
    cmb = R.sem("cmb")
    LBF = SCRB[:, 0:2048]
    LRK = [SCRB[:, 2048:4096], SCRB[:, 4096:6144]]
    R.wait("vector", skv)
    R.op("vector", "tensor_copy", out=LBF, in_=KVOUT[:]).inc(lsp, 1)
    R.op("vector", "memset", KVOUT[:], 0.0)
    R.wait("gpsimd", lsp, 1)
    R.dma("gpsimd", l_out_t.ap(), LBF, lod)
    R.wait("gpsimd", lod, 16)
    R.op("gpsimd", "collective_compute", "AllGather", ALU.bypass,
         replica_groups=[list(range(8))],
         ins=[l_out_t.ap().opt()], outs=[l_all_t.ap().opt()]).inc(ccs, 1)
    R.wait("gpsimd", ccs, 1)
    R.wait("gpsimd", pe, lkv_pe)
    l_all = l_all_t.ap()
    for r in range(8):
        if r >= 2:
            R.wait("gpsimd", cmb, r - 1)
        R.dma("gpsimd", LRK[r % 2][:], l_all[r * 128:(r + 1) * 128, :], lin)
    R.wait("vector", sder)
    for r in range(8):
        R.wait("vector", lin, 16 * (r + 1))
        for h in range(NH):
            o = R.op("vector", "scalar_tensor_tensor",
                     out=KVOUT[:, h * 128:(h + 1) * 128],
                     in0=LRK[r % 2][:, h * 128:(h + 1) * 128],
                     scalar=S["kvw"][:, r * NH + h: r * NH + h + 1],
                     in1=KVOUT[:, h * 128:(h + 1) * 128],
                     op0=ALU.mult, op1=ALU.add)
        o.inc(cmb, 1)

    qkmb = [SCRB[:, 3072 + i * 512: 3584 + i * 512] for i in range(2)]
    R.phases.append(("own_attn", R.twc()))
    maskb = [SCRB[:, 4096 + i * 512: 4608 + i * 512] for i in range(2)]
    qdb = [SCRB[:, 5120 + i * 256: 5376 + i * 256] for i in range(2)]
    ktokb = [SCRB[:, 5632 + i * 256: 5888 + i * 256] for i in range(2)]
    smask = R.sem("smask")
    sqs = R.sem("sqs")
    skvbf = R.sem("skvbf")
    sqkm = R.sem("sqkm")
    soev = R.sem("soev")
    uload = R.sem("uload")
    sq2 = R.sem("sq2")
    osq = [SCRB[:, 0:1024], SCRB[:, 1024:2048]]
    stats_pe = {}
    BK["b4"].acquire()
    BK["b5"].acquire()
    R.wait("vector", pe, own_tail)
    R.wait("vector", sder)
    R.wait("tensor", sevq, q_total)
    R.wait("tensor", trev, v_total)
    R.wait("vector", sevq, q_total)
    uev.wait_all("scalar")
    R.wait("scalar", cmb, 8)
    C_pe_after = {}
    diag_pe_after = {}
    for h in range(NH):
        hp = h % 2
        R.wait("scalar", sder)
        if h >= 2:
            R.wait("scalar", sqkm, 4 * (h - 1))
            R.wait("scalar", sqs, h - 1)
        R.op("scalar", "activation", out=maskb[hp][:, 0:256], in_=S["IDX0"][:],
             func=AF.Exp, scale=S["sneg"][:, h:h + 1])
        R.op("scalar", "activation", out=maskb[hp][:, 256:512], in_=S["IDX1"][:],
             func=AF.Exp, scale=S["sneg"][:, h:h + 1])
        R.op("scalar", "activation", out=qdb[hp][:], in_=S["QDROW"][:],
             func=AF.Exp, scale=S["sneg"][:, h:h + 1]).inc(smask, 1)
        R.wait("vector", smask, h + 1)
        qd = qdb[hp]
        qrep = bass.AP(qd.tensor, qd.offset, [[qd.ap[0][0], 128], [0, 4], [1, 256]])
        R.op("vector", "tensor_tensor",
             out=AR1[:, h * NT:(h + 1) * NT].rearrange("p (b m) -> p b m", b=4),
             in0=AR2[:, h * NT:(h + 1) * NT].rearrange("p (b m) -> p b m", b=4),
             in1=qrep, op=ALU.mult).inc(sqs, 1)
        kvslice = KVOUT[:, h * 128:(h + 1) * 128]
        R.op("vector", "tensor_copy", out=S["kvbf"][:], in_=kvslice)\
            .inc(skvbf, 1)
        for i in range(NBLK):
            blk = h * NBLK + i
            BK["b6"].acquire()
            o = R.op("tensor", "matmul", BK["b6"].t[:, 0:128],
                     AR3[:, h * NT + i * 256: h * NT + i * 256 + 128],
                     S["IDEN_BF"][:], start=True, stop=True)
            o = R.op("tensor", "matmul", BK["b6"].t[:, 128:256],
                     AR3[:, h * NT + i * 256 + 128: h * NT + i * 256 + 256],
                     S["IDEN_BF"][:], start=True, stop=True)
            o.inc(pe, 1)
            tr_pe = pe.v
            R.wait("vector", pe, tr_pe)
            prevu = blk - 1 if i > 0 else ((h - 2) * NBLK + 3 if h >= 2 else None)
            if prevu is not None and prevu in C_pe_after:
                R.wait("vector", pe, C_pe_after[prevu])
            R.op("vector", "tensor_scalar", out=ktokb[hp][:, 0:128],
                 in0=BK["b6"].t[:, 0:128],
                 scalar1=S["kdb_all"][:, 2 * h:2 * h + 1],
                 scalar2=None, op0=ALU.mult)
            R.op("vector", "tensor_scalar", out=ktokb[hp][:, 128:256],
                 in0=BK["b6"].t[:, 128:256],
                 scalar1=S["kdb_all"][:, 2 * h + 1:2 * h + 2],
                 scalar2=None, op0=ALU.mult).inc(sktok, 1)
            BK["b6"].release("vector")
            BK["b7"].acquire()
            R.wait("tensor", sktok)
            tch = 2 * i
            R.op("tensor", "matmul", BK["b7"].t[:, 0:128],
                 ktokb[hp][:, 0:128],
                 AR4[:, tch * DM + h * 128: tch * DM + (h + 1) * 128],
                 start=True, stop=False)
            R.op("tensor", "matmul", BK["b7"].t[:, 0:128],
                 ktokb[hp][:, 128:256],
                 AR4[:, (tch + 1) * DM + h * 128: (tch + 1) * DM + (h + 1) * 128],
                 start=False, stop=True).inc(pe, 1)
            C_pe_after[blk] = pe.v
            qkbn = "b0" if i % 2 == 0 else "b3"
            BK[qkbn].acquire()
            R.op("tensor", "matmul", BK[qkbn].t[:, 0:256],
                 AR3[:, h * NT + i * 256: h * NT + i * 256 + 128],
                 AR2[:, h * NT + i * 256: h * NT + (i + 1) * 256],
                 start=True, stop=True)
            R.op("tensor", "matmul", BK[qkbn].t[:, 256:512],
                 AR3[:, h * NT + i * 256 + 128: h * NT + i * 256 + 256],
                 AR2[:, h * NT + i * 256: h * NT + (i + 1) * 256],
                 start=True, stop=True).inc(pe, 1)
            qk_pe = pe.v
            R.wait("vector", pe, qk_pe)
            prevq = blk - 1 if i > 0 else ((h - 2) * NBLK + 3 if h >= 2 else None)
            if prevq is not None and prevq in diag_pe_after:
                R.wait("vector", pe, diag_pe_after[prevq])
            R.op("vector", "tensor_tensor", out=qkmb[hp][:, 0:256],
                 in0=BK[qkbn].t[:, 0:256], in1=maskb[hp][:, 0:256], op=ALU.mult)
            R.op("vector", "tensor_tensor", out=qkmb[hp][:, 256:512],
                 in0=BK[qkbn].t[:, 256:512], in1=maskb[hp][:, 256:512],
                 op=ALU.mult).inc(sqkm, 1)
            BK[qkbn].release("vector")
            ob = "b1" if i % 2 == 0 else "b2"
            BK[ob].acquire()
            R.wait("tensor", skvbf, 4 * h + i + 1)
            R.wait("tensor", sqs, h + 1)
            R.op("tensor", "matmul", BK[ob].t[:, 0:256], S["kvbf"][:],
                 AR1[:, h * NT + i * 256: h * NT + (i + 1) * 256],
                 start=True, stop=False)
            R.wait("tensor", sqkm, blk + 1)
            assert skvbf.v == 4 * h + i + 1
            R.op("tensor", "matmul", BK[ob].t[:, 0:256],
                 AR4[:, tch * DM + h * 128: tch * DM + (h + 1) * 128],
                 qkmb[hp][:, 0:256], start=False, stop=False)
            R.op("tensor", "matmul", BK[ob].t[:, 0:256],
                 AR4[:, (tch + 1) * DM + h * 128: (tch + 1) * DM + (h + 1) * 128],
                 qkmb[hp][:, 256:512], start=False, stop=True).inc(pe, 1)
            diag_pe_after[blk] = pe.v
            R.wait("vector", pe, pe.v)
            R.op("vector", "scalar_tensor_tensor", out=kvslice, in0=kvslice,
                 scalar=S["bd_bc"][:, h:h + 1], in1=BK["b7"].t[:, 0:128],
                 op0=ALU.mult, op1=ALU.add).inc(skv, 1)
            BK["b7"].release("vector")
            if i < NBLK - 1:
                R.op("vector", "tensor_copy", out=S["kvbf"][:], in_=kvslice)\
                    .inc(skvbf, 1)
            R.op("vector", "tensor_copy",
                 out=AR2[:, h * NT + i * 256: h * NT + (i + 1) * 256],
                 in_=BK[ob].t[:, 0:256]).inc(soev, 1)
            BK[ob].release("vector")
            if i == NBLK - 1:
                R.wait("scalar", soev)
                if h >= 2:
                    R.wait("scalar", pe, stats_pe[h - 2])
                R.op("scalar", "activation", out=osq[h % 2][:],
                     in_=AR2[:, h * NT:(h + 1) * NT], func=AF.Square)\
                    .inc(sq2, 1)
                if h >= 1:
                    hh = h - 1
                    R.wait("tensor", sq2, hh + 1)
                    for half in range(2):
                        o = R.op("tensor", "matmul",
                                 (BK["b4"] if half == 0 else BK["b5"]).t[0:1, 0:512],
                                 S["ONESC_BF"][:],
                                 osq[hh % 2][:, half * 512: half * 512 + 512],
                                 start=(hh == 0), stop=False)
                    o.inc(pe, 1)
                    stats_pe[hh] = pe.v
                if h == 7:
                    uev.wait_all("sync")
                    R.wait("sync", pe, diag_pe_after[31])
                    for fc in range(0, FC, 2):
                        R.dma("sync", AR4[:, fc * NT:(fc + 1) * NT],
                              u_spill[:, fc * NT:(fc + 1) * NT], uload)
                if h == 15:
                    R.wait("sync", pe, diag_pe_after[63])
                    for fc in range(1, FC, 2):
                        R.dma("sync", AR4[:, fc * NT:(fc + 1) * NT],
                              u_spill[:, fc * NT:(fc + 1) * NT], uload)
                gtick(GT_OWN_ATTN)

    R.phases.append(("B3", R.twc()))
    R.wait("tensor", sq2, 16)
    for half in range(2):
        o = R.op("tensor", "matmul",
                 (BK["b4"] if half == 0 else BK["b5"]).t[0:1, 0:512],
                 S["ONESC_BF"][:], osq[15 % 2][:, half * 512: half * 512 + 512],
                 start=False, stop=True)
    o.inc(pe, 1)
    stats_pe[15] = pe.v
    R.wait("scalar", pe, stats_pe[15])
    R.op("scalar", "activation", out=S["rowA2"][0:1, 0:512],
         in_=BK["b4"].t[0:1, 0:512], func=AF.Sqrt,
         scale=S["CROW"][0:1, 3:4], bias=S["CROW"][0:1, 1:2])
    R.op("scalar", "activation", out=S["rowA2"][0:1, 512:1024],
         in_=BK["b5"].t[0:1, 0:512], func=AF.Sqrt,
         scale=S["CROW"][0:1, 3:4], bias=S["CROW"][0:1, 1:2]).inc(act, 1)
    BK["b4"].release("scalar")
    BK["b5"].release("scalar")
    R.wait("vector", act)
    R.op("vector", "reciprocal", out=S["rowA"][0:1, 0:1024],
         in_=S["rowA2"][0:1, 0:1024]).inc(dve, 1)
    BK["b4"].acquire()
    BK["b5"].acquire()
    R.wait("tensor", dve)
    R.op("tensor", "matmul", BK["b4"].t[:, 0:512], S["ONESR_F"][:],
         S["rowA"][0:1, 0:512], start=True, stop=True)
    R.op("tensor", "matmul", BK["b5"].t[:, 0:512], S["ONESR_F"][:],
         S["rowA"][0:1, 512:1024], start=True, stop=True).inc(pe, 1)
    rstdb = SCRB[:, 2048:3072]
    R.wait("vector", pe)
    R.op("vector", "tensor_copy", out=rstdb[:, 0:512], in_=BK["b4"].t[:, 0:512])
    R.op("vector", "tensor_copy", out=rstdb[:, 512:1024],
         in_=BK["b5"].t[:, 0:512]).inc(dve, 1)
    BK["b4"].release("vector")
    BK["b5"].release("vector")


    gdone = R.sem("gdone")
    ttmp = SCRB[:, 3072:4096]
    R.wait("vector", uload, 16 * 16)
    for fc in range(FC):
        R.op("vector", "tensor_mul", out=ttmp[:],
             in0=AR2[:, fc * NT:(fc + 1) * NT], in1=rstdb[:])
        R.op("vector", "tensor_mul", out=AR3[:, fc * NT:(fc + 1) * NT],
             in0=ttmp[:], in1=AR4[:, fc * NT:(fc + 1) * NT]).inc(gdone, 1)

    R.phases.append(("B4", R.twc()))
    xch = [SCRF[:, 0:1024], SCRF[:, 1024:2048]]
    x1done = R.sem("x1done")
    gp_B4 = {}
    b4_wc = {}

    def emit_wload4(jq):
        p = jq % 2
        if jq >= 2:
            R.wait("sync", x1done, 2 * (jq - 1))
        ldx.dma("sync", p, xch[p][:], xT[jq * 128:(jq + 1) * 128, :])
        for half in range(2):
            gg = PRM["g"]
            PRM["g"] += 1
            if gg >= 2:
                R.wait("sync", wcast, gg - 1)
            wld.dma("sync", gg % 2, WG[gg % 2][:],
                    wout[:, jq * 2048 + half * 1024: jq * 2048 + (half + 1) * 1024])
            wld.wait("scalar", gg % 2)
            if jq >= 2:
                R.wait("scalar", pe, gp_B4[jq - 2])
            else:
                R.wait("scalar", pe, own_tail)
            R.op("scalar", "activation", out=WGB[gg % 4][:],
                 in_=WG[gg % 2][:], func=AF.Copy).inc(wcast, 1)
        b4_wc[jq] = wcast.v

    R.wait("sync", cast_done)
    emit_wload4(0)
    for j in range(16):
        p = j % 2
        if j + 1 < 16:
            emit_wload4(j + 1)
        R.wait("tensor", wcast, b4_wc[j])
        if j == 0:
            R.wait("tensor", gdone, FC)
        b_lo, b_hi = WGB[(2 * j) % 4], WGB[(2 * j + 1) % 4]
        pnames = ("b0", "b1") if j % 2 == 0 else ("b2", "b3")
        for tc_ in range(2):
            bn = pnames[tc_]
            BK[bn].acquire()
            for kc in range(16):
                wb = b_lo if kc < 8 else b_hi
                off = (kc % 8) * 128
                o = R.op("tensor", "matmul", BK[bn].t[:, 0:512],
                         wb[:, off:off + 128],
                         AR3[:, kc * NT + tc_ * 512: kc * NT + tc_ * 512 + 512],
                         start=(kc == 0), stop=(kc == 15))
            o.inc(pe, 1)
            R.wait("vector", pe)
            ldx.wait("vector", p)
            R.op("vector", "tensor_tensor",
                 out=AR2[:, j * NT + tc_ * 512: j * NT + tc_ * 512 + 512],
                 in0=BK[bn].t[:, 0:512],
                 in1=xch[p][:, tc_ * 512: tc_ * 512 + 512],
                 op=ALU.add).inc(x1done, 1)
            BK[bn].release("vector")
        gp_B4[j] = pe.v
        gtick(GT_B4)

    R.phases.append(("B5_ln2", R.twc()))
    ln2_done = emit_layernorm(R, S, BK, AR2, AR1, S["CROW"][0:1, 0:1],
                              x1done, "2")
    spill = R.sem("spill")
    R.wait("scalar", ln2_done)
    R.dma("scalar", x1_spill, AR2[:], spill)

    R.phases.append(("thresh", R.twc()))
    emit_gamma_chunks(GCH)
    BK["b7"].acquire()
    R.wait("tensor", gred, GCH)
    R.op("tensor", "matmul", BK["b7"].t[0:1, 0:4], S["ONESC_F"][:],
         S["gacc"][:], start=True, stop=True).inc(pe, 1)
    R.wait("vector", pe)
    R.op("vector", "tensor_copy", out=S["gsum"][:], in_=BK["b7"].t[0:1, 0:4])\
        .inc(dve, 1)
    BK["b7"].release("vector")
    R.op("vector", "tensor_scalar", out=S["th"][:], in0=S["gsum"][:],
         scalar1=1.0 / NW_ELEMS, scalar2=1e-8, op0=ALU.mult, op1=ALU.add)
    R.op("vector", "tensor_scalar", out=S["th"][:], in0=S["th"][:],
         scalar1=0.5, scalar2=None, op0=ALU.mult)
    R.op("vector", "tensor_scalar", out=S["nth"][:], in0=S["th"][:],
         scalar1=-1.0, scalar2=None, op0=ALU.mult).inc(dve, 1)
    BK["b5"].acquire()
    R.wait("tensor", dve)
    R.op("tensor", "matmul", BK["b5"].t[:, 0:4], S["ONESR_F"][:], S["th"][:],
         start=True, stop=True)
    R.op("tensor", "matmul", BK["b5"].t[:, 4:8], S["ONESR_F"][:], S["nth"][:],
         start=True, stop=True).inc(pe, 1)
    R.wait("vector", pe)
    R.op("vector", "tensor_copy", out=S["thb"][:], in_=BK["b5"].t[:, 0:4])
    R.op("vector", "tensor_copy", out=S["nthb"][:], in_=BK["b5"].t[:, 4:8])\
        .inc(dve, 1)
    BK["b5"].release("vector")
    th_ready = dve.v

    def hidden_ap(kc, lo, width):
        if kc < 16:
            return AR2[:, kc * NT + lo: kc * NT + lo + width]
        if kc < 32:
            return AR4[:, (kc - 16) * NT + lo: (kc - 16) * NT + lo + width]
        return AR3[:, (kc - 32) * NT + lo: (kc - 32) * NT + lo + width]

    R.phases.append(("B7", R.twc()))
    f32bufs = {"w1": [SCRF[:, 0:1024], SCRF[:, 1024:2048]],
               "w2": [SCRF[:, 2048:3072], SCRF[:, 3072:4096]]}
    wqbufs = {"w1": [[WQ[:, 0:1024], WQ[:, 1024:2048]],
                     [WQ[:, 4096:5120], WQ[:, 5120:6144]]],
              "w2": [[WQ[:, 2048:3072], WQ[:, 3072:4096]],
                     [WQ[:, 6144:7168], WQ[:, 7168:8192]]]}
    tsD = SCRB[:, 0:2048]
    tsP = SCRB[:, 2048:4096]
    sgb = [SCRB[:, 4096 + i * 512: 4608 + i * 512] for i in range(4)]
    fld = {"w1": Stream(R, "fld1", 2), "w2": Stream(R, "fld2", 2)}
    qbar = R.sem("qbarD")
    qbar_n = [0]
    fq = {"w1": R.sem("fq1"), "w2": R.sem("fq2")}
    sgd = R.sem("sgd")
    hid = R.sem("hid")
    wsrc = {"w1": w1t, "w2": w2t}
    thcol = {"w1": 0, "w2": 1, "w3": 2}
    qtok = {}
    R.wait("sync", spill, 16)
    R.wait("sync", ln2_done)
    R.wait("vector", spill, 16)
    R.wait("gpsimd", spill, 16)
    R.wait("vector", gdone)
    uev.wait_all("scalar")
    mm_pe_after = {}

    def emit_quant7(jq):
        for wname in ("w1", "w2"):
            ts = tsD if wname == "w1" else tsP
            for half in range(2):
                if (wname, half) in qtok:
                    R.wait("sync", qbar, qtok[(wname, half)])
                fld[wname].dma("sync", half, f32bufs[wname][half][:],
                               wsrc[wname][:, jq * 2048 + half * 1024: jq * 2048 + (half + 1) * 1024])
                fld[wname].wait("vector", half)
                R.wait("vector", dve, th_ready)
                if 2 * jq + half >= 1:
                    R.wait("vector", fq[wname], 2 * jq + half)
                col = thcol[wname]
                src = f32bufs[wname][half][:]
                dst = wqbufs[wname][jq % 2][half][:]
                R.op("vector", "tensor_scalar", out=ts[:, 0:1024], in0=src,
                     scalar1=S["thb"][:, col:col + 1], scalar2=None,
                     op0=ALU.is_ge).inc(qbar, 1)
                R.op("vector", "tensor_scalar", out=ts[:, 1024:2048], in0=src,
                     scalar1=S["nthb"][:, col:col + 1], scalar2=None,
                     op0=ALU.is_le).inc(qbar, 1)
                qbar_n[0] += 2
                qtok[(wname, half)] = qbar_n[0]
                R.wait("gpsimd", qbar, qbar_n[0])
                if jq >= 2 and (jq - 2, wname) in mm_pe_after:
                    R.wait("gpsimd", pe, mm_pe_after[(jq - 2, wname)])
                R.op("gpsimd", "tensor_tensor", out=dst, in0=ts[:, 0:1024],
                     in1=ts[:, 1024:2048], op=ALU.subtract).inc(fq[wname], 1)

    w2banks = [("b4", "b5"), ("b6", "b7")]
    emit_quant7(0)
    for j in range(44):
        if j + 1 < 44:
            emit_quant7(j + 1)
        R.wait("tensor", fq["w1"], 2 * (j + 1))
        R.wait("tensor", fq["w2"], 2 * (j + 1))
        x1banks = ("b0", "b1") if j % 2 == 0 else ("b2", "b3")
        for tc_ in range(2):
            bn = x1banks[tc_]
            BK[bn].acquire()
            for kc in range(16):
                wb = wqbufs["w1"][j % 2][kc // 8]
                off = (kc % 8) * 128
                o = R.op("tensor", "matmul", BK[bn].t[:, 0:512],
                         wb[:, off:off + 128],
                         AR1[:, kc * NT + tc_ * 512: kc * NT + tc_ * 512 + 512],
                         start=(kc == 0), stop=(kc == 15))
            o.inc(pe, 1)
            if tc_ == 1:
                mm_pe_after[(j, "w1")] = pe.v
            bn2 = w2banks[j % 2][tc_]
            BK[bn2].acquire()
            for kc in range(16):
                wb = wqbufs["w2"][j % 2][kc // 8]
                off = (kc % 8) * 128
                o = R.op("tensor", "matmul", BK[bn2].t[:, 0:512],
                         wb[:, off:off + 128],
                         AR1[:, kc * NT + tc_ * 512: kc * NT + tc_ * 512 + 512],
                         start=(kc == 0), stop=(kc == 15))
            o.inc(pe, 1)
            if tc_ == 1:
                mm_pe_after[(j, "w2")] = pe.v
            sg = sgb[(j % 2) * 2 + tc_]
            R.wait("scalar", pe)
            if j >= 2:
                R.wait("scalar", hid, 2 * (j - 2) + tc_ + 1)
            R.op("scalar", "activation", out=sg[:], in_=BK[bn].t[:, 0:512],
                 func=AF.Sigmoid).inc(sgd, 1)
            BK[bn].release("scalar")
            R.wait("vector", sgd)
            R.op("vector", "tensor_tensor", out=hidden_ap(j, tc_ * 512, 512),
                 in0=BK[bn2].t[:, 0:512], in1=sg[:], op=ALU.mult)\
                .inc(hid, 1)
            BK[bn2].release("vector")

    R.phases.append(("B8", R.twc()))
    w3f = [SCRF[:, 0:1408], SCRF[:, 1408:2816]]
    OSTG = [SCRF[:, 2816:3328], SCRF[:, 3328:3840]]
    w3q = [[AR1[:, par * 5632 + i * 1408: par * 5632 + (i + 1) * 1408]
            for i in range(4)] for par in range(2)]
    t3a = [WQ[:, 0:1408], WQ[:, 2816:4224]]
    t3b = [WQ[:, 1408:2816], WQ[:, 4224:5632]]
    x1b = [SCRB[:, 0:1024], SCRB[:, 1024:2048]]
    f3ld = Stream(R, "f3ld", 2)
    f3q = R.sem("f3q")
    oev = R.sem("oev")
    ost = Stream(R, "ost", 2)
    R.wait("sync", fq["w1"], 88)
    R.wait("sync", fq["w2"], 88)
    R.wait("vector", sgd, 88)
    R.wait("vector", hid, 88)
    R.wait("gpsimd", pe)
    qtok3 = {}
    w3_pe_after = {}

    def emit_quant8(jq):
        p = jq % 2
        if jq >= 2:
            R.wait("sync", oev, (jq - 1) * 2)
        ldx.dma("sync", p, x1b[p][:], x1_spill[:, jq * NT:(jq + 1) * NT])
        for part in range(4):
            g = jq * 4 + part
            if g >= 2 and (g - 2) in qtok3:
                R.wait("sync", qbar, qtok3[g - 2])
            f3ld.dma("sync", part % 2, w3f[part % 2][:],
                     w3t[:, jq * 5632 + part * 1408: jq * 5632 + (part + 1) * 1408])
            f3ld.wait("vector", part % 2)
            R.wait("vector", dve, th_ready)
            col = thcol["w3"]
            if g >= 2:
                R.wait("vector", f3q, g - 1)
            gp2 = g % 2
            R.op("vector", "tensor_scalar", out=t3a[gp2][:], in0=w3f[part % 2][:],
                 scalar1=S["thb"][:, col:col + 1], scalar2=None, op0=ALU.is_ge)\
                .inc(qbar, 1)
            R.op("vector", "tensor_scalar", out=t3b[gp2][:], in0=w3f[part % 2][:],
                 scalar1=S["nthb"][:, col:col + 1], scalar2=None, op0=ALU.is_le)\
                .inc(qbar, 1)
            qbar_n[0] += 2
            qtok3[g] = qbar_n[0]
            R.wait("gpsimd", qbar, qbar_n[0])
            if jq >= 2 and (jq - 2) in w3_pe_after:
                R.wait("gpsimd", pe, w3_pe_after[jq - 2])
            R.op("gpsimd", "tensor_tensor", out=w3q[jq % 2][part][:],
                 in0=t3a[gp2][:], in1=t3b[gp2][:], op=ALU.subtract).inc(f3q, 1)

    emit_quant8(0)
    for j in range(16):
        p = j % 2
        if j + 1 < 16:
            emit_quant8(j + 1)
        R.wait("tensor", f3q, (j + 1) * 4)
        if j == 0:
            R.wait("tensor", hid, 88)
        pn3 = ("b0", "b1") if j % 2 == 0 else ("b2", "b3")
        for tc_ in range(2):
            bn = pn3[tc_]
            BK[bn].acquire()
            for kc in range(44):
                part = kc // 11
                off = (kc % 11) * 128
                o = R.op("tensor", "matmul", BK[bn].t[:, 0:512],
                         w3q[j % 2][part][:, off:off + 128],
                         hidden_ap(kc, tc_ * 512, 512),
                         start=(kc == 0), stop=(kc == 43))
            o.inc(pe, 1)
            R.wait("vector", pe)
            ldx.wait("vector", p)
            if j >= 1:
                ost.wait("vector", tc_, j)
            R.op("vector", "tensor_tensor", out=OSTG[tc_][:],
                 in0=BK[bn].t[:, 0:512],
                 in1=x1b[p][:, tc_ * 512: tc_ * 512 + 512],
                 op=ALU.add).inc(oev, 1)
            BK[bn].release("vector")
            R.wait("scalar", oev)
            ost.dma("scalar", tc_,
                    out_c[j * 128:(j + 1) * 128, tc_ * 512:(tc_ + 1) * 512],
                    OSTG[tc_][:])
        w3_pe_after[j] = pe.v
    ost.wait_all("scalar")
    R.phases.append(("end", R.twc()))
    global _LAST_PHASES
    _LAST_PHASES = R.phases
    R.replay()
    return nc



_CACHE = {}
_LAST_PHASES = None


def _tile_wT(w, oc):
    t = w.reshape(oc, 128, 16, 128).transpose(3, 0, 2, 1)
    return np.ascontiguousarray(t.reshape(128, oc * 16 * 128))


def _tile_w3T(w3):
    t = w3.reshape(16, 128, 44, 128).transpose(3, 0, 2, 1)
    return np.ascontiguousarray(t.reshape(128, 16 * 44 * 128))


def kernel(L0_attn_x, L0_attn_slope_rate, L0_attn_W_qkvu, L0_attn_W_out,
           L0_ffn_w1, L0_ffn_w2, L0_ffn_w3):
    x = np.asarray(L0_attn_x, dtype=np.float32).reshape(8192, DM)
    slope = np.ascontiguousarray(
        np.asarray(L0_attn_slope_rate, dtype=np.float32).reshape(1, NH))
    Wq = np.asarray(L0_attn_W_qkvu, dtype=np.float32)
    Wo = np.asarray(L0_attn_W_out, dtype=np.float32)
    w1 = np.asarray(L0_ffn_w1, dtype=np.float32)
    w2 = np.asarray(L0_ffn_w2, dtype=np.float32)
    w3 = np.asarray(L0_ffn_w3, dtype=np.float32)

    xT = np.ascontiguousarray(x.T)
    wqkvu_t = _tile_wT(Wq, 64)
    wout_t = _tile_wT(Wo, 16)
    w1_t = _tile_wT(w1, 44)
    w2_t = _tile_wT(w2, 44)
    w3_t = _tile_w3T(w3)

    consts = make_consts()
    if "F" not in _CACHE:
        _CACHE["F"] = build_prog_fused(consts)
    ncF = _CACHE["F"]

    in_F = []
    for c in range(8):
        dist = np.full((128, 8), 1e9, dtype=np.float32)
        for j in range(c):
            dist[:, j] = 1024.0 * (c - 1 - j)
        in_F.append({
            "xT": np.ascontiguousarray(xT[:, c * NT:(c + 1) * NT]),
            "distc": dist,
            "slope": slope,
            "wqkvu": wqkvu_t,
            "wout": wout_t,
            "w1t": w1_t,
            "w2t": w2_t,
            "w3t": w3_t,
        })
    res = run_bass_kernel_spmd(ncF, in_F, core_ids=list(range(8)))

    out = np.concatenate([res.results[c]["out_c"] for c in range(8)], axis=1)
    return np.ascontiguousarray(out.T).reshape(1, 8192, DM).astype(np.float32)


if __name__ == "__main__":
    ncF = build_prog_fused(make_consts())
    print("build ok; sems:", ncF and "n/a")


# version 16 (final)
# speedup vs baseline: 1.5024x; 1.0024x over previous
import numpy as np
import ml_dtypes
import concourse.bass as bass
import concourse.mybir as mybir
from concourse.bass_utils import run_bass_kernel_spmd

F32 = mybir.dt.float32
BF16 = mybir.dt.bfloat16
AF = mybir.ActivationFunctionType
ALU = mybir.AluOpType
BF = ml_dtypes.bfloat16
AX = mybir.AxisListType.X

NT = 1024
DM = 2048
NH = 16
HD = 5632
FC = DM // 128
TC = NT // 128
NBLK = 4
HC = HD // 128
NW_ELEMS = float(HD * DM)
GCH = 528

ENGINES = ("sync", "scalar", "vector", "tensor", "gpsimd")


class Sem:
    __slots__ = ("h", "v", "name")

    def __init__(self, h, name):
        self.h, self.v, self.name = h, 0, name


class Op:
    __slots__ = ("meth", "args", "kwargs", "incs")

    def __init__(self, meth, args, kwargs):
        self.meth, self.args, self.kwargs, self.incs = meth, args, kwargs, []

    def inc(self, sem, n):
        assert not self.incs, f"op {self.meth} already has a sync update"
        self.incs.append((sem, n, sem.v + n))
        sem.v += n
        return self

    def tok(self):
        if self.incs:
            s, n, va = self.incs[0]
            return (s, va)
        return None


class Rec:
    def __init__(self, nc):
        self.nc = nc
        self.q = {e: [] for e in ENGINES}
        self._n = 0
        self._lastwait = {}

    def sem(self, name=None):
        self._n += 1
        nm = name or f"s{self._n}"
        return Sem(self.nc.alloc_semaphore(nm), nm)

    def op(self, eng, meth, *args, **kwargs):
        o = Op(meth, args, kwargs)
        self.q[eng].append(o)
        return o

    def wait(self, eng, sem, val=None):
        v = sem.v if val is None else val
        if v <= 0:
            return
        key = (eng, sem.name)
        if self._lastwait.get(key, -1) >= v:
            return
        self._lastwait[key] = v
        self.q[eng].append(Op("wait_ge", (sem.h, v), {}))

    def dma(self, eng, out, in_, sem, **kw):
        return self.op(eng, "dma_start", out=out, in_=in_, **kw).inc(sem, 16)

    @staticmethod
    def _interval(ap):
        ext = 1
        for (step, count) in ap.ap:
            ext += abs(int(step)) * (int(count) - 1)
        off = int(ap.offset)
        return (ap.tensor.name, off, off + ext)

    def _rw_of(self, o):
        if o.meth in ("wait_ge", "nop"):
            return [], []
        writes, reads = [], []
        args = list(o.args)
        kw = dict(o.kwargs)
        if o.meth == "memset":
            writes.append(self._interval(args[0]))
            return writes, reads
        if "out" in kw:
            writes.append(self._interval(kw.pop("out")))
        for v in list(args) + list(kw.values()):
            if isinstance(v, bass.AP):
                reads.append(self._interval(v))
        return writes, reads

    def _auto_chain(self):
        def overlap(a, b):
            return a[0] == b[0] and a[1] < b[2] and b[1] < a[2]

        for eng in ("vector", "scalar", "gpsimd"):
            ops = self.q[eng]
            chain = self.sem(f"chain_{eng}")
            writers = []
            readers = []
            n = len(ops)
            deps = [None] * n
            is_dma = [o.meth == "dma_start" for o in ops]
            for i, o in enumerate(ops):
                ws, rs = self._rw_of(o)
                need = set()
                for r in rs:
                    for (w, j) in writers:
                        if (not is_dma[j]) and overlap(r, w):
                            need.add(j)
                for w in ws:
                    for (pw, j) in writers:
                        if (not is_dma[j]) and overlap(w, pw):
                            need.add(j)
                    for (pr, j) in readers:
                        if (not is_dma[j]) and overlap(w, pr):
                            need.add(j)
                deps[i] = need
                for w in ws:
                    writers.append((w, i))
                for r in rs:
                    readers.append((r, i))
            producers = set()
            for need in deps:
                producers |= need
            for j in sorted(producers):
                if not ops[j].incs:
                    ops[j].inc(chain, 1)
            new_ops = []
            lastw = {}
            for i, o in enumerate(ops):
                if deps[i]:
                    for j in sorted(deps[i]):
                        t = ops[j].tok()
                        s, v = t
                        if lastw.get(s.name, -1) < v:
                            new_ops.append(Op("wait_ge", (s.h, v), {}))
                            lastw[s.name] = v
                new_ops.append(o)
            self.q[eng] = new_ops

    def replay(self):
        self._auto_chain()
        with self.nc.Block() as block:
            for eng in ENGINES:
                ops = self.q[eng]

                def runner(engine, ops=ops):
                    for o in ops:
                        r = getattr(engine, o.meth)(*o.args, **o.kwargs)
                        if o.incs:
                            s, nn_, _ = o.incs[0]
                            r.then_inc(s.h, nn_)

                getattr(block, eng)(runner)


class Bank:
    def __init__(self, R, t, name):
        self.R, self.t, self.name = R, t, name
        self.sem = R.sem(f"bk_{name}")
        self.tok = None

    def acquire(self, eng="tensor"):
        if self.tok is not None:
            self.R.wait(eng, self.tok[0], self.tok[1])

    def release(self, eng):
        o = self.R.q[eng][-1]
        if not o.incs:
            o.inc(self.sem, 1)
        self.tok = o.tok()


class Stream:
    def __init__(self, R, name, nslots):
        self.R = R
        self.sems = [R.sem(f"{name}_{i}") for i in range(nslots)]
        self.uses = [0] * nslots

    def dma(self, eng, slot, out, in_, **kw):
        self.uses[slot] += 1
        return self.R.op(eng, "dma_start", out=out, in_=in_, **kw)\
            .inc(self.sems[slot], 16)

    def wait(self, eng, slot, use=None):
        u = self.uses[slot] if use is None else use
        self.R.wait(eng, self.sems[slot], 16 * u)

    def wait_all(self, eng):
        for i, s in enumerate(self.sems):
            self.R.wait(eng, s, 16 * self.uses[i])


BF16_CONSTS = {"IDX0", "IDX1", "QDROW", "IDEN_BF", "ONESC_BF", "KD2"}


def make_consts():
    c = {}
    idx = []
    for k in range(2):
        s = 128 * k + np.arange(128)[:, None]
        m = np.arange(256)[None, :]
        d = (m - s).astype(np.float32)
        d[m < s] = 1e9
        idx.append(np.ascontiguousarray(d))
    c["IDX0"], c["IDX1"] = idx
    c["QDROW"] = np.broadcast_to(
        (np.arange(256, dtype=np.float32) + 1.0)[None, :], (128, 256)).copy()
    c["IDEN_BF"] = np.eye(128, dtype=np.float32)
    c["ONESC_BF"] = np.ones((128, 1), dtype=np.float32)
    kd = np.zeros((128, 2), dtype=np.float32)
    kd[:, 0] = 255.0 - np.arange(128)
    kd[:, 1] = 255.0 - (128 + np.arange(128))
    c["KD2"] = kd
    kp = np.zeros((128, 8), dtype=np.float32)
    for tb in range(8):
        kp[:, tb] = 1023.0 - (tb * 128 + np.arange(128))
    c["KDPRE"] = kp
    c["ONESR_F"] = np.ones((1, 128), dtype=np.float32)
    c["ONESC_F"] = np.ones((128, 1), dtype=np.float32)
    cc = np.zeros((128, 8), dtype=np.float32)
    cc[:, 0] = 1e-5
    cc[:, 1] = 1e-6
    cc[:, 2] = 1e-8
    cc[:, 3] = 1.0 / 2048.0
    cc[:, 4] = 256.0
    cc[:, 5] = 1024.0
    c["CCOL"] = cc
    c["CROW"] = np.ascontiguousarray(cc[0:1, :])
    return c


def load_consts(nc, R, names, consts, sem):
    t = {}
    buf = {}
    for k in names:
        arr = consts[k]
        if k in BF16_CONSTS:
            arr = arr.astype(BF)
            buf[k] = nc.alloc_sbuf_tensor(f"cb_{k}", list(arr.shape), BF16)
        else:
            buf[k] = nc.alloc_sbuf_tensor(f"cb_{k}", list(arr.shape), F32)
        t[k] = nc.inline_tensor(arr, name=f"c_{k}")
        R.dma("sync", buf[k][:], t[k].ap(), sem)
    return buf


def emit_layernorm(R, S, BK, src, dst, eps_row, src_sem, tag):
    pe, dve, act = R.s["pe"], R.s["dve"], R.s["act"]
    xsq = [S["scrB"][:, 0:1024], S["scrB"][:, 1024:2048]]
    ssum = [BK["b2"].t[0:1, 0:512], BK["b3"].t[0:1, 0:512]]
    ssq = [BK["b4"].t[0:1, 0:512], BK["b5"].t[0:1, 0:512]]
    for b in ("b2", "b3", "b4", "b5"):
        BK[b].acquire()
    R.wait("scalar", src_sem)
    R.wait("tensor", src_sem)
    sq_done = R.sem(f"lnsq{tag}")
    for fc in range(FC):
        srcc = src[:, fc * NT:(fc + 1) * NT]
        p = fc % 2
        if fc >= 2:
            R.wait("scalar", pe)
        R.op("scalar", "activation", out=xsq[p][:], in_=srcc, func=AF.Square)\
            .inc(sq_done, 1)
        R.wait("tensor", sq_done)
        for half in range(2):
            rsrc = src[:, fc * NT + half * 512: fc * NT + half * 512 + 512]
            R.op("tensor", "matmul", ssum[half], S["ONESC_BF"][:], rsrc,
                 start=(fc == 0), stop=(fc == FC - 1))
            o = R.op("tensor", "matmul", ssq[half], S["ONESC_BF"][:],
                     xsq[p][:, half * 512: half * 512 + 512],
                     start=(fc == 0), stop=(fc == FC - 1))
        o.inc(pe, 1)
    mean, ex2, tmpr = S["rowA"], S["rowB"], S["rowA2"]
    rstd = ex2
    R.wait("vector", pe)
    for half in range(2):
        sl = slice(half * 512, half * 512 + 512)
        R.op("vector", "tensor_scalar", out=mean[0:1, sl], in0=ssum[half],
             scalar1=1.0 / DM, scalar2=None, op0=ALU.mult)
        R.op("vector", "tensor_scalar", out=ex2[0:1, sl], in0=ssq[half],
             scalar1=1.0 / DM, scalar2=None, op0=ALU.mult)
    o = R.op("vector", "tensor_mul", out=tmpr[0:1, 0:1024],
             in0=mean[0:1, 0:1024], in1=mean[0:1, 0:1024])
    o = R.op("vector", "tensor_sub", out=ex2[0:1, 0:1024],
             in0=ex2[0:1, 0:1024], in1=tmpr[0:1, 0:1024])
    o.inc(dve, 1)
    for b in ("b2", "b3", "b4", "b5"):
        BK[b].release("vector")
    R.wait("scalar", dve)
    R.op("scalar", "activation", out=tmpr[0:1, 0:1024], in_=ex2[0:1, 0:1024],
         func=AF.Sqrt, bias=eps_row).inc(act, 1)
    R.wait("vector", act)
    R.op("vector", "reciprocal", out=rstd[0:1, 0:1024],
         in_=tmpr[0:1, 0:1024]).inc(dve, 1)
    for b in ("b2", "b3", "b4", "b5"):
        BK[b].acquire()
    R.wait("tensor", dve)
    bmap = (("b2", 0, True), ("b4", 1, True), ("b3", 0, False), ("b5", 1, False))
    for (bn, half, is_mean) in bmap:
        sl = slice(half * 512, half * 512 + 512)
        o = R.op("tensor", "matmul", BK[bn].t[:, 0:512], S["ONESR_F"][:],
                 (mean if is_mean else rstd)[0:1, sl], start=True, stop=True)
    o.inc(pe, 1)
    meanb = S["scrB"][:, 0:1024]
    rstdb = S["scrB"][:, 1024:2048]
    R.wait("vector", pe)
    for (bn, half, is_mean) in bmap:
        dstb = (meanb if is_mean else rstdb)
        o = R.op("vector", "tensor_copy",
                 out=dstb[:, half * 512: half * 512 + 512],
                 in_=BK[bn].t[:, 0:512])
    o.inc(dve, 1)
    for b in ("b2", "b3", "b4", "b5"):
        BK[b].release("vector")
    ttmp = S["scrB"][:, 2048:3072]
    done = R.sem(f"lnd{tag}")
    for fc in range(FC):
        srcc = src[:, fc * NT:(fc + 1) * NT]
        dstc = dst[:, fc * NT:(fc + 1) * NT]
        R.op("vector", "tensor_sub", out=ttmp[:], in0=srcc, in1=meanb)
        R.op("vector", "tensor_mul", out=dstc, in0=ttmp[:], in1=rstdb)\
            .inc(done, 1)
    return done


def common_setup(nc):
    R = Rec(nc)
    R.s = {k: R.sem(k) for k in ("pe", "dve", "act", "st")}
    BK = {}
    for i in range(6):
        BK[f"b{i}"] = Bank(R, nc.alloc_psum_tensor(f"pb{i}", [128, 512], F32), f"b{i}")
    BK["b6"] = Bank(R, nc.alloc_psum_tensor("pb6", [128, 512], F32), "b6")
    BK["b7"] = Bank(R, nc.alloc_psum_tensor("pb7", [128, 512], F32), "b7")
    return R, BK


def emit_slope_derived(R, S, BK, ld_const):
    pe, dve = R.s["pe"], R.s["dve"]
    sder = R.sem("sder")
    BK["b0"].acquire()
    R.wait("tensor", ld_const)
    R.op("tensor", "matmul", BK["b0"].t[:, 0:16], S["ONESR_F"][:],
         S["slope"][:], start=True, stop=True).inc(pe, 1)
    R.wait("vector", pe)
    R.op("vector", "tensor_scalar", out=S["sneg"][:], in0=BK["b0"].t[:, 0:16],
         scalar1=-1.0, scalar2=None, op0=ALU.mult).inc(dve, 1)
    BK["b0"].release("vector")
    R.wait("scalar", dve)
    R.wait("scalar", ld_const)
    R.op("scalar", "activation", out=S["bd_bc"][:], in_=S["sneg"][:],
         func=AF.Exp, scale=S["CCOL"][:, 4:5]).inc(sder, 1)
    for k in range(8):
        R.op("scalar", "activation", out=S["kvw"][:, k * NH:(k + 1) * NH],
             in_=S["sneg"][:], func=AF.Exp, scale=S["DIST"][:, k:k + 1])\
            .inc(sder, 1)
    for h in range(NH):
        R.op("scalar", "activation", out=S["kdb_all"][:, 2 * h:2 * h + 2],
             in_=S["KD2"][:], func=AF.Exp, scale=S["sneg"][:, h:h + 1])\
            .inc(sder, 1)
    for h in range(NH):
        R.op("scalar", "activation", out=S["kdpre_all"][:, 8 * h:8 * h + 8],
             in_=S["KDPRE"][:], func=AF.Exp, scale=S["sneg"][:, h:h + 1])\
            .inc(sder, 1)
    return sder


def build_prog_fused(consts):
    nc = bass.Bass("TRN2", target_bir_lowering=False, debug=False)
    xT = nc.dram_tensor("xT", [DM, NT], F32, kind="ExternalInput").ap()
    distc = nc.dram_tensor("distc", [128, 8], F32, kind="ExternalInput").ap()
    slope = nc.dram_tensor("slope", [1, NH], F32, kind="ExternalInput").ap()
    wqkvu = nc.dram_tensor("wqkvu", [128, 64 * 2048], F32, kind="ExternalInput").ap()
    wout = nc.dram_tensor("wout", [128, 16 * 2048], F32, kind="ExternalInput").ap()
    w1t = nc.dram_tensor("w1t", [128, 44 * 2048], F32, kind="ExternalInput").ap()
    w2t = nc.dram_tensor("w2t", [128, 44 * 2048], F32, kind="ExternalInput").ap()
    w3t = nc.dram_tensor("w3t", [128, 16 * 5632], F32, kind="ExternalInput").ap()
    out_c = nc.dram_tensor("out_c", [DM, NT], F32, kind="ExternalOutput").ap()
    u_spill = nc.dram_tensor("u_spill", [128, FC * NT], BF16, kind="Internal").ap()
    x1_spill = nc.dram_tensor("x1_spill", [128, FC * NT], BF16, kind="Internal").ap()
    l_out_t = nc.dram_tensor("l_out", [128, NH * 128], BF16, kind="Internal")
    l_all_t = nc.dram_tensor("l_all", [8 * 128, NH * 128], BF16, kind="Internal",
                             addr_space="Shared")

    R, BK = common_setup(nc)
    R.phases = []

    def _twc():
        return sum(2 if o.meth == "matmul" else 1 for o in R.q["tensor"])
    R.twc = _twc
    s = R.s
    pe, dve, act = s["pe"], s["dve"], s["act"]
    A = nc.alloc_sbuf_tensor
    AR1 = A("ar1", [128, FC * NT], BF16)
    AR2 = A("ar2", [128, FC * NT], BF16)
    AR3 = A("ar3", [128, FC * NT], BF16)
    AR4 = A("ar4", [128, FC * NT], BF16)
    SCRB = A("scrb", [128, 6144], BF16)
    SCRF = A("scrf", [128, 5120], F32)
    WGB = [A(f"wgb{i}", [128, 1024], BF16) for i in range(4)]
    scrf_addr = nc.lookup_mloc(SCRF).addr
    WGB += [nc.alloc_sbuf_tensor_at(f"wgbx{i}", [128, 1024], BF16,
                                    offset=scrf_addr + 16384 + i * 2048)
            for i in range(2)]
    WQ = A("wq", [128, 8192], BF16)
    KVOUT = A("kvout", [128, NH * 128], F32)
    S = {
        "slope": A("s_slope", [1, NH], F32),
        "DIST": A("s_dist", [128, 8], F32),
        "kvw": A("s_kvw", [128, 8 * NH], BF16),
        "sneg": A("s_sneg", [128, NH], F32),
        "bd_bc": A("s_bdbc", [128, NH], BF16),
        "kdb_all": A("s_kdball", [128, 2 * NH], F32),
        "kdpre_all": A("s_kdpall", [128, 8 * NH], F32),
        "rowA": A("s_rowA", [1, 1024], F32),
        "rowA2": A("s_rowA2", [1, 1024], F32),
        "rowB": A("s_rowB", [1, 1024], F32),
        "gacc": A("s_gacc", [128, 4], F32),
        "gtmp": A("s_gtmp", [128, 4], F32),
        "gsum": A("s_gsum", [1, 4], F32),
        "th": A("s_th", [1, 4], F32),
        "nth": A("s_nth", [1, 4], F32),
        "thb": A("s_thb", [128, 4], F32),
        "nthb": A("s_nthb", [128, 4], F32),
        "kvbf": SCRB[:, 2048:2176],
        "scrB": SCRB,
    }
    ld_const = R.sem("ldconst")
    CB = load_consts(nc, R,
                     ("IDX0", "IDX1", "QDROW", "KD2", "KDPRE", "ONESR_F",
                      "ONESC_F", "IDEN_BF", "ONESC_BF", "CCOL", "CROW"),
                     consts, ld_const)
    S.update(CB)
    R.dma("sync", S["slope"][:], slope, ld_const)
    R.dma("sync", S["DIST"][:], distc, ld_const)
    R.wait("vector", ld_const)
    R.wait("tensor", ld_const)
    sder = emit_slope_derived(R, S, BK, ld_const)

    wq_addr = nc.lookup_mloc(WQ).addr
    GPOOL = [nc.alloc_sbuf_tensor_at("gpool", [128, 512], F32,
                                     offset=wq_addr + i * 2048)
             for i in range(8)]
    gld = Stream(R, "gld", 8)
    gred = R.sem("gred")
    GLAG = 6
    gsrc = []
    for wi, wt in enumerate((w1t, w2t, w3t)):
        for kk in range(176):
            gsrc.append((wi, wt[:, kk * 512:(kk + 1) * 512]))
    gstate = {"dma": 0, "done": 0, "init": False, "tgt": 0.0}
    GT_XLN, GT_OWN_PROJ, GT_LKV, GT_OWN_ATTN, GT_B4 = 3.0, 2.0, 2.0, 8.0, 12.0

    def emit_gamma_dmas(upto):
        upto = min(upto, gstate["done"] + GLAG, GCH)
        while gstate["dma"] < upto:
            nch = gstate["dma"]
            if nch >= 8:
                R.wait("gpsimd", gred, nch - 7)
            gld.dma("gpsimd", nch % 8, GPOOL[nch % 8][:], gsrc[nch][1])
            gstate["dma"] += 1

    def emit_gamma_chunks(upto):
        if not gstate["init"]:
            R.op("vector", "memset", S["gacc"][:], 0.0)
            gstate["init"] = True
        while gstate["done"] < min(upto, GCH):
            nch = gstate["done"]
            emit_gamma_dmas(nch + GLAG)
            wi = gsrc[nch][0]
            gld.wait("vector", nch % 8, nch // 8 + 1)
            R.op("vector", "tensor_reduce", out=S["gtmp"][:, wi:wi + 1],
                 in_=GPOOL[nch % 8][:], axis=AX, op=ALU.add,
                 apply_absolute_value=True)
            R.op("vector", "tensor_tensor", out=S["gacc"][:, wi:wi + 1],
                 in0=S["gacc"][:, wi:wi + 1], in1=S["gtmp"][:, wi:wi + 1],
                 op=ALU.add).inc(gred, 1)
            gstate["done"] += 1
        emit_gamma_dmas(gstate["done"] + GLAG)

    def gtick(rate):
        gstate["tgt"] += rate
        emit_gamma_chunks(min(int(gstate["tgt"]), GCH))

    ldx = Stream(R, "ldx", 2)
    xc = [SCRF[:, 0:1024], SCRF[:, 1024:2048]]
    cast_done = R.sem("castdone")

    def emit_x_chunk(src_cols, guard_pe):
        base = cast_done.v
        for fc in range(FC):
            p = fc % 2
            if fc >= 2:
                R.wait("sync", cast_done, base + fc - 1)
            elif base > 0:
                R.wait("sync", cast_done, base)
            ldx.dma("sync", p, xc[p][:], src_cols(fc))
            ldx.wait("vector", p)
            if guard_pe and fc < 2:
                R.wait("vector", pe, guard_pe)
            R.op("vector", "tensor_copy", out=AR1[:, fc * NT:(fc + 1) * NT],
                 in_=xc[p][:]).inc(cast_done, 1)
            gtick(GT_XLN)

    wld = Stream(R, "wld", 2)
    wcast = R.sem("wcast")
    WG = [SCRF[:, 2048:3072], SCRF[:, 3072:4096]]
    sevq = R.sem("sevq")
    vev = R.sem("vev")
    trev = R.sem("trev")
    sevu = R.sem("sevu")
    uev = Stream(R, "uev", 2)
    vstg = [SCRB[:, 3072 + i * 512: 3584 + i * 512] for i in range(2)]
    ustg = [SCRB[:, 4096 + i * 512: 4608 + i * 512] for i in range(2)]
    PRM = {"g": 0, "vtr_pe": {}, "un": 0, "stg4": {}}

    def emit_proj_phase(granules, wsrc, src, ln_done, prev_guard_pe,
                        grate=0.0):
        ng = len(granules)
        gp_after = {}
        wcast_after = {}

        def wload(gq):
            oc = granules[gq][0]
            for half in range(2):
                gg = PRM["g"]
                PRM["g"] += 1
                bgi = gg % 6
                if gg >= 2:
                    R.wait("sync", wcast, gg - 1)
                wld.dma("sync", gg % 2, WG[gg % 2][:],
                        wsrc[:, oc * 2048 + half * 1024: oc * 2048 + (half + 1) * 1024])
                wld.wait("scalar", gg % 2)
                if gq >= 3:
                    R.wait("scalar", pe, gp_after[gq - 3])
                elif prev_guard_pe:
                    R.wait("scalar", pe, prev_guard_pe)
                R.op("scalar", "activation", out=WGB[bgi][:],
                     in_=WG[gg % 2][:], func=AF.Copy).inc(wcast, 1)
            wcast_after[gq] = wcast.v

        wload(0)
        wload(1)
        for j in range(ng):
            if j + 2 < ng:
                wload(j + 2)
            R.wait("tensor", wcast, wcast_after[j])
            if j == 0 and ln_done is not None:
                R.wait("tensor", ln_done, FC)
            _, kind, jj, dst = granules[j]
            b_lo, b_hi = WGB[(2 * j) % 6], WGB[(2 * j + 1) % 6]
            pnames = ("b0", "b1") if j % 2 == 0 else ("b2", "b3")
            for tc_ in range(2):
                bn = pnames[tc_]
                BK[bn].acquire()
                for kc in range(16):
                    wb = b_lo if kc < 8 else b_hi
                    off = (kc % 8) * 128
                    o = R.op("tensor", "matmul", BK[bn].t[:, 0:512],
                             wb[:, off:off + 128],
                             src[:, kc * NT + tc_ * 512: kc * NT + tc_ * 512 + 512],
                             start=(kc == 0), stop=(kc == 15))
                o.inc(pe, 1)
                pe_now = pe.v
                if kind in ("q", "k"):
                    dslice = dst[:, jj * NT + tc_ * 512: jj * NT + tc_ * 512 + 512]
                    R.wait("scalar", pe, pe_now)
                    R.op("scalar", "activation", out=dslice,
                         in_=BK[bn].t[:, 0:512], func=AF.Silu).inc(sevq, 1)
                    BK[bn].release("scalar")
                elif kind == "v":
                    R.wait("vector", pe, pe_now)
                    key = tc_
                    if key in PRM["vtr_pe"]:
                        R.wait("vector", pe, PRM["vtr_pe"][key])
                    R.op("vector", "tensor_copy", out=vstg[tc_][:],
                         in_=BK[bn].t[:, 0:512]).inc(vev, 1)
                    BK[bn].release("vector")
                    BK["b6"].acquire()
                    R.wait("tensor", vev)
                    for i4 in range(4):
                        o = R.op("tensor", "matmul",
                                 BK["b6"].t[:, i4 * 128:(i4 + 1) * 128],
                                 vstg[tc_][:, i4 * 128:(i4 + 1) * 128],
                                 S["IDEN_BF"][:], start=True, stop=True)
                    o.inc(pe, 1)
                    PRM["vtr_pe"][key] = pe.v
                    R.wait("vector", pe, pe.v)
                    for i4 in range(4):
                        tch = tc_ * 4 + i4
                        R.op("vector", "tensor_copy",
                             out=dst[:, tch * DM + jj * 128: tch * DM + (jj + 1) * 128],
                             in_=BK["b6"].t[:, i4 * 128:(i4 + 1) * 128]).inc(trev, 1)
                    BK["b6"].release("vector")
                else:
                    un = PRM["un"]
                    PRM["un"] += 1
                    R.wait("vector", pe, pe_now)
                    if un >= 2:
                        uev.wait("vector", un % 2, un // 2)
                    R.op("vector", "tensor_copy", out=ustg[un % 2][:],
                         in_=BK[bn].t[:, 0:512]).inc(sevu, 1)
                    BK[bn].release("vector")
                    R.wait("scalar", sevu)
                    uev.dma("scalar", un % 2,
                            u_spill[:, jj * NT + tc_ * 512: jj * NT + tc_ * 512 + 512],
                            ustg[un % 2][:])
            gp_after[j] = pe.v
            if grate:
                gtick(grate)
        return gp_after, pe.v

    sktok = R.sem("sktok")
    skv = R.sem("skv")

    def emit_local_kv(k_evac_total, v_evac_total):
        R.wait("tensor", sevq, k_evac_total)
        R.wait("tensor", trev, v_evac_total)
        R.wait("vector", sder)
        uev.wait_all("vector")
        stg4 = [SCRB[:, 2048 + i * 512: 2560 + i * 512] for i in range(8)]
        trb = ("b6", "b0")
        C_tok = {}
        sk_after = {}

        def emit_tr(h):
            for half in range(2):
                slot = (h % 4) * 2 + half
                BK[trb[half]].acquire()
                for i4 in range(4):
                    tb = half * 4 + i4
                    o = R.op("tensor", "matmul",
                             BK[trb[half]].t[:, i4 * 128:(i4 + 1) * 128],
                             AR3[:, h * NT + tb * 128: h * NT + (tb + 1) * 128],
                             S["IDEN_BF"][:], start=True, stop=True)
                o.inc(pe, 1)
                R.wait("vector", pe, pe.v)
                g = PRM["stg4"].get(slot)
                if g:
                    R.wait("vector", pe, g)
                for i4 in range(4):
                    tb = half * 4 + i4
                    o = R.op("vector", "tensor_scalar",
                             out=stg4[slot][:, i4 * 128:(i4 + 1) * 128],
                             in0=BK[trb[half]].t[:, i4 * 128:(i4 + 1) * 128],
                             scalar1=S["kdpre_all"][:, h * 8 + tb: h * 8 + tb + 1],
                             scalar2=None, op0=ALU.mult)
                o.inc(sktok, 1)
                BK[trb[half]].release("vector")
            sk_after[h] = sktok.v

        emit_tr(0)
        emit_tr(1)
        emit_tr(2)
        for h in range(NH):
            if h + 3 < NH:
                emit_tr(h + 3)
            cb = BK["b7"] if h % 2 == 0 else BK["b1"]
            cb.acquire()
            R.wait("tensor", sktok, sk_after[h])
            for tb in range(8):
                slot = (h % 4) * 2 + tb // 4
                o = R.op("tensor", "matmul", cb.t[:, 0:128],
                         stg4[slot][:, (tb % 4) * 128:(tb % 4) * 128 + 128],
                         AR4[:, tb * DM + h * 128: tb * DM + (h + 1) * 128],
                         start=(tb == 0), stop=(tb == 7))
            o.inc(pe, 1)
            PRM["stg4"][(h % 4) * 2] = PRM["stg4"][(h % 4) * 2 + 1] = pe.v
            C_tok[h] = pe.v
            R.wait("vector", pe, C_tok[h])
            kvslice = KVOUT[:, h * 128:(h + 1) * 128]
            R.op("vector", "tensor_copy", out=kvslice,
                 in_=cb.t[:, 0:128]).inc(skv, 1)
            cb.release("vector")
            gtick(GT_LKV)
        return pe.v

    emit_x_chunk(lambda fc: xT[fc * 128:(fc + 1) * 128, :], None)
    R.phases.append(("own_ln", R.twc()))
    ln1 = emit_layernorm(R, S, BK, AR1, AR1, S["CROW"][0:1, 0:1],
                         cast_done, "1")
    gran = [(j, "q", j, AR2) for j in range(16)] + \
           [(16 + j, "k", j, AR3) for j in range(16)] + \
           [(48 + j, "u", j, None) for j in range(16)] + \
           [(32 + j, "v", j, AR4) for j in range(16)]
    R.phases.append(("own_proj", R.twc()))
    gp_own, own_tail = emit_proj_phase(gran, wqkvu, AR1, ln1, None,
                                       GT_OWN_PROJ)
    q_total, k_total, v_total = sevq.v, sevq.v, trev.v

    R.phases.append(("lkv", R.twc()))
    lkv_pe = emit_local_kv(sevq.v, trev.v)

    R.phases.append(("cc", R.twc()))
    lsp = R.sem("lsp")
    lod = R.sem("lod")
    ccs = R.sem("ccs")
    lin = R.sem("lin")
    cmb = R.sem("cmb")
    LBF = SCRB[:, 0:2048]
    LRK = [SCRB[:, 2048:4096], SCRB[:, 4096:6144]]
    R.wait("vector", skv)
    R.op("vector", "tensor_copy", out=LBF, in_=KVOUT[:]).inc(lsp, 1)
    R.op("vector", "memset", KVOUT[:], 0.0)
    R.wait("gpsimd", lsp, 1)
    R.dma("gpsimd", l_out_t.ap(), LBF, lod)
    R.wait("gpsimd", lod, 16)
    R.op("gpsimd", "collective_compute", "AllGather", ALU.bypass,
         replica_groups=[list(range(8))],
         ins=[l_out_t.ap().opt()], outs=[l_all_t.ap().opt()]).inc(ccs, 1)
    R.wait("gpsimd", ccs, 1)
    R.wait("gpsimd", pe, lkv_pe)
    l_all = l_all_t.ap()
    for r in range(8):
        if r >= 2:
            R.wait("gpsimd", cmb, r - 1)
        R.dma("gpsimd", LRK[r % 2][:], l_all[r * 128:(r + 1) * 128, :], lin)
    R.wait("vector", sder)
    for r in range(8):
        R.wait("vector", lin, 16 * (r + 1))
        for h in range(NH):
            o = R.op("vector", "scalar_tensor_tensor",
                     out=KVOUT[:, h * 128:(h + 1) * 128],
                     in0=LRK[r % 2][:, h * 128:(h + 1) * 128],
                     scalar=S["kvw"][:, r * NH + h: r * NH + h + 1],
                     in1=KVOUT[:, h * 128:(h + 1) * 128],
                     op0=ALU.mult, op1=ALU.add)
        o.inc(cmb, 1)

    qkmb = [SCRB[:, 3072 + i * 512: 3584 + i * 512] for i in range(2)]
    R.phases.append(("own_attn", R.twc()))
    maskb = [SCRB[:, 4096 + i * 512: 4608 + i * 512] for i in range(2)]
    qdb = [SCRB[:, 5120 + i * 256: 5376 + i * 256] for i in range(2)]
    ktokb = [SCRB[:, 5632 + i * 256: 5888 + i * 256] for i in range(2)]
    smask = R.sem("smask")
    sqs = R.sem("sqs")
    skvbf = R.sem("skvbf")
    sqkm = R.sem("sqkm")
    soev = R.sem("soev")
    uload = R.sem("uload")
    sq2 = R.sem("sq2")
    osq = [SCRB[:, 0:1024], SCRB[:, 1024:2048]]
    stats_pe = {}
    BK["b4"].acquire()
    BK["b5"].acquire()
    R.wait("vector", pe, own_tail)
    R.wait("vector", sder)
    R.wait("tensor", sevq, q_total)
    R.wait("tensor", trev, v_total)
    R.wait("vector", sevq, q_total)
    uev.wait_all("scalar")
    R.wait("scalar", cmb, 8)
    C_pe_after = {}
    diag_pe_after = {}
    for h in range(NH):
        hp = h % 2
        R.wait("scalar", sder)
        if h >= 2:
            R.wait("scalar", sqkm, 4 * (h - 1))
            R.wait("scalar", sqs, h - 1)
        R.op("scalar", "activation", out=maskb[hp][:, 0:256], in_=S["IDX0"][:],
             func=AF.Exp, scale=S["sneg"][:, h:h + 1])
        R.op("scalar", "activation", out=maskb[hp][:, 256:512], in_=S["IDX1"][:],
             func=AF.Exp, scale=S["sneg"][:, h:h + 1])
        R.op("scalar", "activation", out=qdb[hp][:], in_=S["QDROW"][:],
             func=AF.Exp, scale=S["sneg"][:, h:h + 1]).inc(smask, 1)
        R.wait("vector", smask, h + 1)
        qd = qdb[hp]
        qrep = bass.AP(qd.tensor, qd.offset, [[qd.ap[0][0], 128], [0, 4], [1, 256]])
        R.op("vector", "tensor_tensor",
             out=AR1[:, h * NT:(h + 1) * NT].rearrange("p (b m) -> p b m", b=4),
             in0=AR2[:, h * NT:(h + 1) * NT].rearrange("p (b m) -> p b m", b=4),
             in1=qrep, op=ALU.mult).inc(sqs, 1)
        kvslice = KVOUT[:, h * 128:(h + 1) * 128]
        R.op("vector", "tensor_copy", out=S["kvbf"][:], in_=kvslice)\
            .inc(skvbf, 1)
        for i in range(NBLK):
            blk = h * NBLK + i
            BK["b6"].acquire()
            o = R.op("tensor", "matmul", BK["b6"].t[:, 0:128],
                     AR3[:, h * NT + i * 256: h * NT + i * 256 + 128],
                     S["IDEN_BF"][:], start=True, stop=True)
            o = R.op("tensor", "matmul", BK["b6"].t[:, 128:256],
                     AR3[:, h * NT + i * 256 + 128: h * NT + i * 256 + 256],
                     S["IDEN_BF"][:], start=True, stop=True)
            o.inc(pe, 1)
            tr_pe = pe.v
            R.wait("vector", pe, tr_pe)
            prevu = blk - 1 if i > 0 else ((h - 2) * NBLK + 3 if h >= 2 else None)
            if prevu is not None and prevu in C_pe_after:
                R.wait("vector", pe, C_pe_after[prevu])
            R.op("vector", "tensor_scalar", out=ktokb[hp][:, 0:128],
                 in0=BK["b6"].t[:, 0:128],
                 scalar1=S["kdb_all"][:, 2 * h:2 * h + 1],
                 scalar2=None, op0=ALU.mult)
            R.op("vector", "tensor_scalar", out=ktokb[hp][:, 128:256],
                 in0=BK["b6"].t[:, 128:256],
                 scalar1=S["kdb_all"][:, 2 * h + 1:2 * h + 2],
                 scalar2=None, op0=ALU.mult).inc(sktok, 1)
            BK["b6"].release("vector")
            BK["b7"].acquire()
            R.wait("tensor", sktok)
            tch = 2 * i
            R.op("tensor", "matmul", BK["b7"].t[:, 0:128],
                 ktokb[hp][:, 0:128],
                 AR4[:, tch * DM + h * 128: tch * DM + (h + 1) * 128],
                 start=True, stop=False)
            R.op("tensor", "matmul", BK["b7"].t[:, 0:128],
                 ktokb[hp][:, 128:256],
                 AR4[:, (tch + 1) * DM + h * 128: (tch + 1) * DM + (h + 1) * 128],
                 start=False, stop=True).inc(pe, 1)
            C_pe_after[blk] = pe.v
            qkbn = "b0" if i % 2 == 0 else "b3"
            BK[qkbn].acquire()
            R.op("tensor", "matmul", BK[qkbn].t[:, 0:256],
                 AR3[:, h * NT + i * 256: h * NT + i * 256 + 128],
                 AR2[:, h * NT + i * 256: h * NT + (i + 1) * 256],
                 start=True, stop=True)
            R.op("tensor", "matmul", BK[qkbn].t[:, 256:512],
                 AR3[:, h * NT + i * 256 + 128: h * NT + i * 256 + 256],
                 AR2[:, h * NT + i * 256: h * NT + (i + 1) * 256],
                 start=True, stop=True).inc(pe, 1)
            qk_pe = pe.v
            R.wait("vector", pe, qk_pe)
            prevq = blk - 1 if i > 0 else ((h - 2) * NBLK + 3 if h >= 2 else None)
            if prevq is not None and prevq in diag_pe_after:
                R.wait("vector", pe, diag_pe_after[prevq])
            R.op("vector", "tensor_tensor", out=qkmb[hp][:, 0:256],
                 in0=BK[qkbn].t[:, 0:256], in1=maskb[hp][:, 0:256], op=ALU.mult)
            R.op("vector", "tensor_tensor", out=qkmb[hp][:, 256:512],
                 in0=BK[qkbn].t[:, 256:512], in1=maskb[hp][:, 256:512],
                 op=ALU.mult).inc(sqkm, 1)
            BK[qkbn].release("vector")
            ob = "b1" if i % 2 == 0 else "b2"
            BK[ob].acquire()
            R.wait("tensor", skvbf, 4 * h + i + 1)
            R.wait("tensor", sqs, h + 1)
            R.op("tensor", "matmul", BK[ob].t[:, 0:256], S["kvbf"][:],
                 AR1[:, h * NT + i * 256: h * NT + (i + 1) * 256],
                 start=True, stop=False)
            R.wait("tensor", sqkm, blk + 1)
            assert skvbf.v == 4 * h + i + 1
            R.op("tensor", "matmul", BK[ob].t[:, 0:256],
                 AR4[:, tch * DM + h * 128: tch * DM + (h + 1) * 128],
                 qkmb[hp][:, 0:256], start=False, stop=False)
            R.op("tensor", "matmul", BK[ob].t[:, 0:256],
                 AR4[:, (tch + 1) * DM + h * 128: (tch + 1) * DM + (h + 1) * 128],
                 qkmb[hp][:, 256:512], start=False, stop=True).inc(pe, 1)
            diag_pe_after[blk] = pe.v
            R.wait("vector", pe, pe.v)
            R.op("vector", "scalar_tensor_tensor", out=kvslice, in0=kvslice,
                 scalar=S["bd_bc"][:, h:h + 1], in1=BK["b7"].t[:, 0:128],
                 op0=ALU.mult, op1=ALU.add).inc(skv, 1)
            BK["b7"].release("vector")
            if i < NBLK - 1:
                R.op("vector", "tensor_copy", out=S["kvbf"][:], in_=kvslice)\
                    .inc(skvbf, 1)
            R.op("vector", "tensor_copy",
                 out=AR2[:, h * NT + i * 256: h * NT + (i + 1) * 256],
                 in_=BK[ob].t[:, 0:256]).inc(soev, 1)
            BK[ob].release("vector")
            if i == NBLK - 1:
                R.wait("scalar", soev)
                if h >= 2:
                    R.wait("scalar", pe, stats_pe[h - 2])
                R.op("scalar", "activation", out=osq[h % 2][:],
                     in_=AR2[:, h * NT:(h + 1) * NT], func=AF.Square)\
                    .inc(sq2, 1)
                if h >= 1:
                    hh = h - 1
                    R.wait("tensor", sq2, hh + 1)
                    for half in range(2):
                        o = R.op("tensor", "matmul",
                                 (BK["b4"] if half == 0 else BK["b5"]).t[0:1, 0:512],
                                 S["ONESC_BF"][:],
                                 osq[hh % 2][:, half * 512: half * 512 + 512],
                                 start=(hh == 0), stop=False)
                    o.inc(pe, 1)
                    stats_pe[hh] = pe.v
                if h == 7:
                    uev.wait_all("sync")
                    R.wait("sync", pe, diag_pe_after[31])
                    for fc in range(0, FC, 2):
                        R.dma("sync", AR4[:, fc * NT:(fc + 1) * NT],
                              u_spill[:, fc * NT:(fc + 1) * NT], uload)
                if h == 15:
                    R.wait("sync", pe, diag_pe_after[63])
                    for fc in range(1, FC, 2):
                        R.dma("sync", AR4[:, fc * NT:(fc + 1) * NT],
                              u_spill[:, fc * NT:(fc + 1) * NT], uload)
                gtick(GT_OWN_ATTN)

    R.phases.append(("B3", R.twc()))
    R.wait("tensor", sq2, 16)
    for half in range(2):
        o = R.op("tensor", "matmul",
                 (BK["b4"] if half == 0 else BK["b5"]).t[0:1, 0:512],
                 S["ONESC_BF"][:], osq[15 % 2][:, half * 512: half * 512 + 512],
                 start=False, stop=True)
    o.inc(pe, 1)
    stats_pe[15] = pe.v
    R.wait("scalar", pe, stats_pe[15])
    R.op("scalar", "activation", out=S["rowA2"][0:1, 0:512],
         in_=BK["b4"].t[0:1, 0:512], func=AF.Sqrt,
         scale=S["CROW"][0:1, 3:4], bias=S["CROW"][0:1, 1:2])
    R.op("scalar", "activation", out=S["rowA2"][0:1, 512:1024],
         in_=BK["b5"].t[0:1, 0:512], func=AF.Sqrt,
         scale=S["CROW"][0:1, 3:4], bias=S["CROW"][0:1, 1:2]).inc(act, 1)
    BK["b4"].release("scalar")
    BK["b5"].release("scalar")
    R.wait("vector", act)
    R.op("vector", "reciprocal", out=S["rowA"][0:1, 0:1024],
         in_=S["rowA2"][0:1, 0:1024]).inc(dve, 1)
    BK["b4"].acquire()
    BK["b5"].acquire()
    R.wait("tensor", dve)
    R.op("tensor", "matmul", BK["b4"].t[:, 0:512], S["ONESR_F"][:],
         S["rowA"][0:1, 0:512], start=True, stop=True)
    R.op("tensor", "matmul", BK["b5"].t[:, 0:512], S["ONESR_F"][:],
         S["rowA"][0:1, 512:1024], start=True, stop=True).inc(pe, 1)
    rstdb = SCRB[:, 2048:3072]
    R.wait("vector", pe)
    R.op("vector", "tensor_copy", out=rstdb[:, 0:512], in_=BK["b4"].t[:, 0:512])
    R.op("vector", "tensor_copy", out=rstdb[:, 512:1024],
         in_=BK["b5"].t[:, 0:512]).inc(dve, 1)
    BK["b4"].release("vector")
    BK["b5"].release("vector")


    gdone = R.sem("gdone")
    ttmp = SCRB[:, 3072:4096]
    R.wait("vector", uload, 16 * 16)
    for fc in range(FC):
        R.op("vector", "tensor_mul", out=ttmp[:],
             in0=AR2[:, fc * NT:(fc + 1) * NT], in1=rstdb[:])
        R.op("vector", "tensor_mul", out=AR3[:, fc * NT:(fc + 1) * NT],
             in0=ttmp[:], in1=AR4[:, fc * NT:(fc + 1) * NT]).inc(gdone, 1)

    R.phases.append(("B4", R.twc()))
    xch = [SCRF[:, 0:1024], SCRF[:, 1024:2048]]
    x1done = R.sem("x1done")
    gp_B4 = {}
    b4_wc = {}

    b4_ldx_use = {}

    def emit_wload4(jq):
        p = jq % 2
        if jq >= 2:
            R.wait("sync", x1done, 2 * (jq - 1))
        ldx.dma("sync", p, xch[p][:], xT[jq * 128:(jq + 1) * 128, :])
        b4_ldx_use[jq] = ldx.uses[p]
        for half in range(2):
            gg = PRM["g"]
            PRM["g"] += 1
            if gg >= 2:
                R.wait("sync", wcast, gg - 1)
            wld.dma("sync", gg % 2, WG[gg % 2][:],
                    wout[:, jq * 2048 + half * 1024: jq * 2048 + (half + 1) * 1024])
            wld.wait("scalar", gg % 2)
            if jq >= 3:
                R.wait("scalar", pe, gp_B4[jq - 3])
            else:
                R.wait("scalar", pe, own_tail)
            R.op("scalar", "activation", out=WGB[gg % 6][:],
                 in_=WG[gg % 2][:], func=AF.Copy).inc(wcast, 1)
        b4_wc[jq] = wcast.v

    R.wait("sync", cast_done)
    b4_g0 = PRM["g"]
    emit_wload4(0)
    emit_wload4(1)
    for j in range(16):
        p = j % 2
        if j + 2 < 16:
            emit_wload4(j + 2)
        R.wait("tensor", wcast, b4_wc[j])
        if j == 0:
            R.wait("tensor", gdone, FC)
        b_lo = WGB[(b4_g0 + 2 * j) % 6]
        b_hi = WGB[(b4_g0 + 2 * j + 1) % 6]
        pnames = ("b0", "b1") if j % 2 == 0 else ("b2", "b3")
        for tc_ in range(2):
            bn = pnames[tc_]
            BK[bn].acquire()
            for kc in range(16):
                wb = b_lo if kc < 8 else b_hi
                off = (kc % 8) * 128
                o = R.op("tensor", "matmul", BK[bn].t[:, 0:512],
                         wb[:, off:off + 128],
                         AR3[:, kc * NT + tc_ * 512: kc * NT + tc_ * 512 + 512],
                         start=(kc == 0), stop=(kc == 15))
            o.inc(pe, 1)
            R.wait("vector", pe)
            ldx.wait("vector", p, b4_ldx_use[j])
            R.op("vector", "tensor_tensor",
                 out=AR2[:, j * NT + tc_ * 512: j * NT + tc_ * 512 + 512],
                 in0=BK[bn].t[:, 0:512],
                 in1=xch[p][:, tc_ * 512: tc_ * 512 + 512],
                 op=ALU.add).inc(x1done, 1)
            BK[bn].release("vector")
        gp_B4[j] = pe.v
        gtick(GT_B4)

    R.phases.append(("B5_ln2", R.twc()))
    ln2_done = emit_layernorm(R, S, BK, AR2, AR1, S["CROW"][0:1, 0:1],
                              x1done, "2")
    spill = R.sem("spill")
    R.wait("scalar", ln2_done)
    R.dma("scalar", x1_spill, AR2[:], spill)

    R.phases.append(("thresh", R.twc()))
    emit_gamma_chunks(GCH)
    BK["b7"].acquire()
    R.wait("tensor", gred, GCH)
    R.op("tensor", "matmul", BK["b7"].t[0:1, 0:4], S["ONESC_F"][:],
         S["gacc"][:], start=True, stop=True).inc(pe, 1)
    R.wait("vector", pe)
    R.op("vector", "tensor_copy", out=S["gsum"][:], in_=BK["b7"].t[0:1, 0:4])\
        .inc(dve, 1)
    BK["b7"].release("vector")
    R.op("vector", "tensor_scalar", out=S["th"][:], in0=S["gsum"][:],
         scalar1=1.0 / NW_ELEMS, scalar2=1e-8, op0=ALU.mult, op1=ALU.add)
    R.op("vector", "tensor_scalar", out=S["th"][:], in0=S["th"][:],
         scalar1=0.5, scalar2=None, op0=ALU.mult)
    R.op("vector", "tensor_scalar", out=S["nth"][:], in0=S["th"][:],
         scalar1=-1.0, scalar2=None, op0=ALU.mult).inc(dve, 1)
    BK["b5"].acquire()
    R.wait("tensor", dve)
    R.op("tensor", "matmul", BK["b5"].t[:, 0:4], S["ONESR_F"][:], S["th"][:],
         start=True, stop=True)
    R.op("tensor", "matmul", BK["b5"].t[:, 4:8], S["ONESR_F"][:], S["nth"][:],
         start=True, stop=True).inc(pe, 1)
    R.wait("vector", pe)
    R.op("vector", "tensor_copy", out=S["thb"][:], in_=BK["b5"].t[:, 0:4])
    R.op("vector", "tensor_copy", out=S["nthb"][:], in_=BK["b5"].t[:, 4:8])\
        .inc(dve, 1)
    BK["b5"].release("vector")
    th_ready = dve.v

    def hidden_ap(kc, lo, width):
        if kc < 16:
            return AR2[:, kc * NT + lo: kc * NT + lo + width]
        if kc < 32:
            return AR4[:, (kc - 16) * NT + lo: (kc - 16) * NT + lo + width]
        return AR3[:, (kc - 32) * NT + lo: (kc - 32) * NT + lo + width]

    R.phases.append(("B7", R.twc()))
    f32bufs = {"w1": [SCRF[:, 0:1024], SCRF[:, 1024:2048]],
               "w2": [SCRF[:, 2048:3072], SCRF[:, 3072:4096]]}
    wqbufs = {"w1": [[WQ[:, 0:1024], WQ[:, 1024:2048]],
                     [WQ[:, 4096:5120], WQ[:, 5120:6144]]],
              "w2": [[WQ[:, 2048:3072], WQ[:, 3072:4096]],
                     [WQ[:, 6144:7168], WQ[:, 7168:8192]]]}
    tsD = SCRB[:, 0:2048]
    tsP = SCRB[:, 2048:4096]
    sgb = [SCRB[:, 4096 + i * 512: 4608 + i * 512] for i in range(4)]
    fld = {"w1": Stream(R, "fld1", 2), "w2": Stream(R, "fld2", 2)}
    qbar = R.sem("qbarD")
    qbar_n = [0]
    fq = {"w1": R.sem("fq1"), "w2": R.sem("fq2")}
    sgd = R.sem("sgd")
    hid = R.sem("hid")
    wsrc = {"w1": w1t, "w2": w2t}
    thcol = {"w1": 0, "w2": 1, "w3": 2}
    qtok = {}
    R.wait("sync", spill, 16)
    R.wait("sync", ln2_done)
    R.wait("vector", spill, 16)
    R.wait("gpsimd", spill, 16)
    R.wait("vector", gdone)
    uev.wait_all("scalar")
    mm_pe_after = {}

    def emit_quant7(jq):
        for wname in ("w1", "w2"):
            ts = tsD if wname == "w1" else tsP
            for half in range(2):
                if (wname, half) in qtok:
                    R.wait("sync", qbar, qtok[(wname, half)])
                fld[wname].dma("sync", half, f32bufs[wname][half][:],
                               wsrc[wname][:, jq * 2048 + half * 1024: jq * 2048 + (half + 1) * 1024])
                fld[wname].wait("vector", half)
                R.wait("vector", dve, th_ready)
                if 2 * jq + half >= 1:
                    R.wait("vector", fq[wname], 2 * jq + half)
                col = thcol[wname]
                src = f32bufs[wname][half][:]
                dst = wqbufs[wname][jq % 2][half][:]
                R.op("vector", "tensor_scalar", out=ts[:, 0:1024], in0=src,
                     scalar1=S["thb"][:, col:col + 1], scalar2=None,
                     op0=ALU.is_ge).inc(qbar, 1)
                R.op("vector", "tensor_scalar", out=ts[:, 1024:2048], in0=src,
                     scalar1=S["nthb"][:, col:col + 1], scalar2=None,
                     op0=ALU.is_le).inc(qbar, 1)
                qbar_n[0] += 2
                qtok[(wname, half)] = qbar_n[0]
                R.wait("gpsimd", qbar, qbar_n[0])
                if jq >= 2 and (jq - 2, wname) in mm_pe_after:
                    R.wait("gpsimd", pe, mm_pe_after[(jq - 2, wname)])
                R.op("gpsimd", "tensor_tensor", out=dst, in0=ts[:, 0:1024],
                     in1=ts[:, 1024:2048], op=ALU.subtract).inc(fq[wname], 1)

    w2banks = [("b4", "b5"), ("b6", "b7")]
    emit_quant7(0)
    for j in range(44):
        if j + 1 < 44:
            emit_quant7(j + 1)
        R.wait("tensor", fq["w1"], 2 * (j + 1))
        R.wait("tensor", fq["w2"], 2 * (j + 1))
        x1banks = ("b0", "b1") if j % 2 == 0 else ("b2", "b3")
        for tc_ in range(2):
            bn = x1banks[tc_]
            BK[bn].acquire()
            for kc in range(16):
                wb = wqbufs["w1"][j % 2][kc // 8]
                off = (kc % 8) * 128
                o = R.op("tensor", "matmul", BK[bn].t[:, 0:512],
                         wb[:, off:off + 128],
                         AR1[:, kc * NT + tc_ * 512: kc * NT + tc_ * 512 + 512],
                         start=(kc == 0), stop=(kc == 15))
            o.inc(pe, 1)
            if tc_ == 1:
                mm_pe_after[(j, "w1")] = pe.v
            bn2 = w2banks[j % 2][tc_]
            BK[bn2].acquire()
            for kc in range(16):
                wb = wqbufs["w2"][j % 2][kc // 8]
                off = (kc % 8) * 128
                o = R.op("tensor", "matmul", BK[bn2].t[:, 0:512],
                         wb[:, off:off + 128],
                         AR1[:, kc * NT + tc_ * 512: kc * NT + tc_ * 512 + 512],
                         start=(kc == 0), stop=(kc == 15))
            o.inc(pe, 1)
            if tc_ == 1:
                mm_pe_after[(j, "w2")] = pe.v
            sg = sgb[(j % 2) * 2 + tc_]
            R.wait("scalar", pe)
            if j >= 2:
                R.wait("scalar", hid, 2 * (j - 2) + tc_ + 1)
            R.op("scalar", "activation", out=sg[:], in_=BK[bn].t[:, 0:512],
                 func=AF.Sigmoid).inc(sgd, 1)
            BK[bn].release("scalar")
            R.wait("vector", sgd)
            R.op("vector", "tensor_tensor", out=hidden_ap(j, tc_ * 512, 512),
                 in0=BK[bn2].t[:, 0:512], in1=sg[:], op=ALU.mult)\
                .inc(hid, 1)
            BK[bn2].release("vector")

    R.phases.append(("B8", R.twc()))
    w3f = [SCRF[:, 0:1408], SCRF[:, 1408:2816]]
    OSTG = [SCRF[:, 2816:3328], SCRF[:, 3328:3840]]
    w3q = [[AR1[:, par * 5632 + i * 1408: par * 5632 + (i + 1) * 1408]
            for i in range(4)] for par in range(2)]
    t3a = [WQ[:, 0:1408], WQ[:, 2816:4224]]
    t3b = [WQ[:, 1408:2816], WQ[:, 4224:5632]]
    x1b = [SCRB[:, 0:1024], SCRB[:, 1024:2048]]
    f3ld = Stream(R, "f3ld", 2)
    f3q = R.sem("f3q")
    oev = R.sem("oev")
    ost = Stream(R, "ost", 2)
    R.wait("sync", fq["w1"], 88)
    R.wait("sync", fq["w2"], 88)
    R.wait("vector", sgd, 88)
    R.wait("vector", hid, 88)
    R.wait("gpsimd", pe)
    qtok3 = {}
    w3_pe_after = {}

    def emit_quant8(jq):
        p = jq % 2
        if jq >= 2:
            R.wait("sync", oev, (jq - 1) * 2)
        ldx.dma("sync", p, x1b[p][:], x1_spill[:, jq * NT:(jq + 1) * NT])
        for part in range(4):
            g = jq * 4 + part
            if g >= 2 and (g - 2) in qtok3:
                R.wait("sync", qbar, qtok3[g - 2])
            f3ld.dma("sync", part % 2, w3f[part % 2][:],
                     w3t[:, jq * 5632 + part * 1408: jq * 5632 + (part + 1) * 1408])
            f3ld.wait("vector", part % 2)
            R.wait("vector", dve, th_ready)
            col = thcol["w3"]
            if g >= 2:
                R.wait("vector", f3q, g - 1)
            gp2 = g % 2
            R.op("vector", "tensor_scalar", out=t3a[gp2][:], in0=w3f[part % 2][:],
                 scalar1=S["thb"][:, col:col + 1], scalar2=None, op0=ALU.is_ge)\
                .inc(qbar, 1)
            R.op("vector", "tensor_scalar", out=t3b[gp2][:], in0=w3f[part % 2][:],
                 scalar1=S["nthb"][:, col:col + 1], scalar2=None, op0=ALU.is_le)\
                .inc(qbar, 1)
            qbar_n[0] += 2
            qtok3[g] = qbar_n[0]
            R.wait("gpsimd", qbar, qbar_n[0])
            if jq >= 2 and (jq - 2) in w3_pe_after:
                R.wait("gpsimd", pe, w3_pe_after[jq - 2])
            R.op("gpsimd", "tensor_tensor", out=w3q[jq % 2][part][:],
                 in0=t3a[gp2][:], in1=t3b[gp2][:], op=ALU.subtract).inc(f3q, 1)

    emit_quant8(0)
    for j in range(16):
        p = j % 2
        if j + 1 < 16:
            emit_quant8(j + 1)
        R.wait("tensor", f3q, (j + 1) * 4)
        if j == 0:
            R.wait("tensor", hid, 88)
        pn3 = ("b0", "b1") if j % 2 == 0 else ("b2", "b3")
        for tc_ in range(2):
            bn = pn3[tc_]
            BK[bn].acquire()
            for kc in range(44):
                part = kc // 11
                off = (kc % 11) * 128
                o = R.op("tensor", "matmul", BK[bn].t[:, 0:512],
                         w3q[j % 2][part][:, off:off + 128],
                         hidden_ap(kc, tc_ * 512, 512),
                         start=(kc == 0), stop=(kc == 43))
            o.inc(pe, 1)
            R.wait("vector", pe)
            ldx.wait("vector", p)
            if j >= 1:
                ost.wait("vector", tc_, j)
            R.op("vector", "tensor_tensor", out=OSTG[tc_][:],
                 in0=BK[bn].t[:, 0:512],
                 in1=x1b[p][:, tc_ * 512: tc_ * 512 + 512],
                 op=ALU.add).inc(oev, 1)
            BK[bn].release("vector")
            R.wait("scalar", oev)
            ost.dma("scalar", tc_,
                    out_c[j * 128:(j + 1) * 128, tc_ * 512:(tc_ + 1) * 512],
                    OSTG[tc_][:])
        w3_pe_after[j] = pe.v
    ost.wait_all("scalar")
    R.phases.append(("end", R.twc()))
    global _LAST_PHASES
    _LAST_PHASES = R.phases
    R.replay()
    return nc



_CACHE = {}
_LAST_PHASES = None


def _tile_wT(w, oc):
    t = w.reshape(oc, 128, 16, 128).transpose(3, 0, 2, 1)
    return np.ascontiguousarray(t.reshape(128, oc * 16 * 128))


def _tile_w3T(w3):
    t = w3.reshape(16, 128, 44, 128).transpose(3, 0, 2, 1)
    return np.ascontiguousarray(t.reshape(128, 16 * 44 * 128))


def kernel(L0_attn_x, L0_attn_slope_rate, L0_attn_W_qkvu, L0_attn_W_out,
           L0_ffn_w1, L0_ffn_w2, L0_ffn_w3):
    x = np.asarray(L0_attn_x, dtype=np.float32).reshape(8192, DM)
    slope = np.ascontiguousarray(
        np.asarray(L0_attn_slope_rate, dtype=np.float32).reshape(1, NH))
    Wq = np.asarray(L0_attn_W_qkvu, dtype=np.float32)
    Wo = np.asarray(L0_attn_W_out, dtype=np.float32)
    w1 = np.asarray(L0_ffn_w1, dtype=np.float32)
    w2 = np.asarray(L0_ffn_w2, dtype=np.float32)
    w3 = np.asarray(L0_ffn_w3, dtype=np.float32)

    xT = np.ascontiguousarray(x.T)
    wqkvu_t = _tile_wT(Wq, 64)
    wout_t = _tile_wT(Wo, 16)
    w1_t = _tile_wT(w1, 44)
    w2_t = _tile_wT(w2, 44)
    w3_t = _tile_w3T(w3)

    consts = make_consts()
    if "F" not in _CACHE:
        _CACHE["F"] = build_prog_fused(consts)
    ncF = _CACHE["F"]

    in_F = []
    for c in range(8):
        dist = np.full((128, 8), 1e9, dtype=np.float32)
        for j in range(c):
            dist[:, j] = 1024.0 * (c - 1 - j)
        in_F.append({
            "xT": np.ascontiguousarray(xT[:, c * NT:(c + 1) * NT]),
            "distc": dist,
            "slope": slope,
            "wqkvu": wqkvu_t,
            "wout": wout_t,
            "w1t": w1_t,
            "w2t": w2_t,
            "w3t": w3_t,
        })
    res = run_bass_kernel_spmd(ncF, in_F, core_ids=list(range(8)))

    out = np.concatenate([res.results[c]["out_c"] for c in range(8)], axis=1)
    return np.ascontiguousarray(out.T).reshape(1, 8192, DM).astype(np.float32)


if __name__ == "__main__":
    ncF = build_prog_fused(make_consts())
    print("build ok; sems:", ncF and "n/a")
